# version 02
# baseline (speedup 1.0000x reference)
import math
import numpy as np
import ml_dtypes
from contextlib import ExitStack

import concourse.bass as bass
import concourse.mybir as mybir
from concourse.bass_utils import run_bass_kernel_spmd

AF = mybir.ActivationFunctionType
ALU = mybir.AluOpType
F32 = mybir.dt.float32
BF16 = mybir.dt.bfloat16
AXX = mybir.AxisListType.X

NCORES = 8
BATCH = 4
SEQ = 4096
DM = 1024
TCORE = SEQ // 2

SEM_ROTATE_LIMIT = 24000


def _cdiv(a, b):
    return (a + b - 1) // b


def akey(ap):
    return ap.tensor.name


class Op:
    __slots__ = ("queue", "emit", "waits", "inc", "tag", "semkey")

    def __init__(self, queue, emit, waits, inc, tag, semkey):
        self.queue = queue
        self.emit = emit
        self.waits = waits
        self.inc = inc
        self.tag = tag
        self.semkey = semkey


COMPUTE_QUEUES = ("tensor", "vector", "scalar", "gpsimd")
DMA_QUEUES = ("qsync", "qgpsimd", "qscalar", "qvector", "qtensor")
ALL_QUEUES = COMPUTE_QUEUES + DMA_QUEUES
QUEUE_ENGINE = {
    "tensor": "tensor", "vector": "vector", "scalar": "scalar",
    "gpsimd": "gpsimd", "qsync": "sync", "qgpsimd": "gpsimd",
    "qscalar": "scalar", "qvector": "vector", "qtensor": "tensor",
}


class Prog:

    def __init__(self, nc):
        self.nc = nc
        self.ops = []
        self.count = {q: 0 for q in ALL_QUEUES}
        self.gen = {q: 0 for q in ALL_QUEUES}
        self.genbase = {q: 0 for q in ALL_QUEUES}
        self.w = {}
        self.r = {}
        self.waited = {q: {} for q in ALL_QUEUES}
        self.final = {}

    def _need(self, q, deps):
        waits = {}
        for (dq, dgen, dcount) in deps:
            if dq == q and dgen == self.gen[q] and dq == "tensor":
                continue
            key = (dq, dgen)
            have = self.waited[q].get(key, 0)
            if dcount > have:
                waits[key] = max(waits.get(key, 0), dcount)
        out = []
        for key, c in waits.items():
            self.waited[q][key] = c
            out.append((key, c))
        return out

    DMA_SLOTS = 12

    def op(self, queue, emit, reads=(), writes=(), tag="", inc=None):
        assert queue in ALL_QUEUES, queue
        is_dma = queue.startswith("q")
        if inc is None:
            inc = 16 if is_dma else 1
        rkeys = [x if isinstance(x, str) else akey(x) for x in reads]
        wkeys = [x if isinstance(x, str) else akey(x) for x in writes]
        deps = []
        for kk in rkeys:
            lw = self.w.get(kk)
            if lw is not None:
                deps.append(lw)
        for kk in wkeys:
            lw = self.w.get(kk)
            if lw is not None:
                deps.append(lw)
            for (rqg, rcount) in self.r.get(kk, {}).items():
                deps.append((rqg[0], rqg[1], rcount))
        if is_dma:
            j = self.count[queue]
            self.count[queue] += 1
            slot = j % self.DMA_SLOTS
            use = j // self.DMA_SLOTS
            if use > 0:
                deps.append((queue, slot, 16 * use))
            waits = self._need(queue, deps)
            semkey = (queue, slot)
            rel = 16 * (use + 1)
            self.final[semkey] = rel
        else:
            waits = self._need(queue, deps)
            if self.count[queue] - self.genbase[queue] + inc > SEM_ROTATE_LIMIT:
                self.gen[queue] += 1
                self.genbase[queue] = self.count[queue]
            self.count[queue] += inc
            g = self.gen[queue]
            rel = self.count[queue] - self.genbase[queue]
            semkey = (queue, g)
            self.final[semkey] = rel
        o = Op(queue, emit, waits, inc, tag, semkey)
        self.ops.append(o)
        me = (semkey[0], semkey[1], rel)
        for kk in wkeys:
            self.w[kk] = me
            self.r[kk] = {}
        for kk in rkeys:
            self.r.setdefault(kk, {})[semkey] = rel
        return o

    def dma(self, out, in_, queue="qsync", tag="dma", reads=(), writes=(), **kw):
        eng = QUEUE_ENGINE[queue]
        self.op(queue,
                lambda e, out=out, in_=in_: getattr(self.nc, eng).dma_start(out=out, in_=in_, **kw),
                reads=[in_, *reads], writes=[out, *writes], tag=tag)

    def mm(self, out, lhsT, rhs, start=True, stop=True, tag="mm", **kw):
        self.op("tensor",
                lambda e, out=out, l=lhsT, r=rhs: self.nc.tensor.matmul(out, l, r, start=start, stop=stop, **kw),
                reads=[lhsT, rhs], writes=[out], tag=tag)

    def transpose(self, out, in_, ident, tag="tr"):
        self.op("tensor",
                lambda e, out=out, i=in_, d=ident: self.nc.tensor.transpose(out, i, d),
                reads=[in_, ident], writes=[out], tag=tag)

    def act(self, out, in_, func, bias=0.0, scale=1.0, accum_out=None, tag="act"):
        reads = [in_]
        if isinstance(bias, bass.AP):
            reads.append(bias)
        if isinstance(scale, bass.AP):
            reads.append(scale)
        writes = [out]
        if accum_out is not None:
            writes.append(accum_out)
        self.op("scalar",
                lambda e, out=out, i=in_: self.nc.scalar.activation(out, i, func, bias=bias, scale=scale, accum_out=accum_out),
                reads=reads, writes=writes, tag=tag)

    def sign(self, out, in_, tag="sign"):
        self.op("scalar",
                lambda e, out=out, i=in_: self.nc.scalar.sign(out, i),
                reads=[in_], writes=[out], tag=tag)

    def tt(self, out, a, b, alu, engine="vector", tag="tt"):
        self.op(engine,
                lambda e, out=out, a=a, b=b: getattr(self.nc, engine).tensor_tensor(out, a, b, alu),
                reads=[a, b], writes=[out], tag=tag)

    def ts(self, out, in0, s1, s2, op0, op1=ALU.bypass, engine="vector", accum_out=None, tag="ts"):
        reads = [in0]
        if isinstance(s1, bass.AP):
            reads.append(s1)
        if isinstance(s2, bass.AP):
            reads.append(s2)
        writes = [out]
        if accum_out is not None:
            writes.append(accum_out)
        self.op(engine,
                lambda e, out=out, i=in0: getattr(self.nc, engine).tensor_scalar(out, i, s1, s2, op0, op1, accum_out=accum_out),
                reads=reads, writes=writes, tag=tag)

    def stt(self, out, in0, scalar, in1, op0, op1, engine="vector", tag="stt"):
        reads = [in0, in1]
        if isinstance(scalar, bass.AP):
            reads.append(scalar)
        self.op(engine,
                lambda e, out=out, a=in0, b=in1: getattr(self.nc, engine).scalar_tensor_tensor(out, a, scalar, b, op0, op1),
                reads=reads, writes=[out], tag=tag)

    def scan(self, out, data0, data1, initial, op0=ALU.mult, op1=ALU.add, engine="vector", tag="scan"):
        reads = [data0, data1]
        if isinstance(initial, bass.AP):
            reads.append(initial)
        self.op(engine,
                lambda e, out=out, a=data0, b=data1: getattr(self.nc, engine).tensor_tensor_scan(out, a, b, initial, op0, op1),
                reads=reads, writes=[out], tag=tag)

    def red(self, out, in_, alu, axis=AXX, engine="vector", absval=None, tag="red"):
        self.op(engine,
                lambda e, out=out, i=in_: getattr(self.nc, engine).tensor_reduce(out, i, axis, alu, apply_absolute_value=absval),
                reads=[in_], writes=[out], tag=tag)

    def copy(self, out, in_, engine="vector", tag="copy"):
        try:
            if engine == "scalar" and "PSUM" in str(in_.space):
                engine = "vector"
        except Exception:
            pass
        if engine == "scalar":
            self.op(engine, lambda e, out=out, i=in_: self.nc.scalar.copy(out, i),
                    reads=[in_], writes=[out], tag=tag)
        else:
            self.op(engine, lambda e, out=out, i=in_: getattr(self.nc, engine).tensor_copy(out, i),
                    reads=[in_], writes=[out], tag=tag)

    def memset(self, ap, val, engine="vector", tag="memset"):
        self.op(engine, lambda e, ap=ap: getattr(self.nc, engine).memset(ap, val),
                reads=[], writes=[ap], tag=tag)

    def reciprocal(self, out, in_, tag="recip"):
        self.op("vector", lambda e, out=out, i=in_: self.nc.vector.reciprocal(out, i),
                reads=[in_], writes=[out], tag=tag)

    def build(self):
        nc = self.nc
        sems = {}
        ctxs = []
        for semkey in self.final:
            h = nc.semaphore(f"s_{semkey[0]}_{semkey[1]}")
            sems[semkey] = h.__enter__()
            ctxs.append(h)
        per_engine_ops = {}
        for o in self.ops:
            per_engine_ops.setdefault(QUEUE_ENGINE[o.queue], []).append(o)
        dma_finals = [(kk, c) for kk, c in self.final.items() if kk[0].startswith("q")]

        def make_body(ops_list):
            def body(eng):
                for o in ops_list:
                    for (dk, c) in o.waits:
                        eng.wait_ge(sems[dk], c)
                    instr = o.emit(eng)
                    instr.then_inc(sems[o.semkey], o.inc)
                for (kk, fc) in dma_finals:
                    eng.wait_ge(sems[kk], fc)
            return body

        with nc.Block() as block:
            for en in ("sync", "tensor", "vector", "scalar", "gpsimd"):
                ops_list = per_engine_ops.get(en)
                if ops_list:
                    getattr(block, en)(make_body(ops_list))
        for h in reversed(ctxs):
            h.__exit__(None, None, None)


class Fm:

    def __init__(self, t, C, T):
        self.t = t
        self.C = C
        self.T = T

    def c(self, i, t0=0, t1=None):
        if t1 is None:
            t1 = self.T
        return self.t[:, i * self.T + t0: i * self.T + t1]


class K:

    def __init__(self, name):
        self.name = name
        self.nc = bass.Bass("TRN2", target_bir_lowering=False)
        self.P = Prog(self.nc)
        self.st = ExitStack()
        self.consts = {}
        self.outs = []
        self._uid = 0
        self._ps_pool = []
        self._ps_i = 0
        self._built = False

    def uid(self, base):
        self._uid += 1
        return f"{base}_{self._uid}"

    def inp(self, name, shape, dtype=F32):
        return self.nc.dram_tensor(name, list(shape), dtype, kind="ExternalInput").ap()

    def outp(self, name, shape, dtype=F32):
        self.outs.append(name)
        return self.nc.dram_tensor(name, list(shape), dtype, kind="ExternalOutput").ap()

    def sb(self, name, p, f, dtype):
        return self.st.enter_context(self.nc.sbuf_tensor(self.uid(name), [p, f], dtype))

    def fm(self, name, C, T, dtype):
        return Fm(self.sb(name, 128, C * T, dtype), C, T)

    def mkpsums(self, n=8, f=512):
        for i in range(n):
            self._ps_pool.append(self.st.enter_context(self.nc.psum_tensor(self.uid("pp"), [128, f], F32)))

    def ps(self):
        t = self._ps_pool[self._ps_i % len(self._ps_pool)]
        self._ps_i += 1
        return t

    def const(self, name, arr):
        if name not in self.consts:
            self.consts[name] = arr
            dt = {np.dtype(np.float32): F32, np.dtype(ml_dtypes.bfloat16): BF16}[arr.dtype]
            ap = self.inp(name, arr.shape, dt)
            setattr(self, "_c_" + name, ap)
        return getattr(self, "_c_" + name)

    def const_sb(self, name, arr):
        att = "_csb_" + name
        if not hasattr(self, att):
            hap = self.const(name, arr)
            t = self.sb("csb_" + name, arr.shape[0], arr.shape[1], {np.dtype(np.float32): F32, np.dtype(ml_dtypes.bfloat16): BF16}[arr.dtype])
            self.P.dma(t[:], hap, queue="qsync")
            setattr(self, att, t)
        return getattr(self, att)

    def ones_col_bf(self):
        return self.const_sb("c_ones_col", np.ones((128, 1), ml_dtypes.bfloat16))[:]

    def ones_row_f(self):
        return self.const_sb("c_ones_row", np.ones((1, 128), np.float32))[:]

    def ident_f(self):
        return self.const_sb("c_ident_f", np.eye(128, dtype=np.float32))[:]

    def ident_bf(self):
        return self.const_sb("c_ident_bf", np.eye(128, dtype=ml_dtypes.bfloat16))[:]

    def hot16(self):
        h = np.zeros((16, 16 * 128), np.float32)
        for n in range(16):
            h[n, n * 128:(n + 1) * 128] = 1.0
        return self.const("c_hot16", h)

    def hot8x(self):
        h = np.zeros((64, 8 * 128), np.float32)
        for c in range(8):
            for p in range(128):
                h[c * 8 + p // 16, c * 128 + p] = 1.0
        return self.const("c_hot8x", h)

    def build(self):
        if not self._built:
            self.P.build()
            self._built = True
        return self.nc

    def run(self, in_maps, core_ids=None):
        self.build()
        if core_ids is None:
            core_ids = list(range(len(in_maps)))
        for m in in_maps:
            m.update(self.consts)
        return run_bass_kernel_spmd(self.nc, in_maps, core_ids=core_ids)



def load_fm(k, fmobj, hbm, D, T, t0=0, dma_queue="qsync"):
    C = _cdiv(D, 128)
    for c in range(C):
        p = min(128, D - c * 128)
        k.P.dma(fmobj.t[0:p, c * T:c * T + T], hbm[c * 128:c * 128 + p, t0:t0 + T], queue=dma_queue)


def store_fm(k, hbm, fmobj, D, T, dma_queue="qsync"):
    C = _cdiv(D, 128)
    for c in range(C):
        p = min(128, D - c * 128)
        k.P.dma(hbm[c * 128:c * 128 + p, 0:T], fmobj.t[0:p, c * T:c * T + T], queue=dma_queue)


def load_wslabT(k, wT_h, Kdim, N, name="w", queue="qgpsimd", slab=None, col0=0, slabN=None):
    KC = _cdiv(Kdim, 128)
    if slabN is None:
        slabN = N
    if slab is None:
        slab = k.sb(name + "_slab", 128, KC * slabN, BF16)
    for kc in range(KC):
        p = min(128, Kdim - kc * 128)
        k.P.dma(slab[0:p, kc * slabN:kc * slabN + N], wT_h[kc * 128:kc * 128 + p, col0:col0 + N], queue=queue)
    return slab


def load_bias_cols(k, b_h, N, name="b"):
    NC = _cdiv(N, 128)
    t = k.sb(name + "_cols", 128, NC, F32)
    for n in range(NC):
        p = min(128, N - n * 128)
        k.P.dma(t[0:p, n:n + 1], b_h[n * 128:n * 128 + p, None], queue="qsync")
    return t


def linear(k, x, slab, Kdim, N, T, evac, Tt=512, tag="lin", t_tiles=None, slabN=None):
    KC = _cdiv(Kdim, 128)
    NC = _cdiv(N, 128)
    if slabN is None:
        slabN = N
    if t_tiles is None:
        t_tiles = [(t0, min(Tt, T - t0)) for t0 in range(0, T, Tt)]
    for n in range(NC):
        npp = min(128, N - n * 128)
        for (t0, tw) in t_tiles:
            ps = k.ps()
            for kc in range(KC):
                kp = min(128, Kdim - kc * 128)
                k.P.mm(ps[0:npp, 0:tw],
                       slab[0:kp, kc * slabN + n * 128: kc * slabN + n * 128 + npp],
                       x.c(kc, t0, t0 + tw)[0:kp, :],
                       start=(kc == 0), stop=(kc == KC - 1), tag=tag)
            evac(n, t0, tw, ps[0:npp, 0:tw])


def bcast_row_psum(k, row_ap, tw, tag="bcast"):
    ps = k.ps()
    k.P.mm(ps[:, 0:tw], k.ones_row_f(), row_ap, tag=tag)
    return ps


def layernorm_stream(k, x_h, D, Te, xn_bf, xn_f32_h=None, eps=1e-5, Tt=512, also_fp32_fm=None):
    C = _cdiv(D, 128)
    xt = k.sb("ln_x", 128, C * Tt, F32)
    xb = k.sb("ln_xb", 128, Tt, BF16)
    xsq = k.sb("ln_xsq", 128, Tt, BF16)
    rows = k.sb("ln_rows", 1, 6 * Tt, F32)
    xnf = k.sb("ln_xnf", 128, Tt, F32)
    for t0 in range(0, Te, Tt):
        tw = min(Tt, Te - t0)
        for c in range(C):
            k.P.dma(xt[:, c * Tt:c * Tt + tw], x_h[c * 128:(c + 1) * 128, t0:t0 + tw])
        ps_s = k.ps()
        ps_q = k.ps()
        for c in range(C):
            k.P.copy(xb[:, 0:tw], xt[:, c * Tt:c * Tt + tw], engine="gpsimd")
            k.P.mm(ps_s[0:1, 0:tw], k.ones_col_bf(), xb[:, 0:tw], start=(c == 0), stop=(c == C - 1), tag="lnsum")
            k.P.act(xsq[:, 0:tw], xt[:, c * Tt:c * Tt + tw], AF.Square, tag="lnsq")
            k.P.mm(ps_q[0:1, 0:tw], k.ones_col_bf(), xsq[:, 0:tw], start=(c == 0), stop=(c == C - 1), tag="lnsum2")
        mean = rows[:, 0:tw]
        rstd = rows[:, Tt:Tt + tw]
        mr = rows[:, 2 * Tt:2 * Tt + tw]
        var = rows[:, 3 * Tt:3 * Tt + tw]
        k.P.ts(mean, ps_s[0:1, 0:tw], 1.0 / D, None, ALU.mult)
        k.P.ts(var, ps_q[0:1, 0:tw], 1.0 / D, None, ALU.mult)
        k.P.tt(rows[:, 4 * Tt:4 * Tt + tw], mean, mean, ALU.mult)
        k.P.tt(var, var, rows[:, 4 * Tt:4 * Tt + tw], ALU.subtract)
        k.P.ts(var, var, eps, None, ALU.add)
        k.P.act(rows[:, 5 * Tt:5 * Tt + tw], var, AF.Sqrt)
        k.P.reciprocal(rstd, rows[:, 5 * Tt:5 * Tt + tw])
        k.P.tt(mr, mean, rstd, ALU.mult)
        bps = bcast_row_psum(k, rstd, tw)
        bpm = bcast_row_psum(k, mr, tw)
        for c in range(C):
            k.P.tt(xnf[:, 0:tw], xt[:, c * Tt:c * Tt + tw], bps[:, 0:tw], ALU.mult)
            k.P.tt(xnf[:, 0:tw], xnf[:, 0:tw], bpm[:, 0:tw], ALU.subtract)
            k.P.copy(xn_bf.c(c, t0, t0 + tw), xnf[:, 0:tw], engine="scalar")
            if also_fp32_fm is not None:
                k.P.copy(also_fp32_fm.c(c, t0, t0 + tw), xnf[:, 0:tw], engine="gpsimd")
            if xn_f32_h is not None:
                k.P.dma(xn_f32_h[c * 128:(c + 1) * 128, t0:t0 + tw], xnf[:, 0:tw], queue="qscalar")


def softplus_tiles(k, out_ap, in_ap, scr1, scr2):
    k.P.act(scr1, in_ap, AF.Abs)
    k.P.act(scr1, scr1, AF.Exp, scale=-1.0)
    k.P.act(scr1, scr1, AF.Ln, bias=1.0)
    k.P.ts(scr2, in_ap, 0.0, None, ALU.max)
    k.P.tt(out_ap, scr1, scr2, ALU.add)


def quant_fp4_tiles(k, out_ap, x_ap, sc_bc_ap, rsc_bc_ap, scr, tag="q"):
    v, a, kc, m = scr["v"], scr["a"], scr["kc"], scr["m"]
    MID = [0.25, 0.75, 1.25, 1.75, 2.5, 3.5, 5.0]
    k.P.tt(v, x_ap, rsc_bc_ap, ALU.mult, tag=tag + "div")
    k.P.act(a, v, AF.Abs, tag=tag + "abs")
    k.P.ts(kc, a, MID[0], None, ALU.is_gt, tag=tag + "t0")
    for mm_ in MID[1:]:
        k.P.stt(kc, a, mm_, kc, ALU.is_gt, ALU.add, tag=tag + "t")
    k.P.ts(m, kc, -4.0, 0.0, ALU.add, ALU.max, tag=tag + "m1")
    k.P.tt(kc, kc, m, ALU.add, tag=tag + "km1")
    k.P.ts(m, kc, -6.0 - 2.0, 0.0, ALU.add, ALU.max, tag=tag + "m2")
    k.P.ts(m, m, 0.5, None, ALU.mult, tag=tag + "m2b")
    k.P.stt(v, kc, 0.5, m, ALU.mult, ALU.add, tag=tag + "comb")
    k.P.sign(a, x_ap, tag=tag + "sgn")
    k.P.tt(v, v, a, ALU.mult, tag=tag + "sg")
    k.P.tt(out_ap, v, sc_bc_ap, ALU.mult, tag=tag + "rs")



HALO_L0 = 1
TE0 = TCORE + 4


def build_l0a():
    k = K("l0a")
    Te = TE0
    x_h = k.inp("x", (DM, Te))
    winT = k.inp("winT", (DM, 4096))
    wxT = k.inp("wxT", (DM, 96))
    wdtT = k.inp("wdtT", (64, 2048))
    kconv = k.inp("kconv", (2048, 4))
    bconv = k.inp("bconv", (2048,))
    bdt = k.inp("bdt", (2048,))
    xn_o = k.outp("xn", (DM, Te))
    u_o = k.outp("u", (2048, TCORE))
    delta_o = k.outp("delta", (2048, TCORE))
    B_o = k.outp("B", (16, TCORE))
    C_o = k.outp("Cc", (16, TCORE))
    res_o = k.outp("res", (2048, Te))
    k.mkpsums(8)

    xn_bf = k.fm("xn_bf", 8, Te, BF16)
    layernorm_stream(k, x_h, DM, Te, xn_bf, xn_f32_h=xn_o)

    xproj = k.fm("xproj", 16, Te, BF16)
    slab_in = load_wslabT(k, winT, DM, 2048, "win")
    stage = k.sb("stage", 128, 2 * 512, F32)

    def evac_xp(n, t0, tw, ps):
        k.P.copy(xproj.c(n, t0, t0 + tw), ps, engine="scalar")
    linear(k, xn_bf, slab_in, DM, 2048, Te, evac_xp, tag="win")
    load_wslabT(k, winT, DM, 2048, "win2", slab=slab_in, col0=2048)

    def evac_res(n, t0, tw, ps):
        s = stage[:, (t0 // 512 % 2) * 512:(t0 // 512 % 2) * 512 + tw]
        k.P.copy(s, ps, engine="vector")
        k.P.dma(res_o[n * 128:(n + 1) * 128, t0:t0 + tw], s, queue="qscalar")
    linear(k, xn_bf, slab_in, DM, 2048, Te, evac_res, tag="wres")

    slab_x = load_wslabT(k, wxT, DM, 96, "wx")
    xdbl = k.sb("xdbl", 96, Te, F32)
    xdbl_bf = k.sb("xdbl_bf", 64, Te, BF16)

    def evac_xdbl(n, t0, tw, ps):
        k.P.copy(xdbl[:, t0:t0 + tw], ps, engine="vector")
        k.P.copy(xdbl_bf[:, t0:t0 + tw], ps[0:64, :], engine="scalar")
    linear(k, xn_bf, slab_x, DM, 96, Te, evac_xdbl, tag="wx")
    k.P.dma(B_o, xdbl[64:80, 1:1 + TCORE], queue="qsync")
    k.P.dma(C_o, xdbl[80:96, 1:1 + TCORE], queue="qsync")

    slab_dt = load_wslabT(k, wdtT, 64, 2048, "wdt")
    bdt_cols = load_bias_cols(k, bdt, 2048, "bdt")
    xdbl_fm = Fm(xdbl_bf, 1, Te)
    dsc1 = k.sb("dsc1", 128, 512, F32)
    dsc2 = k.sb("dsc2", 128, 512, F32)
    dstage = k.sb("dstage", 128, 2 * 512, F32)

    def evac_delta(n, t0, tw, ps):
        s = dstage[:, (t0 // 512 % 2) * 512:(t0 // 512 % 2) * 512 + tw]
        k.P.ts(s, ps, bdt_cols[:, n:n + 1], None, ALU.add)
        softplus_tiles(k, s, s, dsc1[:, 0:tw], dsc2[:, 0:tw])
        k.P.dma(delta_o[n * 128:(n + 1) * 128, t0 - 1:t0 - 1 + tw], s, queue="qscalar")
    linear(k, xdbl_fm, slab_dt, 64, 2048, Te, evac_delta, tag="wdt",
           t_tiles=[(1, 512), (513, 512), (1025, 512), (1537, 512)])

    kw = k.sb("kw", 128, 16 * 4, F32)
    for c in range(16):
        k.P.dma(kw[:, c * 4:(c + 1) * 4], kconv[c * 128:(c + 1) * 128, :], queue="qsync")
    bcv = load_bias_cols(k, bconv, 2048, "bconv")
    cacc = k.sb("cacc", 128, 512, F32)
    ustage = k.sb("ustage", 128, 2 * 512, F32)
    for c in range(16):
        for t0 in range(0, TCORE, 512):
            tw = min(512, TCORE - t0)
            base = c * Te + t0
            k.P.ts(cacc[:, 0:tw], xproj.t[:, base:base + tw], kw[:, c * 4:c * 4 + 1], bcv[:, c:c + 1], ALU.mult, ALU.add)
            for i in (1, 2, 3):
                k.P.stt(cacc[:, 0:tw], xproj.t[:, base + i:base + i + tw], kw[:, c * 4 + i:c * 4 + i + 1], cacc[:, 0:tw], ALU.mult, ALU.add)
            s = ustage[:, (t0 // 512 % 2) * 512:(t0 // 512 % 2) * 512 + tw]
            k.P.act(s, cacc[:, 0:tw], AF.Silu)
            k.P.dma(u_o[c * 128:(c + 1) * 128, t0:t0 + tw], s, queue="qscalar")
    return k


def build_l0s():
    k = K("l0s")
    DS = 1024
    T = SEQ
    TC = 1024
    delta_h = k.inp("delta", (DS, T))
    u_h = k.inp("u", (DS, T))
    B_h = k.inp("B", (16, T))
    C_h = k.inp("Cc", (16, T))
    alog_h = k.inp("alog", (DS, 16))
    d_h = k.inp("dd", (DS,))
    y_o = k.outp("y", (DS, T))
    k.mkpsums(8)
    hot = k.hot16()
    hot_bf = k.sb("hotbf", 16, 16 * 128, F32)
    k.P.dma(hot_bf[:], hot, queue="qsync")

    At = k.sb("At", 128, 8 * 16, F32)
    for c in range(8):
        k.P.dma(At[:, c * 16:(c + 1) * 16], alog_h[c * 128:(c + 1) * 128, :], queue="qsync")
    k.P.act(At[:], At[:], AF.Exp)
    k.P.ts(At[:], At[:], -1.0, None, ALU.mult)
    Dt = k.sb("Dt", 128, 8, F32)
    for c in range(8):
        k.P.dma(Dt[:, c:c + 1], d_h[c * 128:(c + 1) * 128, None], queue="qsync")

    dbuf = k.fm("dbuf", 8, TC, F32)
    ubuf = k.fm("ubuf", 8, TC, F32)
    wbuf = k.fm("wbuf", 8, TC, F32)
    ybuf = k.fm("ybuf", 8, TC, F32)
    Brow = k.sb("Brow", 16, TC, F32)
    Crow = k.sb("Crow", 16, TC, F32)
    Bbc = k.sb("Bbc", 128, TC, F32)
    Cbc = k.sb("Cbc", 128, TC, F32)
    sa = [k.sb("sa", 128, TC, F32) for _ in range(2)]
    sbb = [k.sb("sbb", 128, TC, F32) for _ in range(2)]
    sh = [k.sb("sh", 128, TC, F32) for _ in range(2)]
    shc = [k.sb("shc", 128, TC, F32) for _ in range(2)]
    stt_ = k.sb("state", 128, 8 * 16, F32)
    k.P.memset(stt_[:], 0.0)

    for t0 in range(0, T, TC):
        for c in range(8):
            k.P.dma(dbuf.c(c), delta_h[c * 128:(c + 1) * 128, t0:t0 + TC], queue="qsync")
            k.P.dma(ubuf.c(c), u_h[c * 128:(c + 1) * 128, t0:t0 + TC], queue="qgpsimd")
        k.P.dma(Brow[:], B_h[:, t0:t0 + TC], queue="qsync")
        k.P.dma(Crow[:], C_h[:, t0:t0 + TC], queue="qsync")
        for c in range(8):
            k.P.tt(wbuf.c(c), dbuf.c(c), ubuf.c(c), ALU.mult, engine="gpsimd")
            k.P.ts(ybuf.c(c), ubuf.c(c), Dt[:, c:c + 1], None, ALU.mult)
        for n in range(16):
            psB = k.ps()
            psC = k.ps()
            k.P.mm(psB[:, 0:512], hot_bf[:, n * 128:(n + 1) * 128], Brow[:, 0:512], tag="bcB")
            k.P.mm(psB[:, 512:1024] if False else psB[:, 0:512], hot_bf[:, n * 128:(n + 1) * 128], Brow[:, 512:1024], tag="bcB2") if False else None
            psB2 = k.ps()
            k.P.mm(psB2[:, 0:512], hot_bf[:, n * 128:(n + 1) * 128], Brow[:, 512:1024], tag="bcB2")
            k.P.mm(psC[:, 0:512], hot_bf[:, n * 128:(n + 1) * 128], Crow[:, 0:512], tag="bcC")
            psC2 = k.ps()
            k.P.mm(psC2[:, 0:512], hot_bf[:, n * 128:(n + 1) * 128], Crow[:, 512:1024], tag="bcC2")
            k.P.copy(Bbc[:, 0:512], psB[:, 0:512], engine="vector")
            k.P.copy(Bbc[:, 512:1024], psB2[:, 0:512], engine="vector")
            k.P.copy(Cbc[:, 0:512], psC[:, 0:512], engine="vector")
            k.P.copy(Cbc[:, 512:1024], psC2[:, 0:512], engine="vector")
            for c in range(8):
                pp = (n * 8 + c) % 2
                k.P.act(sa[pp][:], dbuf.c(c), AF.Exp, scale=At[:, c * 16 + n:c * 16 + n + 1], tag="deltaA")
                k.P.tt(sbb[pp][:], wbuf.c(c), Bbc[:], ALU.mult, tag="dBu", engine="gpsimd")
                k.P.scan(sh[pp][:], sa[pp][:], sbb[pp][:], stt_[:, c * 16 + n:c * 16 + n + 1], tag="scan")
                k.P.copy(stt_[:, c * 16 + n:c * 16 + n + 1], sh[pp][:, TC - 1:TC], engine="scalar")
                k.P.tt(shc[pp][:], sh[pp][:], Cbc[:], ALU.mult, tag="hC", engine="gpsimd")
                k.P.tt(ybuf.c(c), ybuf.c(c), shc[pp][:], ALU.add, tag="yacc")
        for c in range(8):
            k.P.dma(y_o[c * 128:(c + 1) * 128, t0:t0 + TC], ybuf.c(c), queue="qscalar")
    return k


def build_l0f1():
    k = K("l0f1")
    T = TCORE
    II = 2816
    xn_h = k.inp("xn", (DM, T))
    xntm_h = k.inp("xntm", (T, DM))
    wgq_h = k.inp("wgqT", (DM, II))
    wuq_h = k.inp("wuqT", (DM, II))
    hq_o = k.outp("hq", (II, T), BF16)
    k.mkpsums(8)

    idf = k.ident_f()
    xtm = k.sb("xtm", 128, DM, F32)
    sctm = k.sb("sctm", 128, 64, F32)
    scfm = k.sb("scfm", 64, T, F32)
    rscfm = k.sb("rscfm", 64, T, F32)
    for tt_ in range(T // 128):
        k.P.dma(xtm[:], xntm_h[tt_ * 128:(tt_ + 1) * 128, :], queue="qsync")
        k.P.red(sctm[:, :, None], xtm[:].rearrange("p (g i) -> p g i", i=16), ALU.max, absval=True)
        k.P.ts(sctm[:], sctm[:], 1.0 / 6.0, 1e-12, ALU.mult, ALU.max)
        ps = k.ps()
        k.P.transpose(ps[0:64, 0:128], sctm[:], idf)
        k.P.copy(scfm[:, tt_ * 128:(tt_ + 1) * 128], ps[0:64, 0:128], engine="vector")
        k.P.reciprocal(rscfm[:, tt_ * 128:(tt_ + 1) * 128], ps[0:64, 0:128])

    xq = k.fm("xq", 8, T, BF16)
    hot8 = k.hot8x()
    hot8t = k.sb("hot8t", 64, 8 * 128, F32)
    k.P.dma(hot8t[:], hot8, queue="qsync")
    scbc = k.sb("scbc", 128, 512, F32)
    rscbc = k.sb("rscbc", 128, 512, F32)
    xnc = k.sb("xnc", 128, T, F32)
    qv = k.sb("qv", 128, 512, F32)
    qa = k.sb("qa", 128, 512, F32)
    qk = k.sb("qk", 128, 512, F32)
    qm = k.sb("qm", 128, 512, F32)
    for c in range(8):
        k.P.dma(xnc[:], xn_h[c * 128:(c + 1) * 128, :], queue="qsync")
        for t0 in range(0, T, 512):
            ps = k.ps()
            k.P.mm(ps[:, 0:512], hot8t[:, c * 128:(c + 1) * 128], scfm[:, t0:t0 + 512], tag="scbc")
            k.P.copy(scbc[:, 0:512], ps[:, 0:512], engine="scalar")
            ps = k.ps()
            k.P.mm(ps[:, 0:512], hot8t[:, c * 128:(c + 1) * 128], rscfm[:, t0:t0 + 512], tag="rscbc")
            k.P.copy(rscbc[:, 0:512], ps[:, 0:512], engine="scalar")
            scr2 = {"v": qv[:], "a": qa[:], "kc": qk[:], "m": qm[:]}
            quant_fp4_tiles(k, xq.c(c, t0, t0 + 512), xnc[:, t0:t0 + 512], scbc[:, 0:512], rscbc[:, 0:512], scr2, tag="qx")

    slab_g = load_wslabT(k, wgq_h, DM, II, "wg")
    slab_u = load_wslabT(k, wuq_h, DM, II, "wu")
    hfull = k.sb("hfull", 128, 512, F32)
    htm = k.sb("htm", 128, 512, F32)
    hsc_tm = k.sb("hsctm", 128, 4 * 8, F32)
    hsc_fm = k.sb("hscfm", 8, 512, F32)
    hrsc_fm = k.sb("hrscfm", 8, 512, F32)
    hscbc = k.sb("hscbc", 128, 512, F32)
    hrscbc = k.sb("hrscbc", 128, 512, F32)
    hqt = k.sb("hqt", 128, 512, BF16)
    NC2 = _cdiv(II, 128)
    for n in range(NC2):
        npp = min(128, II - n * 128)
        for t0 in range(0, T, 512):
            psg = k.ps()
            psu = k.ps()
            for kc in range(8):
                k.P.mm(psg[0:npp, 0:512], slab_g[:, kc * II + n * 128: kc * II + n * 128 + npp], xq.c(kc, t0, t0 + 512), start=(kc == 0), stop=(kc == 7), tag="gmm")
            for kc in range(8):
                k.P.mm(psu[0:npp, 0:512], slab_u[:, kc * II + n * 128: kc * II + n * 128 + npp], xq.c(kc, t0, t0 + 512), start=(kc == 0), stop=(kc == 7), tag="umm")
            k.P.act(hfull[0:npp, 0:512], psg[0:npp, 0:512], AF.Silu, tag="hsilu")
            k.P.tt(hfull[0:npp, 0:512], hfull[0:npp, 0:512], psu[0:npp, 0:512], ALU.mult, tag="hmul")
            for s4 in range(4):
                pst = k.ps()
                k.P.transpose(pst[0:128, 0:npp], hfull[0:npp, s4 * 128:(s4 + 1) * 128], idf[0:npp, 0:npp], tag="htr")
                k.P.copy(htm[:, 0:npp], pst[0:128, 0:npp], engine="vector")
                k.P.red(hsc_tm[:, s4 * 8:s4 * 8 + 8, None], htm[:, 0:npp].rearrange("p (g i) -> p g i", i=16), ALU.max, absval=True, tag="hmax")
            k.P.ts(hsc_tm[:], hsc_tm[:], 1.0 / 6.0, 1e-12, ALU.mult, ALU.max, tag="hsc")
            for s4 in range(4):
                pst = k.ps()
                k.P.transpose(pst[0:8, 0:128], hsc_tm[:, s4 * 8:(s4 + 1) * 8], idf, tag="hsctr")
                k.P.copy(hsc_fm[0:8, s4 * 128:(s4 + 1) * 128], pst[0:8, 0:128], engine="vector")
            k.P.reciprocal(hrsc_fm[0:8, 0:512], hsc_fm[0:8, 0:512])
            ps = k.ps()
            k.P.mm(ps[0:npp, 0:512], hot8t[0:8, 0:npp], hsc_fm[0:8, 0:512], tag="hscbc")
            k.P.copy(hscbc[0:npp, 0:512], ps[0:npp, 0:512], engine="scalar")
            ps = k.ps()
            k.P.mm(ps[0:npp, 0:512], hot8t[0:8, 0:npp], hrsc_fm[0:8, 0:512], tag="hrscbc")
            k.P.copy(hrscbc[0:npp, 0:512], ps[0:npp, 0:512], engine="scalar")
            scr2 = {"v": qv[:], "a": qa[:], "kc": qk[:], "m": qm[:]}
            quant_fp4_tiles(k, hqt[0:npp, 0:512], hfull[0:npp, 0:512], hscbc[0:npp, 0:512], hrscbc[0:npp, 0:512], {kk: vv[0:npp, 0:512] for kk, vv in scr2.items()}, tag="qh")
            k.P.dma(hq_o[n * 128:n * 128 + npp, t0:t0 + 512], hqt[0:npp, 0:512], queue="qscalar")
    return k


def build_l0f2():
    k = K("l0f2")
    T = TCORE
    II = 2816
    hq_h = k.inp("hq", (II, T), BF16)
    wdq_h = k.inp("wdqT", (II, DM))
    ffn_o = k.outp("ffn", (DM, T))
    k.mkpsums(8)
    KC = _cdiv(II, 128)
    slab_d = load_wslabT(k, wdq_h, II, DM, "wd")
    hqt = k.sb("hqtile", 128, KC * 512, BF16)
    dstage = k.sb("dnstage", 128, 2 * 512, F32)
    for t0 in range(0, T, 512):
        for kc in range(KC):
            p = min(128, II - kc * 128)
            k.P.dma(hqt[0:p, kc * 512:(kc + 1) * 512], hq_h[kc * 128:kc * 128 + p, t0:t0 + 512], queue="qsync")
        for n in range(8):
            ps = k.ps()
            for kc in range(KC):
                p = min(128, II - kc * 128)
                k.P.mm(ps[:, 0:512], slab_d[0:p, kc * DM + n * 128: kc * DM + (n + 1) * 128], hqt[0:p, kc * 512:(kc + 1) * 512], start=(kc == 0), stop=(kc == KC - 1), tag="dn")
            s = dstage[:, (n % 2) * 512:(n % 2) * 512 + 512]
            k.P.copy(s, ps[:, 0:512], engine="vector")
            k.P.dma(ffn_o[n * 128:(n + 1) * 128, t0:t0 + 512], s, queue="qscalar")
    return k


def build_l0w():
    k = K("l0w")
    wg_h = k.inp("wg", (352, DM))
    wu_h = k.inp("wu", (352, DM))
    wd_h = k.inp("wd", (128, 2816))
    wgq_o = k.outp("wgq", (352, DM))
    wuq_o = k.outp("wuq", (352, DM))
    wdq_o = k.outp("wdq", (128, 2816))
    k.mkpsums(2)
    for (w_h, w_o, rows, Kd) in ((wg_h, wgq_o, 352, DM), (wu_h, wuq_o, 352, DM), (wd_h, wdq_o, 128, 2816)):
        wt = k.sb("wt", 128, Kd, F32)
        wq = k.sb("wq", 128, Kd, F32)
        sc = k.sb("sc", 128, Kd // 16, F32)
        rsc = k.sb("rsc", 128, Kd // 16, F32)
        qa = k.sb("qa", 128, Kd, F32)
        qk2 = k.sb("qk2", 128, Kd, F32)
        qm2 = k.sb("qm2", 128, Kd, F32)
        for r0 in range(0, rows, 128):
            rp = min(128, rows - r0)
            k.P.dma(wt[0:rp, :], w_h[r0:r0 + rp, :], queue="qsync")
            k.P.red(sc[0:rp, :, None], wt[0:rp, :].rearrange("p (g i) -> p g i", i=16), ALU.max, absval=True)
            k.P.ts(sc[0:rp, :], sc[0:rp, :], 1.0 / 6.0, 1e-12, ALU.mult, ALU.max)
            k.P.reciprocal(rsc[0:rp, :], sc[0:rp, :])
            scb = bass.AP(sc.tensor if hasattr(sc, 'tensor') else sc[:].tensor, sc[:].offset,
                          [list(sc[:].ap[0][:]), list(sc[:].ap[1][:])][:2])
            scg = sc[0:rp, :, None]
            sc0 = bass.AP(scg.tensor, scg.offset, [list(scg.ap[0]), list(scg.ap[1]), [0, 16]])
            rscg = rsc[0:rp, :, None]
            rsc0 = bass.AP(rscg.tensor, rscg.offset, [list(rscg.ap[0]), list(rscg.ap[1]), [0, 16]])
            wtg = wt[0:rp, :].rearrange("p (g i) -> p g i", i=16)
            wqg = wq[0:rp, :].rearrange("p (g i) -> p g i", i=16)
            scr = {"v": wq[0:rp, :], "a": qa[0:rp, :], "kc": qk2[0:rp, :], "m": qm2[0:rp, :]}
            MID = [0.25, 0.75, 1.25, 1.75, 2.5, 3.5, 5.0]
            k.P.tt(wqg, wtg, rsc0, ALU.mult)
            k.P.act(qa[0:rp, :], wq[0:rp, :], AF.Abs)
            k.P.ts(qk2[0:rp, :], qa[0:rp, :], MID[0], None, ALU.is_gt)
            for m_ in MID[1:]:
                k.P.stt(qk2[0:rp, :], qa[0:rp, :], m_, qk2[0:rp, :], ALU.is_gt, ALU.add)
            k.P.ts(qm2[0:rp, :], qk2[0:rp, :], -4.0, 0.0, ALU.add, ALU.max)
            k.P.tt(qk2[0:rp, :], qk2[0:rp, :], qm2[0:rp, :], ALU.add)
            k.P.ts(qm2[0:rp, :], qk2[0:rp, :], -8.0, 0.0, ALU.add, ALU.max)
            k.P.ts(qm2[0:rp, :], qm2[0:rp, :], 0.5, None, ALU.mult)
            k.P.stt(qk2[0:rp, :], qk2[0:rp, :], 0.5, qm2[0:rp, :], ALU.mult, ALU.add)
            k.P.sign(qa[0:rp, :], wq[0:rp, :])
            k.P.tt(qk2[0:rp, :], qk2[0:rp, :], qa[0:rp, :], ALU.mult)
            k.P.tt(wqg, qk2[0:rp, :].rearrange("p (g i) -> p g i", i=16), sc0, ALU.mult)
            k.P.dma(w_o[r0:r0 + rp, :], wq[0:rp, :], queue="qsync")
    return k


def build_l0p():
    k = K("l0p")
    T = TCORE
    y_h = k.inp("y", (2048, T))
    res_h = k.inp("res", (2048, T))
    x_h = k.inp("x", (DM, T))
    ffn_h = k.inp("ffn", (DM, T))
    wout_h = k.inp("woutT", (2048, DM))
    bout_h = k.inp("bout", (DM,))
    x1_o = k.outp("x1", (DM, T))
    k.mkpsums(8)
    t1 = k.fm("t1", 16, T, BF16)
    ybuf = k.sb("yb", 128, T, F32)
    rbuf = k.sb("rb", 128, T, F32)
    for c in range(16):
        k.P.dma(ybuf[:], y_h[c * 128:(c + 1) * 128, :], queue="qsync")
        k.P.dma(rbuf[:], res_h[c * 128:(c + 1) * 128, :], queue="qgpsimd")
        k.P.act(rbuf[:], rbuf[:], AF.Silu)
        k.P.tt(t1.c(c), ybuf[:], rbuf[:], ALU.mult)
    slab_o = load_wslabT(k, wout_h, 2048, DM, "wo")
    bcols = load_bias_cols(k, bout_h, DM, "bout")
    xbuf = k.sb("xb", 128, 512, F32)
    fbuf = k.sb("fb", 128, 512, F32)
    ostage = k.sb("ostage", 128, 2 * 512, F32)

    def evac_out(n, t0, tw, ps):
        s = ostage[:, (t0 // 512 % 2) * 512:(t0 // 512 % 2) * 512 + tw]
        k.P.dma(xbuf[:, 0:tw], x_h[n * 128:(n + 1) * 128, t0:t0 + tw], queue="qsync")
        k.P.dma(fbuf[:, 0:tw], ffn_h[n * 128:(n + 1) * 128, t0:t0 + tw], queue="qgpsimd")
        k.P.ts(s, ps, bcols[:, n:n + 1], None, ALU.add)
        k.P.tt(s, s, xbuf[:, 0:tw], ALU.add)
        k.P.tt(s, s, fbuf[:, 0:tw], ALU.add)
        k.P.dma(x1_o[n * 128:(n + 1) * 128, t0:t0 + tw], s, queue="qscalar")
    linear(k, t1, slab_o, 2048, DM, T, evac_out, tag="wo")
    return k


def _f32(a):
    return np.ascontiguousarray(np.asarray(a), dtype=np.float32)


def _halo_slice(x_dm, t0, t1, pad_l, pad_r):
    D = x_dm.shape[0]
    lo, hi = t0 - pad_l, t1 + pad_r
    out = np.zeros((D, hi - lo), np.float32)
    clo, chi = max(lo, 0), min(hi, x_dm.shape[1])
    out[:, clo - lo:chi - lo] = x_dm[:, clo:chi]
    return out


def run_l0(inputs, x_dm, ks=None):
    kq = build_l0w()
    wg = _f32(inputs["L0_ffn_w_gate"])
    wu = _f32(inputs["L0_ffn_w_up"])
    wd = _f32(inputs["L0_ffn_w_down"])
    maps = []
    for c in range(NCORES):
        maps.append({"wg": wg[c * 352:(c + 1) * 352], "wu": wu[c * 352:(c + 1) * 352],
                     "wd": wd[c * 128:(c + 1) * 128]})
    r = kq.run(maps)
    wgq = np.concatenate([r.results[c]["wgq"] for c in range(NCORES)], 0)
    wuq = np.concatenate([r.results[c]["wuq"] for c in range(NCORES)], 0)
    wdq = np.concatenate([r.results[c]["wdq"] for c in range(NCORES)], 0)

    ka = build_l0a()
    winT = _f32(inputs["L0_attn_W_in"])
    wxT = _f32(inputs["L0_attn_W_x"])
    wdtT = _f32(inputs["L0_attn_W_dt"])
    kconv = _f32(np.transpose(np.asarray(inputs["L0_attn_K_conv"]), (2, 1, 0)).reshape(2048, 4))
    bconv = _f32(inputs["L0_attn_b_conv"])
    bdt = _f32(inputs["L0_attn_b_dt"])
    maps = []
    for b in range(BATCH):
        for h in range(2):
            t0, t1 = h * TCORE, (h + 1) * TCORE
            maps.append({"x": _halo_slice(x_dm[b], t0, t1, 1, 3),
                         "winT": winT, "wxT": wxT, "wdtT": wdtT,
                         "kconv": kconv, "bconv": bconv, "bdt": bdt})
    r = ka.run(maps)
    RES = r.results
    xn_cores = [RES[c]["xn"][:, 1:1 + TCORE] for c in range(NCORES)]
    res_cores = [RES[c]["res"][:, 1:1 + TCORE] for c in range(NCORES)]
    u_full = [np.concatenate([RES[2 * b]["u"], RES[2 * b + 1]["u"]], 1) for b in range(BATCH)]
    d_full = [np.concatenate([RES[2 * b]["delta"], RES[2 * b + 1]["delta"]], 1) for b in range(BATCH)]
    B_full = [np.concatenate([RES[2 * b]["B"], RES[2 * b + 1]["B"]], 1) for b in range(BATCH)]
    C_full = [np.concatenate([RES[2 * b]["Cc"], RES[2 * b + 1]["Cc"]], 1) for b in range(BATCH)]

    if ks is None:
        ks = build_l0s()
    alog = _f32(inputs["L0_attn_A_log"])
    dvec = _f32(inputs["L0_attn_D"])
    maps = []
    for b in range(BATCH):
        for dh in range(2):
            sl = slice(dh * 1024, (dh + 1) * 1024)
            maps.append({"delta": d_full[b][sl], "u": u_full[b][sl],
                         "B": B_full[b], "Cc": C_full[b],
                         "alog": alog[sl], "dd": dvec[sl]})
    r = ks.run(maps)
    y_full = [np.concatenate([r.results[2 * b]["y"], r.results[2 * b + 1]["y"]], 0) for b in range(BATCH)]

    kf1 = build_l0f1()
    wgqT = _f32(wgq.T)
    wuqT = _f32(wuq.T)
    wdqT = _f32(wdq.T)
    maps = []
    for c in range(NCORES):
        maps.append({"xn": xn_cores[c], "xntm": _f32(xn_cores[c].T),
                     "wgqT": wgqT, "wuqT": wuqT})
    r = kf1.run(maps)
    hq_cores = [r.results[c]["hq"] for c in range(NCORES)]
    kf2 = build_l0f2()
    maps = [{"hq": hq_cores[c], "wdqT": wdqT} for c in range(NCORES)]
    r = kf2.run(maps)
    ffn_cores = [r.results[c]["ffn"] for c in range(NCORES)]

    kp = build_l0p()
    woutT = _f32(inputs["L0_attn_W_out"])
    bout = _f32(inputs["L0_attn_b_out"])
    maps = []
    for b in range(BATCH):
        for h in range(2):
            c = 2 * b + h
            t0, t1 = h * TCORE, (h + 1) * TCORE
            maps.append({"y": y_full[b][:, t0:t1], "res": res_cores[c],
                         "x": x_dm[b][:, t0:t1], "ffn": ffn_cores[c],
                         "woutT": woutT, "bout": bout})
    r = kp.run(maps)
    x1_dm = [np.concatenate([r.results[2 * b]["x1"], r.results[2 * b + 1]["x1"]], 1) for b in range(BATCH)]
    return x1_dm


def kernel(**inputs):
    x = _f32(inputs["L0_attn_inputs"])
    x_dm = [np.ascontiguousarray(x[b].T) for b in range(BATCH)]
    ks = build_l0s()
    x_dm = run_l0(inputs, x_dm, ks)
    x_dm = run_l1(inputs, x_dm)
    x_dm = run_l2(inputs, x_dm, ks)
    x_dm = run_l3(inputs, x_dm)
    out = np.stack([x_dm[b].T for b in range(BATCH)], 0)
    return _f32(out)


def _psum_named(k, name, f=512):
    return k.st.enter_context(k.nc.psum_tensor(k.uid(name), [128, f], F32))


def build_l1a():
    k = K("l1a")
    Te = TCORE + 4
    DIN = 2048
    CONV = 2304
    NPROJ = 4384
    x_h = k.inp("x", (DM, Te))
    winT = k.inp("winT", (DM, NPROJ))
    convw_h = k.inp("convw", (CONV, 4))
    convb_h = k.inp("convb", (CONV,))
    dtb_h = k.inp("dtb", (32,))
    xn_o = k.outp("xn", (DM, Te))
    z_o = k.outp("z", (DIN, Te))
    xc_o = k.outp("xc", (CONV, TCORE))
    dt_o = k.outp("dt", (32, TCORE))
    k.mkpsums(8)
    xn_bf = k.fm("xn_bf", 8, Te, BF16)
    layernorm_stream(k, x_h, DM, Te, xn_bf, xn_f32_h=xn_o)

    xbc = k.fm("xbc", 18, Te, BF16)
    slab = load_wslabT(k, winT, DM, 2048, "win", slabN=2336)
    stage = k.sb("stage", 128, 2 * 512, F32)
    dsc1 = k.sb("dsc1", 32, 512, F32)
    dsc2 = k.sb("dsc2", 32, 512, F32)
    dsc3 = k.sb("dsc3", 32, 512, F32)
    dtb_c = k.sb("dtbc", 32, 1, F32)
    k.P.dma(dtb_c[:], dtb_h[:, None], queue="qsync")

    def evac_z(n, t0, tw, ps):
        s = stage[:, (t0 // 512 % 2) * 512:(t0 // 512 % 2) * 512 + tw]
        k.P.copy(s, ps, engine="vector")
        k.P.dma(z_o[n * 128:(n + 1) * 128, t0:t0 + tw], s, queue="qscalar")
    linear(k, xn_bf, slab, DM, 2048, Te, evac_z, tag="winz", slabN=2336)
    load_wslabT(k, winT, DM, 2336, "win2", slab=slab, col0=2048)

    def evac_bc(n, t0, tw, ps):
        if n < 18:
            k.P.copy(xbc.c(n, t0, t0 + tw), ps, engine="scalar")
        else:
            lo = max(t0, 3)
            hi = min(t0 + tw, 3 + TCORE)
            if hi > lo:
                k.P.ts(dsc1[:, 0:tw], ps[0:32, :], dtb_c[:, 0:1], None, ALU.add)
                softplus_tiles(k, dsc1[:, 0:tw], dsc1[:, 0:tw], dsc2[:, 0:tw], dsc3[:, 0:tw])
                k.P.dma(dt_o[:, lo - 3:hi - 3], dsc1[0:32, lo - t0:hi - t0], queue="qscalar")
    linear(k, xn_bf, slab, DM, 2336, Te, evac_bc, tag="winbc")

    kw = k.sb("kw", 128, 18 * 4, F32)
    for c in range(18):
        k.P.dma(kw[:, c * 4:(c + 1) * 4], convw_h[c * 128:(c + 1) * 128, :], queue="qsync")
    bcv = load_bias_cols(k, convb_h, CONV, "convb")
    cacc = k.sb("cacc", 128, 512, F32)
    ustage = k.sb("ustage", 128, 2 * 512, F32)
    for c in range(18):
        for t0 in range(0, TCORE, 512):
            tw = min(512, TCORE - t0)
            base = c * Te + t0
            k.P.ts(cacc[:, 0:tw], xbc.t[:, base:base + tw], kw[:, c * 4:c * 4 + 1], bcv[:, c:c + 1], ALU.mult, ALU.add)
            for i in (1, 2, 3):
                k.P.stt(cacc[:, 0:tw], xbc.t[:, base + i:base + i + tw], kw[:, c * 4 + i:c * 4 + i + 1], cacc[:, 0:tw], ALU.mult, ALU.add)
            s = ustage[:, (t0 // 512 % 2) * 512:(t0 // 512 % 2) * 512 + tw]
            k.P.act(s, cacc[:, 0:tw], AF.Silu)
            k.P.dma(xc_o[c * 128:(c + 1) * 128, t0:t0 + tw], s, queue="qscalar")
    return k


def build_l1s():
    k = K("l1s")
    T = SEQ
    NH = 16
    HD = 64
    NST = 128
    CH = 128
    NCH = T // CH
    xh_h = k.inp("xh", (NH * HD, T))
    dt_h = k.inp("dt", (NH, T))
    B_h = k.inp("B", (NST, T))
    C_h = k.inp("Cc", (NST, T))
    alog_h = k.inp("alog", (NH,))
    drep_h = k.inp("drep", (NH * HD,))
    y_o = k.outp("y", (NH * HD, T))
    k.mkpsums(3)
    psg = [_psum_named(k, "psg") for _ in range(4)]
    psbt = k.st.enter_context(k.nc.psum_tensor(k.uid("psbt"), [128, 128], BF16))
    idf = k.ident_f()
    idb = k.ident_bf()

    B_bf = k.sb("Bbf", 128, T, BF16)
    C_bf = k.sb("Cbf", 128, T, BF16)
    k.P.dma(B_bf[:], B_h, queue="qgpsimd")
    k.P.dma(C_bf[:], C_h, queue="qgpsimd")
    dt_t = k.sb("dtt", NH, T, F32)
    k.P.dma(dt_t[:], dt_h, queue="qsync")
    arow = k.sb("arow", NH, 1, F32)
    k.P.dma(arow[:], alog_h[:, None], queue="qsync")
    k.P.act(arow[:], arow[:], AF.Exp)
    k.P.ts(arow[:], arow[:], -1.0, None, ALU.mult)
    dta = k.sb("dta", NH, T, F32)
    k.P.ts(dta[:], dt_t[:], arow[:, 0:1], None, ALU.mult)
    zrow = k.sb("zrow", NH, CH, F32)
    k.P.memset(zrow[:], 0.0)
    drepc = k.sb("drepc", 64, NH, F32)
    for h in range(NH):
        k.P.dma(drepc[:, h:h + 1], drep_h[h * 64:(h + 1) * 64, None], queue="qsync")

    Gc = k.sb("Gc", NH, CH, F32)
    nGc = k.sb("nGc", NH, CH, F32)
    G_tm = k.sb("Gtm", 128, NH, F32)
    negG = k.sb("negG", 128, NH, F32)
    dt_tm = k.sb("dttm", 128, NH, F32)
    grow = k.sb("grow", 1, NH * CH, F32)
    ngrow = k.sb("ngrow", 1, NH * CH, F32)
    Xc2 = k.sb("Xc2", 64, NH * CH, F32)
    bct = k.sb("bct", 128, CH, F32)
    btm = k.sb("btm", 128, NST, BF16)
    ltf = k.sb("ltf", 128, CH, F32)
    stbf = k.sb("stbf", 128, CH, BF16)
    dtxf = k.sb("dtxf", 128, HD, F32)
    dtx_bf = k.sb("dtxbf", 128, HD, BF16)
    dtxw_bf = k.sb("dtxwbf", 128, HD, BF16)
    dec_c = k.sb("dec", 128, 1, F32)
    gam_c = k.sb("gam", 128, 1, F32)
    elf = k.sb("elf", 128, CH, F32)
    cw_bf = k.sb("cwbf", 128, CH, BF16)
    S = k.sb("S", 128, NH * HD, F32)
    S_bf = k.sb("Sbf", 128, NH * HD, BF16)
    k.P.memset(S[:], 0.0)
    k.P.memset(S_bf[:], 0.0)
    ystage = k.sb("ystage", 64, CH, F32)
    lscr = k.sb("lscr", 128, CH, F32)
    maskadd = k.const_sb("c_maskadd", np.where(np.triu(np.ones((128, 128))) > 0, 0.0, -1e9).astype(np.float32))

    for ci in range(NCH):
        c0 = ci * CH
        k.P.scan(Gc[:], dta[:, c0:c0 + CH], zrow[:], 0.0, op0=ALU.add, op1=ALU.add, tag="gcum")
        pst = k.ps()
        k.P.transpose(pst[0:CH, 0:NH], Gc[:], idf[0:NH, 0:NH], tag="gT")
        k.P.copy(G_tm[:, 0:NH], pst[0:CH, 0:NH], engine="vector")
        k.P.ts(negG[:, 0:NH], G_tm[:, 0:NH], -1.0, None, ALU.mult)
        pst2 = k.ps()
        k.P.transpose(pst2[0:CH, 0:NH], dt_t[:, c0:c0 + CH], idf[0:NH, 0:NH], tag="dtT")
        k.P.copy(dt_tm[:, 0:NH], pst2[0:CH, 0:NH], engine="vector")
        k.P.dma(grow[:], Gc[:], queue="qsync")
        for j in range(4):
            k.P.mm(psg[j][:, 0:512], k.ones_row_f(), grow[:, j * 512:(j + 1) * 512], tag="gbc")
        for h in range(NH):
            k.P.dma(Xc2[:, h * CH:(h + 1) * CH], xh_h[h * 64:(h + 1) * 64, c0:c0 + CH], queue="qgpsimd")
        psbc = k.ps()
        k.P.mm(psbc[:, 0:CH], B_bf[:, c0:c0 + CH], C_bf[:, c0:c0 + CH], tag="bctmm")
        k.P.copy(bct[:], psbc[:, 0:CH], engine="vector")
        k.P.transpose(psbt[:, 0:NST], B_bf[:, c0:c0 + CH], idb, tag="bT")
        k.P.copy(btm[:], psbt[:, 0:NST], engine="vector")
        for h in range(NH):
            gi = psg[h // 4][:, (h % 4) * 128:(h % 4) * 128 + 128]
            k.P.tt(lscr[:], gi, maskadd[:], ALU.add, tag="ltmaskadd")
            k.P.act(ltf[:], lscr[:], AF.Exp, bias=negG[:, h:h + 1], tag="ltexp")
            k.P.tt(stbf[:], ltf[:], bct[:], ALU.mult, tag="stmul")
            pxt = k.ps()
            k.P.transpose(pxt[0:CH, 0:HD], Xc2[:, h * CH:(h + 1) * CH], idf[0:64, 0:64], tag="xT")
            k.P.ts(dtxf[:, 0:HD], pxt[0:CH, 0:HD], dt_tm[:, h:h + 1], None, ALU.mult, tag="dtx")
            k.P.copy(dtx_bf[:, 0:HD], dtxf[:, 0:HD], engine="gpsimd")
            k.P.act(dec_c[:, 0:1], gi[:, CH - 1:CH], AF.Exp, bias=negG[:, h:h + 1], tag="dec")
            k.P.act(gam_c[:, 0:1], gi[:, CH - 1:CH], AF.Exp, tag="gam")
            k.P.ts(dtxw_bf[:, 0:HD], dtxf[:, 0:HD], dec_c[:, 0:1], None, ALU.mult, tag="dtxw")
            psy = k.ps()
            k.P.mm(psy[0:HD, 0:CH], dtx_bf[:, 0:HD], stbf[:], start=True, stop=False, tag="yintra")
            k.P.act(elf[:], gi, AF.Exp, tag="el")
            k.P.tt(cw_bf[:], C_bf[:, c0:c0 + CH], elf[:], ALU.mult, tag="cw")
            k.P.mm(psy[0:HD, 0:CH], S_bf[:, h * HD:(h + 1) * HD], cw_bf[:], start=False, stop=True, tag="yinter")
            k.P.stt(ystage[:], Xc2[:, h * CH:(h + 1) * CH], drepc[:, h:h + 1], psy[0:HD, 0:CH], ALU.mult, ALU.add, tag="yev")
            k.P.dma(y_o[h * HD:(h + 1) * HD, c0:c0 + CH], ystage[:], queue="qscalar")
            pss = k.ps()
            k.P.mm(pss[:, 0:HD], btm[:], dtxw_bf[:, 0:HD], tag="contrib")
            k.P.stt(S[:, h * HD:(h + 1) * HD], S[:, h * HD:(h + 1) * HD], gam_c[:, 0:1], pss[:, 0:HD], ALU.mult, ALU.add, tag="supd")
            k.P.copy(S_bf[:, h * HD:(h + 1) * HD], S[:, h * HD:(h + 1) * HD], engine="gpsimd")
    return k


def build_l1p():
    k = K("l1p")
    T = TCORE
    DIN = 2048
    y_h = k.inp("y", (DIN, T))
    z_h = k.inp("z", (DIN, T))
    x_h = k.inp("x", (DM, T))
    woutT_h = k.inp("woutT", (DIN, DM))
    nw_h = k.inp("nw", (DIN,))
    x2a_o = k.outp("x2a", (DM, T))
    k.mkpsums(8)
    t1 = k.fm("t1", 16, T, BF16)
    ybuf = k.sb("yb", 128, T, F32)
    zbuf = k.sb("zb", 128, T, F32)
    sq = k.sb("sq", 128, 512, BF16)
    nwc = load_bias_cols(k, nw_h, DIN, "nw")
    for c in range(16):
        k.P.dma(ybuf[:], y_h[c * 128:(c + 1) * 128, :], queue="qsync")
        k.P.dma(zbuf[:], z_h[c * 128:(c + 1) * 128, :], queue="qgpsimd")
        k.P.act(zbuf[:], zbuf[:], AF.Silu)
        k.P.tt(t1.c(c), ybuf[:], zbuf[:], ALU.mult)
    rows = k.sb("rows", 1, 3 * 512, F32)
    yn = k.fm("yn", 16, T, BF16)
    for t0 in range(0, T, 512):
        psq = k.ps()
        for c in range(16):
            k.P.act(sq[:], t1.c(c, t0, t0 + 512), AF.Square, tag="rsq")
            k.P.mm(psq[0:1, 0:512], k.ones_col_bf(), sq[:], start=(c == 0), stop=(c == 15), tag="rsum")
        r0 = rows[:, 0:512]
        k.P.ts(r0, psq[0:1, 0:512], 1.0 / DIN, 1e-5, ALU.mult, ALU.add)
        k.P.act(rows[:, 512:1024], r0, AF.Sqrt)
        k.P.reciprocal(r0, rows[:, 512:1024])
        bps = bcast_row_psum(k, r0, 512)
        for c in range(16):
            k.P.stt(yn.c(c, t0, t0 + 512), t1.c(c, t0, t0 + 512), nwc[:, c:c + 1], bps[:, 0:512], ALU.mult, ALU.mult, tag="ynorm")
    slab_o = load_wslabT(k, woutT_h, DIN, DM, "wo")
    xbuf = k.sb("xb", 128, 512, F32)
    ostage = k.sb("ostage", 128, 2 * 512, F32)

    def evac_out(n, t0, tw, ps):
        s = ostage[:, (t0 // 512 % 2) * 512:(t0 // 512 % 2) * 512 + tw]
        k.P.dma(xbuf[:, 0:tw], x_h[n * 128:(n + 1) * 128, t0:t0 + tw], queue="qsync")
        k.P.tt(s, ps, xbuf[:, 0:tw], ALU.add)
        k.P.dma(x2a_o[n * 128:(n + 1) * 128, t0:t0 + tw], s, queue="qscalar")
    linear(k, yn, slab_o, DIN, DM, T, evac_out, tag="wo")
    return k


def build_l1f():
    k = K("l1f")
    T = TCORE
    FF = 4096
    xn_h = k.inp("xn", (DM, T))
    wfcT_h = k.inp("wfcT", (DM, FF))
    wprT_h = k.inp("wprT", (FF, DM))
    x2a_h = k.inp("x2a", (DM, T))
    x2_o = k.outp("x2", (DM, T))
    k.mkpsums(8)
    xn_bf = k.fm("xn_bf", 8, T, BF16)
    for c in range(8):
        k.P.dma(xn_bf.c(c), xn_h[c * 128:(c + 1) * 128, :], queue="qgpsimd")
    slab_fc = load_wslabT(k, wfcT_h, DM, FF, "wfc")
    slab_pr = load_wslabT(k, wprT_h, FF, DM, "wpr")
    hsp = k.sb("hsp", 128, 32 * 512, BF16)
    mem = k.sb("mem", 128, 512, F32)
    acc = k.sb("acc", 128, 512, F32)
    rst = k.sb("rst", 128, 512, F32)
    x2abuf = k.sb("x2ab", 128, 512, F32)
    fstage = k.sb("fstage", 128, 2 * 512, F32)
    for t0 in range(0, T, 512):
        for n in range(32):
            ps = k.ps()
            for kc in range(8):
                k.P.mm(ps[:, 0:512], slab_fc[:, kc * FF + n * 128: kc * FF + (n + 1) * 128], xn_bf.c(kc, t0, t0 + 512), start=(kc == 0), stop=(kc == 7), tag="fcmm")
            k.P.ts(mem[:], ps[:, 0:512], 0.0, None, ALU.max, tag="fr")
            k.P.tt(hsp[:, n * 512:(n + 1) * 512], mem[:], mem[:], ALU.mult, tag="fsq")
        for n2 in range(8):
            ps = k.ps()
            for kc in range(32):
                k.P.mm(ps[:, 0:512], slab_pr[:, kc * DM + n2 * 128:kc * DM + (n2 + 1) * 128], hsp[:, kc * 512:(kc + 1) * 512], start=(kc == 0), stop=(kc == 31), tag="prmm")
            k.P.copy(mem[:], ps[:, 0:512], engine="vector")
            k.P.ts(acc[:], mem[:], 1.0, None, ALU.is_gt, tag="spk1")
            for it in range(4):
                k.P.ts(rst[:], mem[:], 1.0, None, ALU.is_gt, tag="rst")
                k.P.stt(mem[:], mem[:], 0.9, ps[:, 0:512], ALU.mult, ALU.add, tag="mup")
                k.P.tt(mem[:], mem[:], rst[:], ALU.subtract, tag="mrst")
                k.P.stt(acc[:], mem[:], 1.0, acc[:], ALU.is_gt, ALU.add, tag="spka")
            s = fstage[:, (n2 % 2) * 512:(n2 % 2) * 512 + 512]
            k.P.dma(x2abuf[:], x2a_h[n2 * 128:(n2 + 1) * 128, t0:t0 + 512], queue="qsync")
            k.P.stt(s, acc[:], 0.2, x2abuf[:], ALU.mult, ALU.add, tag="fout")
            k.P.dma(x2_o[n2 * 128:(n2 + 1) * 128, t0:t0 + 512], s, queue="qscalar")
    return k


def run_l1(inputs, x_dm):
    ka = build_l1a()
    winT = _f32(np.asarray(inputs["L1_attn_W_in"]).T)
    convw = _f32(inputs["L1_attn_conv_w"])
    convb = _f32(inputs["L1_attn_conv_b"])
    dtb = _f32(inputs["L1_attn_dt_bias"])
    maps = []
    for b in range(BATCH):
        for h in range(2):
            t0, t1 = h * TCORE, (h + 1) * TCORE
            maps.append({"x": _halo_slice(x_dm[b], t0, t1, 3, 1),
                         "winT": winT, "convw": convw, "convb": convb, "dtb": dtb})
    r = ka.run(maps)
    RES = r.results
    xn_cores = [RES[c]["xn"][:, 3:3 + TCORE] for c in range(NCORES)]
    z_cores = [RES[c]["z"][:, 3:3 + TCORE] for c in range(NCORES)]
    xc_full = [np.concatenate([RES[2 * b]["xc"], RES[2 * b + 1]["xc"]], 1) for b in range(BATCH)]
    dt_full = [np.concatenate([RES[2 * b]["dt"], RES[2 * b + 1]["dt"]], 1) for b in range(BATCH)]

    ks = build_l1s()
    alog = _f32(inputs["L1_attn_A_log"])
    drep = _f32(np.repeat(np.asarray(inputs["L1_attn_D"]), 64))
    maps = []
    for b in range(BATCH):
        for hg in range(2):
            rs = slice(hg * 1024, (hg + 1) * 1024)
            maps.append({"xh": xc_full[b][:2048][rs], "dt": dt_full[b][hg * 16:(hg + 1) * 16],
                         "B": xc_full[b][2048:2176], "Cc": xc_full[b][2176:2304],
                         "alog": alog[hg * 16:(hg + 1) * 16], "drep": drep[rs]})
    r = ks.run(maps)
    y_full = [np.concatenate([r.results[2 * b]["y"], r.results[2 * b + 1]["y"]], 0) for b in range(BATCH)]

    kp = build_l1p()
    woutT = _f32(np.asarray(inputs["L1_attn_W_out"]).T)
    nw = _f32(inputs["L1_attn_norm_w"])
    maps = []
    for b in range(BATCH):
        for h in range(2):
            c = 2 * b + h
            t0, t1 = h * TCORE, (h + 1) * TCORE
            maps.append({"y": y_full[b][:, t0:t1], "z": z_cores[c],
                         "x": x_dm[b][:, t0:t1], "woutT": woutT, "nw": nw})
    r = kp.run(maps)
    x2a_cores = [r.results[c]["x2a"] for c in range(NCORES)]

    kf = build_l1f()
    wfcT = _f32(np.asarray(inputs["L1_ffn_W_fc"]).T)
    wprT = _f32(np.asarray(inputs["L1_ffn_W_proj"]).T)
    maps = []
    for c in range(NCORES):
        maps.append({"xn": xn_cores[c], "wfcT": wfcT, "wprT": wprT, "x2a": x2a_cores[c]})
    r = kf.run(maps)
    x2_dm = [np.concatenate([r.results[2 * b]["x2"], r.results[2 * b + 1]["x2"]], 1) for b in range(BATCH)]
    return x2_dm



def build_l2a_pre(parts=5):
    k = K("l2a_pre")
    Te = TCORE + 4
    x_h = k.inp("x", (DM, Te))
    nw_h = k.inp("nw", (DM,))
    winT = k.inp("winT", (DM, 4096))
    convw_h = k.inp("convw", (2048, 4))
    convb_h = k.inp("convb", (2048,))
    wxT_h = k.inp("wxT", (2048, 96))
    wdtT_h = k.inp("wdtT", (64, 2048))
    bdt_h = k.inp("bdt", (2048,))
    z_o = k.outp("z", (2048, Te))
    xc_o = k.outp("xc", (2048, TCORE))
    delta_o = k.outp("delta", (2048, TCORE))
    B_o = k.outp("B", (16, TCORE))
    C_o = k.outp("Cc", (16, TCORE))
    k.mkpsums(8)
    xn_bf = k.fm("xn_bf", 8, Te, BF16)
    nwc = load_bias_cols(k, nw_h, DM, "nw")
    xt = k.sb("rms_x", 128, 8 * 512, F32)
    sqb = k.sb("rms_sq", 128, 512, BF16)
    rows = k.sb("rms_rows", 1, 2 * 512, F32)
    for t0 in range(0, Te, 512):
        tw = min(512, Te - t0)
        for c in range(8):
            k.P.dma(xt[:, c * 512:c * 512 + tw], x_h[c * 128:(c + 1) * 128, t0:t0 + tw])
        psq = k.ps()
        for c in range(8):
            k.P.act(sqb[:, 0:tw], xt[:, c * 512:c * 512 + tw], AF.Square, tag="rsq")
            k.P.mm(psq[0:1, 0:tw], k.ones_col_bf(), sqb[:, 0:tw], start=(c == 0), stop=(c == 7), tag="rsum")
        r0 = rows[:, 0:tw]
        k.P.ts(r0, psq[0:1, 0:tw], 1.0 / DM, 1e-5, ALU.mult, ALU.add)
        k.P.act(rows[:, 512:512 + tw], r0, AF.Sqrt)
        k.P.reciprocal(r0, rows[:, 512:512 + tw])
        bps = bcast_row_psum(k, r0, tw)
        for c in range(8):
            k.P.stt(xn_bf.c(c, t0, t0 + tw), xt[:, c * 512:c * 512 + tw], nwc[:, c:c + 1], bps[:, 0:tw], ALU.mult, ALU.mult, tag="rnorm")
    if parts < 1:
        return k
    slab = load_wslabT(k, winT, DM, 2048, "win", col0=2048)
    stage = k.sb("stage", 128, 2 * 512, F32)

    def evac_z(n, t0, tw, ps):
        s = stage[:, (t0 // 512 % 2) * 512:(t0 // 512 % 2) * 512 + tw]
        k.P.copy(s, ps, engine="vector")
        k.P.dma(z_o[n * 128:(n + 1) * 128, t0:t0 + tw], s, queue="qscalar")
    linear(k, xn_bf, slab, DM, 2048, Te, evac_z, tag="winz")
    if parts < 2:
        return k
    load_wslabT(k, winT, DM, 2048, "win2", slab=slab, col0=0)
    kw = k.sb("kw", 128, 16 * 4, F32)
    for c in range(16):
        k.P.dma(kw[:, c * 4:(c + 1) * 4], convw_h[c * 128:(c + 1) * 128, :], queue="qsync")
    bcv = load_bias_cols(k, convb_h, 2048, "convb")
    xi_c = k.sb("xi_c", 128, Te, BF16)
    xc_bf = k.fm("xc_bf", 16, TCORE, BF16)
    cacc = k.sb("cacc", 128, 512, F32)
    ustage = k.sb("ustage", 128, 2 * 512, F32)
    KC = 8
    for n in range(16):
        for t0 in range(0, Te, 512):
            tw = min(512, Te - t0)
            ps = k.ps()
            for kc in range(KC):
                k.P.mm(ps[:, 0:tw], slab[:, kc * 2048 + n * 128:(kc * 2048 + n * 128 + 128)], xn_bf.c(kc, t0, t0 + tw), start=(kc == 0), stop=(kc == KC - 1), tag="xi")
            k.P.copy(xi_c[:, t0:t0 + tw], ps[:, 0:tw], engine="scalar")
        for t0 in range(0, TCORE, 512):
            tw = min(512, TCORE - t0)
            k.P.ts(cacc[:, 0:tw], xi_c[:, t0:t0 + tw], kw[:, n * 4:n * 4 + 1], bcv[:, n:n + 1], ALU.mult, ALU.add)
            for i in (1, 2, 3):
                k.P.stt(cacc[:, 0:tw], xi_c[:, t0 + i:t0 + i + tw], kw[:, n * 4 + i:n * 4 + i + 1], cacc[:, 0:tw], ALU.mult, ALU.add)
            s = ustage[:, (t0 // 512 % 2) * 512:(t0 // 512 % 2) * 512 + tw]
            k.P.act(s, cacc[:, 0:tw], AF.Silu)
            k.P.copy(xc_bf.c(n, t0, t0 + tw), s, engine="gpsimd")
            k.P.dma(xc_o[n * 128:(n + 1) * 128, t0:t0 + tw], s, queue="qscalar")
    if parts < 3:
        return k
    slab_x = load_wslabT(k, wxT_h, 2048, 96, "wx")
    xdbl = k.sb("xdbl", 96, TCORE, F32)
    xdbl_bf = k.sb("xdbl_bf", 64, TCORE, BF16)

    def evac_xdbl(n, t0, tw, ps):
        k.P.copy(xdbl[:, t0:t0 + tw], ps, engine="vector")
        k.P.copy(xdbl_bf[:, t0:t0 + tw], ps[0:64, :], engine="scalar")
    linear(k, xc_bf, slab_x, 2048, 96, TCORE, evac_xdbl, tag="wx")
    k.P.dma(B_o, xdbl[64:80, :], queue="qsync")
    k.P.dma(C_o, xdbl[80:96, :], queue="qsync")
    if parts < 4:
        return k
    slab_dt = load_wslabT(k, wdtT_h, 64, 2048, "wdt")
    bdt_cols = load_bias_cols(k, bdt_h, 2048, "bdt")
    xdbl_fm = Fm(xdbl_bf, 1, TCORE)
    dsc1 = k.sb("dsc1", 128, 512, F32)
    dsc2 = k.sb("dsc2", 128, 512, F32)
    dstage = k.sb("dstage", 128, 2 * 512, F32)

    def evac_delta(n, t0, tw, ps):
        s = dstage[:, (t0 // 512 % 2) * 512:(t0 // 512 % 2) * 512 + tw]
        k.P.ts(s, ps, bdt_cols[:, n:n + 1], None, ALU.add)
        softplus_tiles(k, s, s, dsc1[:, 0:tw], dsc2[:, 0:tw])
        k.P.dma(delta_o[n * 128:(n + 1) * 128, t0:t0 + tw], s, queue="qscalar")
    linear(k, xdbl_fm, slab_dt, 64, 2048, TCORE, evac_delta, tag="wdt")
    return k


def build_l2a_post():
    k = K("l2a_post")
    T = TCORE
    y_h = k.inp("y", (2048, T))
    z_h = k.inp("z", (2048, T))
    x_h = k.inp("x", (DM, T))
    woutT_h = k.inp("woutT", (2048, DM))
    xo_o = k.outp("xo", (DM, T))
    k.mkpsums(8)
    t1 = k.fm("t1", 16, T, BF16)
    ybuf = k.sb("yb", 128, T, F32)
    zbuf = k.sb("zb", 128, T, F32)
    for c in range(16):
        k.P.dma(ybuf[:], y_h[c * 128:(c + 1) * 128, :], queue="qsync")
        k.P.dma(zbuf[:], z_h[c * 128:(c + 1) * 128, :], queue="qgpsimd")
        k.P.act(zbuf[:], zbuf[:], AF.Silu)
        k.P.tt(t1.c(c), ybuf[:], zbuf[:], ALU.mult)
    slab_o = load_wslabT(k, woutT_h, 2048, DM, "wo")
    xbuf = k.sb("xb", 128, 512, F32)
    ostage = k.sb("ostage", 128, 2 * 512, F32)

    def evac_out(n, t0, tw, ps):
        s = ostage[:, (t0 // 512 % 2) * 512:(t0 // 512 % 2) * 512 + tw]
        k.P.dma(xbuf[:, 0:tw], x_h[n * 128:(n + 1) * 128, t0:t0 + tw], queue="qsync")
        k.P.tt(s, ps, xbuf[:, 0:tw], ALU.add)
        k.P.dma(xo_o[n * 128:(n + 1) * 128, t0:t0 + tw], s, queue="qscalar")
    linear(k, t1, slab_o, 2048, DM, T, evac_out, tag="wo")
    return k


def run_l2a(inputs, x_dm, ks_scan):
    kpre = build_l2a_pre()
    kpost = build_l2a_post()
    for li in range(4):
        winT = _f32(np.asarray(inputs["L2_attn_W_in"][li]).T)
        nw = _f32(inputs["L2_attn_norm_w"][li])
        convw = _f32(inputs["L2_attn_conv_w"][li])
        convb = _f32(inputs["L2_attn_conv_b"][li])
        wxT = _f32(np.asarray(inputs["L2_attn_W_x"][li]).T)
        wdtT = _f32(np.asarray(inputs["L2_attn_W_dt"][li]).T)
        bdt = _f32(inputs["L2_attn_b_dt"][li])
        maps = []
        for b in range(BATCH):
            for h in range(2):
                t0, t1 = h * TCORE, (h + 1) * TCORE
                maps.append({"x": _halo_slice(x_dm[b], t0, t1, 3, 1), "nw": nw,
                             "winT": winT, "convw": convw, "convb": convb,
                             "wxT": wxT, "wdtT": wdtT, "bdt": bdt})
        r = kpre.run(maps)
        RES = r.results
        z_cores = [RES[c]["z"][:, 3:3 + TCORE] for c in range(NCORES)]
        xc_full = [np.concatenate([RES[2 * b]["xc"], RES[2 * b + 1]["xc"]], 1) for b in range(BATCH)]
        d_full = [np.concatenate([RES[2 * b]["delta"], RES[2 * b + 1]["delta"]], 1) for b in range(BATCH)]
        B_full = [np.concatenate([RES[2 * b]["B"], RES[2 * b + 1]["B"]], 1) for b in range(BATCH)]
        C_full = [np.concatenate([RES[2 * b]["Cc"], RES[2 * b + 1]["Cc"]], 1) for b in range(BATCH)]
        alog = _f32(inputs["L2_attn_A_log"][li])
        dvec = _f32(inputs["L2_attn_D"][li])
        maps = []
        for b in range(BATCH):
            for dh in range(2):
                sl = slice(dh * 1024, (dh + 1) * 1024)
                maps.append({"delta": d_full[b][sl], "u": xc_full[b][sl],
                             "B": B_full[b], "Cc": C_full[b],
                             "alog": alog[sl], "dd": dvec[sl]})
        r = ks_scan.run(maps)
        y_full = [np.concatenate([r.results[2 * b]["y"], r.results[2 * b + 1]["y"]], 0) for b in range(BATCH)]
        woutT = _f32(np.asarray(inputs["L2_attn_W_out"][li]).T)
        maps = []
        for b in range(BATCH):
            for h in range(2):
                c = 2 * b + h
                t0, t1 = h * TCORE, (h + 1) * TCORE
                maps.append({"y": y_full[b][:, t0:t1], "z": z_cores[c],
                             "x": x_dm[b][:, t0:t1], "woutT": woutT})
        r = kpost.run(maps)
        x_dm = [np.concatenate([r.results[2 * b]["xo"], r.results[2 * b + 1]["xo"]], 1) for b in range(BATCH)]
    return x_dm



def build_l2f_attn():
    k = K("l2fa")
    T = TCORE
    Te2 = T + 256
    NTIL = Te2 // 128
    x_h = k.inp("x", (DM, T + 257))
    wqkv_h = k.inp("wqkv", (DM, 3072))
    wo_h = k.inp("wo", (DM, DM))
    qsc_h = k.inp("qsc", (128,))
    ksc_h = k.inp("ksc", (128,))
    ctq_h = k.inp("ctq", (128, Te2))
    stq_h = k.inp("stq", (128, Te2))
    ctk_h = k.inp("ctk", (128, Te2))
    stk_h = k.inp("stk", (128, Te2))
    m0_h = k.inp("m0", (128, 512))
    m1_h = k.inp("m1", (128, 512))
    mf0_h = k.inp("mf0", (128, 512))
    mf1_h = k.inp("mf1", (128, 512))
    xatt_o = k.outp("xatt", (DM, T))
    k.mkpsums(6)
    psbt = k.st.enter_context(k.nc.psum_tensor(k.uid("psbt"), [128, 512], BF16))
    perm64 = np.zeros((128, 128), ml_dtypes.bfloat16)
    for p in range(128):
        perm64[(p % 64 + 32) % 64 + 64 * (p // 64), p] = 1
    permc = k.const_sb("c_perm64", perm64)[:]
    b64 = np.zeros((128, 128), ml_dtypes.bfloat16)
    for kk in range(128):
        for p in range(128):
            if kk // 64 == p // 64:
                b64[kk, p] = 1
    b64c = k.const_sb("c_b64", b64)[:]
    masks = {}
    for nm, hh in (("m0", m0_h), ("m1", m1_h), ("mf0", mf0_h), ("mf1", mf1_h)):
        t = k.sb("msk_" + nm, 128, 512, F32)
        k.P.dma(t[:], hh, queue="qsync")
        masks[nm] = t
    qscc = k.sb("qscc", 128, 1, F32)
    kscc = k.sb("kscc", 128, 1, F32)
    k.P.dma(qscc[:], qsc_h[:, None], queue="qsync")
    k.P.dma(kscc[:], ksc_h[:, None], queue="qsync")

    slab = k.sb("wslab", 128, 8 * 1024, BF16)
    xs_t = k.sb("xs_t", 128, 8 * 512, BF16)
    tab = k.sb("tab", 128, 2 * 512, F32)
    q_rot = k.fm("q_rot", 8, Te2, BF16)
    k_rot = k.fm("k_rot", 8, Te2, BF16)
    v_tm = k.sb("v_tm", 128, NTIL * 1024, BF16)
    sq_b = k.sb("sq_b", 128, 512, BF16)
    nsc2 = k.sb("nsc2", 128, 512, F32)
    qa_f = k.sb("qa_f", 128, 512, F32)
    qa_b = k.sb("qa_b", 128, 512, BF16)
    rt1 = k.sb("rt1", 128, 512, F32)
    rt2 = k.sb("rt2", 128, 512, F32)
    t_tiles = [(0, 512), (512, 512), (1024, 512), (1536, 512), (2048, 256)]

    def build_xs(t0, tw):
        for c in range(8):
            off = 1 if c < 4 else 0
            k.P.dma(xs_t[:, c * 512:c * 512 + tw], x_h[c * 128:(c + 1) * 128, t0 + off:t0 + off + tw], queue="qgpsimd")

    for (col0, ct_h, st_h, scc, rot_fm) in ((0, ctq_h, stq_h, qscc, q_rot), (1024, ctk_h, stk_h, kscc, k_rot)):
        load_wslabT(k, wqkv_h, DM, 1024, "wqk", slab=slab, col0=col0)
        for (t0, tw) in t_tiles:
            build_xs(t0, tw)
            k.P.dma(tab[:, 0:tw], ct_h[:, t0:t0 + tw], queue="qsync")
            k.P.dma(tab[:, 512:512 + tw], st_h[:, t0:t0 + tw], queue="qscalar")
            for n in range(8):
                ps = k.ps()
                for kc in range(8):
                    k.P.mm(ps[:, 0:tw], slab[:, kc * 1024 + n * 128:kc * 1024 + n * 128 + 128], xs_t[:, kc * 512:kc * 512 + tw], start=(kc == 0), stop=(kc == 7), tag="qkmm")
                k.P.act(sq_b[:, 0:tw], ps[:, 0:tw], AF.Square, tag="l2sq")
                psn = k.ps()
                k.P.mm(psn[:, 0:tw], b64c, sq_b[:, 0:tw], tag="nsmm")
                k.P.ts(nsc2[:, 0:tw], psn[:, 0:tw], 1e-24, None, ALU.max, tag="l2mx")
                k.P.act(nsc2[:, 0:tw], nsc2[:, 0:tw], AF.Ln, tag="l2ln")
                k.P.act(nsc2[:, 0:tw], nsc2[:, 0:tw], AF.Exp, scale=-0.5, tag="l2ex")
                k.P.stt(qa_f[:, 0:tw], ps[:, 0:tw], scc[:, 0:1], nsc2[:, 0:tw], ALU.mult, ALU.mult, tag="l2ap")
                k.P.copy(qa_b[:, 0:tw], qa_f[:, 0:tw], engine="gpsimd")
                psp = k.ps()
                k.P.mm(psp[:, 0:tw], permc, qa_b[:, 0:tw], tag="perm")
                k.P.tt(rt1[:, 0:tw], qa_f[:, 0:tw], tab[:, 0:tw], ALU.mult, tag="rotc")
                k.P.tt(rt2[:, 0:tw], psp[:, 0:tw], tab[:, 512:512 + tw], ALU.mult, tag="rots")
                k.P.tt(rot_fm.c(n, t0, t0 + tw), rt1[:, 0:tw], rt2[:, 0:tw], ALU.add, tag="rota")

    load_wslabT(k, wqkv_h, DM, 1024, "wv", slab=slab, col0=2048)
    for (t0, tw) in t_tiles:
        build_xs(t0, tw)
        for tsub in range(tw // 128):
            ti = t0 // 128 + tsub
            for nh in range(2):
                ps = k.ps()
                for kc in range(8):
                    k.P.mm(ps[:, 0:512], xs_t[:, kc * 512 + tsub * 128:kc * 512 + tsub * 128 + 128], slab[:, kc * 1024 + nh * 512:kc * 1024 + (nh + 1) * 512], start=(kc == 0), stop=(kc == 7), tag="vmm")
                k.P.copy(v_tm[:, ti * 1024 + nh * 512:ti * 1024 + (nh + 1) * 512], ps[:, 0:512], engine="scalar")

    afm = k.fm("afm", 8, T, BF16)
    NR = 3
    smsk = [k.sb("smsk", 128, 512, F32) for _ in range(NR)]
    nmax = [k.sb("nmax", 128, 1, F32) for _ in range(NR)]
    rsum = [k.sb("rsum", 128, 1, F32) for _ in range(NR)]
    probs = [k.sb("probs", 128, 512, BF16) for _ in range(NR)]
    probsT = [k.sb("probsT", 128, 512, BF16) for _ in range(NR)]
    it = 0
    for w in range(8):
        kcol0 = 256 * w
        for h in range(16):
            hc, hb = h // 2, (h % 2) * 64
            for s in range(2):
                r_ = it % NR
                it += 1
                qc0 = 256 + 256 * w + 128 * s
                ps = k.ps()
                k.P.mm(ps[:, 0:512], q_rot.c(hc, qc0, qc0 + 128)[hb:hb + 64, :], k_rot.c(hc, kcol0, kcol0 + 512)[hb:hb + 64, :], tag="score")
                mk = masks[("mf" if w == 0 else "m") + str(s)]
                k.P.tt(smsk[r_][:], ps[:, 0:512], mk[:], ALU.add, tag="smask")
                k.P.red(nmax[r_][:, 0:1], smsk[r_][:], ALU.max, tag="smax")
                k.P.ts(nmax[r_][:, 0:1], nmax[r_][:, 0:1], -8.0, None, ALU.mult, tag="sneg")
                k.P.act(probs[r_][:], smsk[r_][:], AF.Exp, scale=8.0, bias=nmax[r_][:, 0:1], accum_out=rsum[r_][:, 0:1], tag="sexp")
                k.P.reciprocal(rsum[r_][:, 0:1], rsum[r_][:, 0:1], tag="srec")
                k.P.ts(probs[r_][:], probs[r_][:], rsum[r_][:, 0:1], None, ALU.mult, tag="snorm")
                for j in range(4):
                    k.P.transpose(psbt[:, j * 128:(j + 1) * 128], probs[r_][:, j * 128:(j + 1) * 128], k.ident_bf(), tag="ptr")
                k.P.copy(probsT[r_][:], psbt[:], engine="vector")
                psav = k.ps()
                for j in range(4):
                    ti = 2 * w + j
                    k.P.mm(psav[0:64, 0:128], v_tm[:, ti * 1024 + h * 64:ti * 1024 + (h + 1) * 64], probsT[r_][:, j * 128:(j + 1) * 128], start=(j == 0), stop=(j == 3), tag="av")
                k.P.copy(afm.c(hc, 256 * w + 128 * s, 256 * w + 128 * s + 128)[hb:hb + 64, :], psav[0:64, 0:128], engine="scalar")

    load_wslabT(k, wo_h, DM, DM, "wo", slab=slab)
    xbuf = k.sb("xb", 128, 512, F32)
    ostage = k.sb("ostage", 128, 2 * 512, F32)

    def evac_out(n, t0, tw, ps):
        s = ostage[:, (t0 // 512 % 2) * 512:(t0 // 512 % 2) * 512 + tw]
        k.P.dma(xbuf[:, 0:tw], x_h[n * 128:(n + 1) * 128, 257 + t0:257 + t0 + tw], queue="qsync")
        k.P.tt(s, ps, xbuf[:, 0:tw], ALU.add)
        k.P.dma(xatt_o[n * 128:(n + 1) * 128, t0:t0 + tw], s, queue="qscalar")
    linear(k, afm, slab, DM, DM, T, evac_out, tag="womm", slabN=1024)
    return k


def build_l2f_f1():
    k = K("l2ff1")
    T = TCORE
    DI = 2730
    x_h = k.inp("x", (DM, T + 1))
    g_h = k.inp("g", (DM,))
    b_h = k.inp("b", (DM,))
    w1aT_h = k.inp("w1aT", (DM, DI))
    w1gT_h = k.inp("w1gT", (DM, DI))
    b1a_h = k.inp("b1a", (DI,))
    b1g_h = k.inp("b1g", (DI,))
    t_o = k.outp("t", (DI, T), BF16)
    k.mkpsums(8)
    gc = load_bias_cols(k, g_h, DM, "g")
    bc = load_bias_cols(k, b_h, DM, "b")
    xn_bf = k.fm("xn_bf", 8, T, BF16)
    xt = k.sb("ln_x", 128, 8 * 512, F32)
    xb = k.sb("ln_xb", 128, 512, BF16)
    xsq = k.sb("ln_xsq", 128, 512, BF16)
    rows = k.sb("ln_rows", 1, 6 * 512, F32)
    xnf = k.sb("ln_xnf", 128, 512, F32)
    Tt = 512
    for t0 in range(0, T, Tt):
        tw = min(Tt, T - t0)
        for c in range(8):
            off = 1 if c < 4 else 0
            k.P.dma(xt[:, c * Tt:c * Tt + tw], x_h[c * 128:(c + 1) * 128, t0 + off:t0 + off + tw])
        ps_s = k.ps()
        ps_q = k.ps()
        for c in range(8):
            k.P.copy(xb[:, 0:tw], xt[:, c * Tt:c * Tt + tw], engine="gpsimd")
            k.P.mm(ps_s[0:1, 0:tw], k.ones_col_bf(), xb[:, 0:tw], start=(c == 0), stop=(c == 7), tag="lnsum")
            k.P.act(xsq[:, 0:tw], xt[:, c * Tt:c * Tt + tw], AF.Square, tag="lnsq")
            k.P.mm(ps_q[0:1, 0:tw], k.ones_col_bf(), xsq[:, 0:tw], start=(c == 0), stop=(c == 7), tag="lnsum2")
        mean = rows[:, 0:tw]
        rstd = rows[:, Tt:Tt + tw]
        mr = rows[:, 2 * Tt:2 * Tt + tw]
        var = rows[:, 3 * Tt:3 * Tt + tw]
        k.P.ts(mean, ps_s[0:1, 0:tw], 1.0 / DM, None, ALU.mult)
        k.P.ts(var, ps_q[0:1, 0:tw], 1.0 / DM, None, ALU.mult)
        k.P.tt(rows[:, 4 * Tt:4 * Tt + tw], mean, mean, ALU.mult)
        k.P.tt(var, var, rows[:, 4 * Tt:4 * Tt + tw], ALU.subtract)
        k.P.ts(var, var, 1e-5, None, ALU.add)
        k.P.act(rows[:, 5 * Tt:5 * Tt + tw], var, AF.Sqrt)
        k.P.reciprocal(rstd, rows[:, 5 * Tt:5 * Tt + tw])
        k.P.tt(mr, mean, rstd, ALU.mult)
        bps = bcast_row_psum(k, rstd, tw)
        bpm = bcast_row_psum(k, mr, tw)
        for c in range(8):
            k.P.tt(xnf[:, 0:tw], xt[:, c * Tt:c * Tt + tw], bps[:, 0:tw], ALU.mult)
            k.P.tt(xnf[:, 0:tw], xnf[:, 0:tw], bpm[:, 0:tw], ALU.subtract)
            k.P.ts(xn_bf.c(c, t0, t0 + tw), xnf[:, 0:tw], gc[:, c:c + 1], bc[:, c:c + 1], ALU.mult, ALU.add, tag="lngb")
    slab_a = load_wslabT(k, w1aT_h, DM, DI, "w1a")
    slab_g = load_wslabT(k, w1gT_h, DM, DI, "w1g")
    ba = load_bias_cols(k, b1a_h, DI, "b1a")
    bg = load_bias_cols(k, b1g_h, DI, "b1g")
    av = k.sb("av", 128, 512, F32)
    gv = k.sb("gv", 128, 512, F32)
    tstage = k.sb("tstage", 128, 2 * 512, BF16)
    NC = _cdiv(DI, 128)
    for n in range(NC):
        npp = min(128, DI - n * 128)
        for t0 in range(0, T, 512):
            psa = k.ps()
            psg2 = k.ps()
            for kc in range(8):
                k.P.mm(psa[0:npp, 0:512], slab_a[:, kc * DI + n * 128:kc * DI + n * 128 + npp], xn_bf.c(kc, t0, t0 + 512), start=(kc == 0), stop=(kc == 7), tag="amm")
            for kc in range(8):
                k.P.mm(psg2[0:npp, 0:512], slab_g[:, kc * DI + n * 128:kc * DI + n * 128 + npp], xn_bf.c(kc, t0, t0 + 512), start=(kc == 0), stop=(kc == 7), tag="gmm")
            k.P.ts(av[0:npp, 0:512], psa[0:npp, 0:512], ba[0:npp, n:n + 1], None, ALU.add, tag="abias")
            k.P.act(gv[:, 0:512], psg2[:, 0:512], AF.Gelu, bias=bg[:, n:n + 1], tag="ggelu")
            s = tstage[:, (t0 // 512 % 2) * 512:(t0 // 512 % 2) * 512 + 512]
            k.P.tt(s[0:npp, :], av[0:npp, 0:512], gv[0:npp, 0:512], ALU.mult, tag="tmul")
            k.P.dma(t_o[n * 128:n * 128 + npp, t0:t0 + 512], s[0:npp, :], queue="qscalar")
    return k


def build_l2f_f2():
    k = K("l2ff2")
    T = TCORE
    DI = 2730
    t_h = k.inp("t", (DI, T), BF16)
    w2T_h = k.inp("w2T", (DI, DM))
    b2_h = k.inp("b2", (DM,))
    xatt_h = k.inp("xatt", (DM, T))
    xo_o = k.outp("xo", (DM, T))
    k.mkpsums(8)
    KC = _cdiv(DI, 128)
    slab = load_wslabT(k, w2T_h, DI, DM, "w2")
    b2c = load_bias_cols(k, b2_h, DM, "b2")
    tq = k.sb("tq", 128, KC * 512, BF16)
    xbuf = k.sb("xb", 128, 512, F32)
    stage = k.sb("stage", 128, 2 * 512, F32)
    for t0 in range(0, T, 512):
        for kc in range(KC):
            p = min(128, DI - kc * 128)
            k.P.dma(tq[0:p, kc * 512:(kc + 1) * 512], t_h[kc * 128:kc * 128 + p, t0:t0 + 512], queue="qsync")
        for n in range(8):
            ps = k.ps()
            for kc in range(KC):
                p = min(128, DI - kc * 128)
                k.P.mm(ps[:, 0:512], slab[0:p, kc * DM + n * 128:kc * DM + (n + 1) * 128], tq[0:p, kc * 512:(kc + 1) * 512], start=(kc == 0), stop=(kc == KC - 1), tag="w2mm")
            s = stage[:, (n % 2) * 512:(n % 2) * 512 + 512]
            k.P.dma(xbuf[:], xatt_h[n * 128:(n + 1) * 128, t0:t0 + 512], queue="qgpsimd")
            k.P.ts(s, ps[:, 0:512], b2c[:, n:n + 1], None, ALU.add)
            k.P.tt(s, s, xbuf[:], ALU.add)
            k.P.dma(xo_o[n * 128:(n + 1) * 128, t0:t0 + 512], s, queue="qscalar")
    return k


def build_lnx():
    k = K("lnx")
    T = TCORE
    x_h = k.inp("x", (DM, T))
    xn_o = k.outp("xn", (DM, T))
    k.mkpsums(4)
    xn_bf = k.fm("lnx_bf", 8, T, BF16)
    layernorm_stream(k, x_h, DM, T, xn_bf, xn_f32_h=xn_o)
    return k


def build_add3():
    k = K("add3")
    T = TCORE
    a_h = k.inp("a", (DM, T))
    b_h = k.inp("b", (DM, T))
    c_h = k.inp("c", (DM, T))
    o_h = k.outp("o", (DM, T))
    ta = k.sb("ta", 128, T, F32)
    tb = k.sb("tb", 128, T, F32)
    tc = k.sb("tc", 128, T, F32)
    for c in range(8):
        k.P.dma(ta[:], a_h[c * 128:(c + 1) * 128, :], queue="qsync")
        k.P.dma(tb[:], b_h[c * 128:(c + 1) * 128, :], queue="qgpsimd")
        k.P.dma(tc[:], c_h[c * 128:(c + 1) * 128, :], queue="qscalar")
        k.P.tt(ta[:], ta[:], tb[:], ALU.add)
        k.P.tt(ta[:], ta[:], tc[:], ALU.add)
        k.P.dma(o_h[c * 128:(c + 1) * 128, :], ta[:], queue="qsync")
    return k


def _rotary_tables():
    n = SEQ
    DH = 64
    W = 256
    half = np.arange(0, DH, 2, dtype=np.float64)
    inv_freq = 1.0 / 10000.0 ** (half / DH)
    t = np.arange(n, dtype=np.float64)
    freqs = t[:, None] * inv_freq[None, :]
    freqs = np.concatenate([freqs, freqs], -1)
    base_scale = (half + 0.4 * DH) / (1.4 * DH)
    power = (t - n // 2) / (W // 2)
    scale = base_scale[None, :] ** power[:, None]
    scale = np.concatenate([scale, scale], -1)
    cos = np.cos(freqs)
    sin = np.sin(freqs)
    sgn = np.where(np.arange(DH) < 32, -1.0, 1.0)[None, :]
    ctq = (cos * scale).T
    stq = (sin * scale * sgn).T
    inv = 1.0 / scale
    ctk = (cos * inv).T
    stk = (sin * inv * sgn).T
    out = []
    for a in (ctq, stq, ctk, stk):
        out.append(np.ascontiguousarray(np.tile(a, (2, 1)), dtype=np.float32))
    return out


def _l2f_masks():
    m = []
    for s in range(2):
        i = np.arange(128)[:, None]
        j = np.arange(512)[None, :]
        ok = (j >= s * 128 + i) & (j <= s * 128 + i + 256)
        m.append(np.where(ok, 0.0, -1e9).astype(np.float32))
    mf = [np.where((np.arange(512)[None, :] >= 256), m[s], -1e9).astype(np.float32) for s in range(2)]
    return m[0], m[1], mf[0], mf[1]


def run_l2f(inputs, x_dm):
    ka = build_l2f_attn()
    kf1 = build_l2f_f1()
    kf2 = build_l2f_f2()
    ctq, stq, ctk, stk = _rotary_tables()
    m0, m1, mf0, mf1 = _l2f_masks()
    def tabslice(tab, t0):
        lo = t0 - 256
        if lo < 0:
            pad = np.repeat(tab[:, 0:1], -lo, axis=1)
            return _f32(np.concatenate([pad, tab[:, 0:t0 + TCORE]], 1))
        return _f32(tab[:, lo:t0 + TCORE])
    for li in range(6):
        wqkv = _f32(inputs["L2_ffn_Wqkv"][li])
        wo = _f32(inputs["L2_ffn_Wo"][li])
        qsc = _f32(np.tile(np.asarray(inputs["L2_ffn_q_scale"][li]), 2))
        ksc = _f32(np.tile(np.asarray(inputs["L2_ffn_k_scale"][li]), 2))
        maps = []
        for b in range(BATCH):
            for h in range(2):
                t0, t1 = h * TCORE, (h + 1) * TCORE
                maps.append({"x": _halo_slice(x_dm[b], t0, t1, 257, 0),
                             "wqkv": wqkv, "wo": wo, "qsc": qsc, "ksc": ksc,
                             "ctq": tabslice(ctq, t0), "stq": tabslice(stq, t0),
                             "ctk": tabslice(ctk, t0), "stk": tabslice(stk, t0),
                             "m0": m0, "m1": m1,
                             "mf0": mf0 if h == 0 else m0, "mf1": mf1 if h == 0 else m1})
        r = ka.run(maps)
        xatt_dm = [np.concatenate([r.results[2 * b]["xatt"], r.results[2 * b + 1]["xatt"]], 1) for b in range(BATCH)]
        g = _f32(inputs["L2_ffn_ln_g"][li])
        bb = _f32(inputs["L2_ffn_ln_b"][li])
        W1 = np.asarray(inputs["L2_ffn_W1"][li])
        w1aT = _f32(W1[:, :2730])
        w1gT = _f32(W1[:, 2730:])
        b1 = np.asarray(inputs["L2_ffn_b1"][li])
        b1a, b1g = _f32(b1[:2730]), _f32(b1[2730:])
        maps = []
        for b in range(BATCH):
            for h in range(2):
                t0, t1 = h * TCORE, (h + 1) * TCORE
                maps.append({"x": _halo_slice(xatt_dm[b], t0, t1, 1, 0),
                             "g": g, "b": bb, "w1aT": w1aT, "w1gT": w1gT,
                             "b1a": b1a, "b1g": b1g})
        r = kf1.run(maps)
        t_cores = [r.results[c]["t"] for c in range(NCORES)]
        w2T = _f32(inputs["L2_ffn_W2"][li])
        b2 = _f32(inputs["L2_ffn_b2"][li])
        maps = []
        for b in range(BATCH):
            for h in range(2):
                c = 2 * b + h
                t0, t1 = h * TCORE, (h + 1) * TCORE
                maps.append({"t": t_cores[c], "w2T": w2T, "b2": b2,
                             "xatt": xatt_dm[b][:, t0:t1]})
        r = kf2.run(maps)
        x_dm = [np.concatenate([r.results[2 * b]["xo"], r.results[2 * b + 1]["xo"]], 1) for b in range(BATCH)]
    return x_dm


def run_l2(inputs, x_dm, ks_scan):
    kl = build_lnx()
    maps = []
    for b in range(BATCH):
        for h in range(2):
            maps.append({"x": np.ascontiguousarray(x_dm[b][:, h * TCORE:(h + 1) * TCORE])})
    r = kl.run(maps)
    xn_dm = [np.concatenate([r.results[2 * b]["xn"], r.results[2 * b + 1]["xn"]], 1) for b in range(BATCH)]
    attn_dm = run_l2a(inputs, xn_dm, ks_scan)
    ffn_dm = run_l2f(inputs, xn_dm)
    kadd = build_add3()
    maps = []
    for b in range(BATCH):
        for h in range(2):
            sl = slice(h * TCORE, (h + 1) * TCORE)
            maps.append({"a": np.ascontiguousarray(x_dm[b][:, sl]),
                         "b": np.ascontiguousarray(attn_dm[b][:, sl]),
                         "c": np.ascontiguousarray(ffn_dm[b][:, sl])})
    r = kadd.run(maps)
    x3_dm = [np.concatenate([r.results[2 * b]["o"], r.results[2 * b + 1]["o"]], 1) for b in range(BATCH)]
    return x3_dm



def conv4_causal(k, src_fm, CC, Te, kw, bcv, out_cb, tag="cv"):
    if not hasattr(k, "_conv_scr"):
        k._conv_scr = (k.sb("cacc_sh", 128, 512, F32), k.sb("cst_sh", 128, 512, F32))
    cacc, cst = k._conv_scr
    for c in range(CC):
        for t0 in range(0, TCORE, 512):
            tw = min(512, TCORE - t0)
            base = c * Te + t0
            k.P.ts(cacc[:, 0:tw], src_fm.t[:, base:base + tw], kw[:, c * 4:c * 4 + 1], None, ALU.mult, tag=tag)
            for i in (1, 2, 3):
                k.P.stt(cacc[:, 0:tw], src_fm.t[:, base + i:base + i + tw], kw[:, c * 4 + i:c * 4 + i + 1], cacc[:, 0:tw], ALU.mult, ALU.add, tag=tag)
            k.P.act(cst[:, 0:tw], cacc[:, 0:tw], AF.Silu, tag=tag + "si")
            out_cb(c, t0, tw, cst[:, 0:tw])


def build_l3a():
    k = K("l3a")
    Te = TCORE + 4
    KD, VD = 768, 1536
    x_h = k.inp("x", (DM, Te))
    n1_h = k.inp("n1", (DM,))
    wq_h = k.inp("wq", (DM, KD))
    wk_h = k.inp("wk", (DM, KD))
    wv_h = k.inp("wv", (DM, VD))
    wg_h = k.inp("wg", (DM, VD))
    wb_h = k.inp("wb", (DM, 8))
    wa_h = k.inp("wa", (DM, 8))
    cq_h = k.inp("cq", (KD, 4))
    ck_h = k.inp("ck", (KD, 4))
    cv_h = k.inp("cv", (VD, 4))
    alog_h = k.inp("alog", (8,))
    dtb_h = k.inp("dtb", (8,))
    mqk_h = k.inp("mqk", (KD, KD))
    xn_o = k.outp("xn", (DM, Te))
    q_o = k.outp("q", (KD, TCORE))
    k_o = k.outp("kk", (KD, TCORE))
    v_o = k.outp("v", (VD, TCORE))
    go_o = k.outp("go", (VD, TCORE))
    beta_o = k.outp("beta", (8, TCORE))
    g_o = k.outp("g", (8, TCORE))
    k.mkpsums(8)
    xn_bf = k.fm("xn_bf", 8, Te, BF16)
    layernorm_stream(k, x_h, DM, Te, xn_bf, xn_f32_h=xn_o)
    h_bf = k.fm("h_bf", 8, Te, BF16)
    n1c = load_bias_cols(k, n1_h, DM, "n1")
    sqb = k.sb("sqb", 128, 512, BF16)
    rows = k.sb("rrows", 1, 2 * 512, F32)
    for t0 in range(0, Te, 512):
        tw = min(512, Te - t0)
        psq = k.ps()
        for c in range(8):
            k.P.act(sqb[:, 0:tw], xn_bf.c(c, t0, t0 + tw), AF.Square, tag="rsq")
            k.P.mm(psq[0:1, 0:tw], k.ones_col_bf(), sqb[:, 0:tw], start=(c == 0), stop=(c == 7), tag="rsum")
        r0 = rows[:, 0:tw]
        k.P.ts(r0, psq[0:1, 0:tw], 1.0 / DM, 1e-6, ALU.mult, ALU.add)
        k.P.act(rows[:, 512:512 + tw], r0, AF.Sqrt)
        k.P.reciprocal(r0, rows[:, 512:512 + tw])
        bps = bcast_row_psum(k, r0, tw)
        for c in range(8):
            k.P.stt(h_bf.c(c, t0, t0 + tw), xn_bf.c(c, t0, t0 + tw), n1c[:, c:c + 1], bps[:, 0:tw], ALU.mult, ALU.mult, tag="rnorm")
    slab8 = k.sb("slab8", 128, 8 * 8, BF16)
    arow = k.sb("arow", 8, 1, F32)
    dtbc = k.sb("dtbc", 8, 1, F32)
    k.P.dma(arow[:], alog_h[:, None], queue="qsync")
    k.P.act(arow[:], arow[:], AF.Exp)
    k.P.ts(arow[:], arow[:], -1.0, None, ALU.mult)
    k.P.dma(dtbc[:], dtb_h[:, None], queue="qsync")
    bstage = k.sb("bstage", 8, 512, F32)
    bsc1 = k.sb("bsc1", 8, 512, F32)
    bsc2 = k.sb("bsc2", 8, 512, F32)
    for (w_h, out_h, isbeta) in ((wb_h, beta_o, True), (wa_h, g_o, False)):
        load_wslabT(k, w_h, DM, 8, "wba", slab=slab8)
        for (t0, tw) in [(3, 512), (515, 512), (1027, 512), (1539, 512)]:
            ps = k.ps()
            for kc in range(8):
                k.P.mm(ps[0:8, 0:tw], slab8[:, kc * 8:kc * 8 + 8], h_bf.c(kc, t0, t0 + tw), start=(kc == 0), stop=(kc == 7), tag="ba")
            if isbeta:
                k.P.act(bstage[:, 0:tw], ps[0:8, 0:tw], AF.Sigmoid, tag="bsig")
            else:
                k.P.ts(bstage[:, 0:tw], ps[0:8, 0:tw], dtbc[:, 0:1], None, ALU.add, tag="gadd")
                softplus_tiles(k, bstage[:, 0:tw], bstage[:, 0:tw], bsc1[:, 0:tw], bsc2[:, 0:tw])
                k.P.ts(bstage[:, 0:tw], bstage[:, 0:tw], arow[:, 0:1], None, ALU.mult, tag="gmul")
            k.P.dma(out_h[:, t0 - 3:t0 - 3 + tw], bstage[0:8, 0:tw], queue="qscalar")
    slabv = k.sb("slabv", 128, 8 * VD, BF16)
    raw = k.fm("raw", 12, Te, BF16)
    mslab = k.sb("mslab", 128, 6 * KD, BF16)
    rawv = Fm(raw.t, 12, Te)

    class _CsilView:
        def c(self, c, t0=0, t1=None):
            if t1 is None:
                t1 = TCORE
            return rawv.c(6 + c, t0, t1)
    csil = _CsilView()
    kwq = k.sb("kwq", 128, 12 * 4, F32)
    nsq = k.sb("nsq", 128, 512, BF16)
    nrm = k.sb("nrm", 128, 512, F32)
    qstage = k.sb("qstage", 128, 512, F32)
    for (w_h, c_h, out_h, scl) in ((wq_h, cq_h, q_o, 96.0 ** -0.5), (wk_h, ck_h, k_o, 1.0)):
        load_wslabT(k, w_h, DM, KD, "wqk", slab=slabv, slabN=VD)
        def evac_raw(n, t0, tw, ps):
            k.P.copy(raw.c(n, t0, t0 + tw), ps, engine="vector")
        linear(k, h_bf, slabv, DM, KD, Te, evac_raw, tag="qkp", slabN=VD)
        for c in range(6):
            k.P.dma(kwq[:, c * 4:(c + 1) * 4], c_h[c * 128:(c + 1) * 128, :], queue="qsync")

        def conv_cb(c, t0, tw, sil):
            k.P.copy(csil.c(c, t0, t0 + tw), sil, engine="gpsimd")
        conv4_causal(k, raw, 6, Te, kwq, None, conv_cb, tag="cq")
        load_wslabT(k, mqk_h, KD, KD, "mqk", slab=mslab)
        for t0 in range(0, TCORE, 512):
            for co in range(6):
                cis = [ci for ci in range(6) if _mqk_overlap(ci, co)]
                psn = k.ps()
                for (ii, ci) in enumerate(cis):
                    k.P.act(nsq[:, 0:512], csil.c(ci, t0, t0 + 512), AF.Square, tag="l3sq")
                    k.P.mm(psn[:, 0:512], mslab[:, ci * KD + co * 128: ci * KD + co * 128 + 128], nsq[:, 0:512], start=(ii == 0), stop=(ii == len(cis) - 1), tag="gsum")
                k.P.ts(nrm[:, 0:512], psn[:, 0:512], 1e-6, None, ALU.add, tag="l3eps")
                k.P.act(nrm[:, 0:512], nrm[:, 0:512], AF.Sqrt, tag="l3rt")
                k.P.reciprocal(nrm[:, 0:512], nrm[:, 0:512], tag="l3rc")
                k.P.stt(qstage[:, 0:512], csil.c(co, t0, t0 + 512), scl, nrm[:, 0:512], ALU.mult, ALU.mult, tag="l3ap")
                k.P.dma(out_h[co * 128:(co + 1) * 128, t0:t0 + 512], qstage[:, 0:512], queue="qscalar")
    kwv = k.sb("kwv", 128, 12 * 4, F32)
    vstage = k.sb("vstage", 128, 2 * 512, F32)
    load_wslabT(k, wv_h, DM, VD, "wv", slab=slabv)

    def evac_raw2(n, t0, tw, ps):
        k.P.copy(raw.c(n, t0, t0 + tw), ps, engine="vector")
    linear(k, h_bf, slabv, DM, VD, Te, evac_raw2, tag="vp")
    for c in range(12):
        k.P.dma(kwv[:, c * 4:(c + 1) * 4], cv_h[c * 128:(c + 1) * 128, :], queue="qsync")

    def conv_cb_v(c, t0, tw, sil):
        s = vstage[:, (t0 // 512 % 2) * 512:(t0 // 512 % 2) * 512 + tw]
        k.P.copy(s, sil, engine="gpsimd")
        k.P.dma(v_o[c * 128:(c + 1) * 128, t0:t0 + tw], s, queue="qscalar")
    conv4_causal(k, raw, 12, Te, kwv, None, conv_cb_v, tag="cvv")
    load_wslabT(k, wg_h, DM, VD, "wgo", slab=slabv)
    gstage = k.sb("gstage", 128, 2 * 512, F32)

    def evac_go(n, t0, tw, ps):
        s = gstage[:, (t0 // 512 % 2) * 512:(t0 // 512 % 2) * 512 + tw]
        k.P.copy(s, ps, engine="vector")
        lo = max(t0, 3)
        hi = min(t0 + tw, 3 + TCORE)
        if hi > lo:
            k.P.dma(go_o[n * 128:(n + 1) * 128, lo - 3:hi - 3], s[:, lo - t0:hi - t0], queue="qscalar")
    linear(k, h_bf, slabv, DM, VD, Te, evac_go, tag="gop")
    return k


def _mqk_overlap(ci, co):
    for h in range(8):
        lo, hi = 96 * h, 96 * h + 96
        if lo < (ci + 1) * 128 and hi > ci * 128 and lo < (co + 1) * 128 and hi > co * 128:
            return True
    return False


def build_l3s(nch=None, sect=9, nheads=4, adrop=()):
    k = K("l3s")
    T = SEQ
    NH = 4
    DK, DV = 96, 192
    CH = 128
    NCH = (T // CH) if nch is None else nch
    q_h = k.inp("q", (NH * DK, T))
    k_h = k.inp("kk", (NH * DK, T))
    vtm_h = k.inp("vtm", (T, NH * DV))
    beta_h = k.inp("beta", (NH, T))
    g_h = k.inp("g", (NH, T))
    o_o = k.outp("o", (T, NH * DV))
    k.mkpsums(4)
    psg = _psum_named(k, "psg")
    psgN = _psum_named(k, "psgN")
    psb = _psum_named(k, "psb")
    psbt = k.st.enter_context(k.nc.psum_tensor(k.uid("psbt"), [128, 128], BF16))
    idf = k.ident_f()
    idb = k.ident_bf()

    beta_t = k.sb("betat", NH, T, F32)
    g_t = k.sb("gt", NH, T, F32)
    k.P.dma(beta_t[:], beta_h, queue="qsync")
    k.P.dma(g_t[:], g_h, queue="qsync")
    zrow = k.sb("zrow", NH, CH, F32)
    k.P.memset(zrow[:], 0.0)

    Gc = k.sb("Gc", NH, CH, F32)
    nGc = k.sb("nGc", NH, CH, F32)
    G_tm = k.sb("Gtm", 128, NH, F32)
    negG = k.sb("negG", 128, NH, F32)
    beta_tm = k.sb("btm", 128, NH, F32)
    grow = k.sb("grow", 1, NH * CH, F32)
    ngrow = k.sb("ngrow", 1, NH * CH, F32)
    qh = k.sb("qh", DK, CH, BF16)
    kh = k.sb("kh", DK, CH, BF16)
    kh2 = k.sb("kh2", DK, CH, BF16)
    vt = k.sb("vt", 128, DV, F32)
    S = k.sb("S", DK, NH * DV, F32)
    S_bf = k.sb("Sbf", DK, NH * DV, BF16)
    k.P.memset(S[:], 0.0)
    k.P.memset(S_bf[:], 0.0)
    lam = k.sb("lam", 128, 1, F32)
    dec = k.sb("dec", 128, 1, F32)
    gam = k.sb("gam", 128, 1, F32)
    e1 = k.sb("e1", 128, CH, F32)
    e2 = k.sb("e2", 128, CH, F32)
    Af = k.sb("Af", 128, CH, F32)
    P_bf = k.sb("Pbf", 128, CH, BF16)
    PT_bf = k.sb("PTbf", 128, CH, BF16)
    Rt = k.sb("Rt", 128, DV, F32)
    X_f = k.sb("Xf", 128, DV, F32)
    X_bf = k.sb("Xbf", 128, DV, BF16)
    WT_bf = k.sb("WTbf", 128, CH, BF16)
    O1 = k.sb("O1", 128, DV, F32)
    ktm_bf = k.sb("ktmbf", 128, DK, BF16)
    ostage = k.sb("ostage", 128, DV, F32)
    maskSL = k.const_sb("c_maskSL", np.where(np.tril(np.ones((128, 128)), -1) > 0, 0.0, -1e9).astype(np.float32))
    maskIU = k.const_sb("c_maskIU", np.where(np.triu(np.ones((128, 128))) > 0, 0.0, -1e9).astype(np.float32))

    for ci in range(NCH):
        c0 = ci * CH
        k.P.scan(Gc[:], g_t[:, c0:c0 + CH], zrow[:], 0.0, op0=ALU.add, op1=ALU.add, tag="gcum")
        pst = k.ps()
        k.P.transpose(pst[0:CH, 0:NH], Gc[:], idf[0:NH, 0:NH], tag="gT")
        k.P.copy(G_tm[:, 0:NH], pst[0:CH, 0:NH], engine="vector")
        k.P.ts(negG[:, 0:NH], G_tm[:, 0:NH], -1.0, None, ALU.mult)
        pst2 = k.ps()
        k.P.transpose(pst2[0:CH, 0:NH], beta_t[:, c0:c0 + CH], idf[0:NH, 0:NH], tag="bT")
        k.P.copy(beta_tm[:, 0:NH], pst2[0:CH, 0:NH], engine="vector")
        k.P.ts(nGc[:], Gc[:], -1.0, None, ALU.mult, tag="ngc")
        k.P.dma(grow[:], Gc[:], queue="qsync")
        k.P.dma(ngrow[:], nGc[:], queue="qsync")
        k.P.mm(psg[:, 0:512], k.ones_row_f(), grow[:], tag="gbc")
        k.P.mm(psgN[:, 0:512], k.ones_row_f(), ngrow[:], tag="gnbc")
        for h in range(nheads):
            hs = h * 128
            k.P.dma(qh[:], q_h[h * DK:(h + 1) * DK, c0:c0 + CH], queue="qgpsimd")
            k.P.dma(kh[:], k_h[h * DK:(h + 1) * DK, c0:c0 + CH], queue="qgpsimd")
            k.P.dma(vt[:], vtm_h[c0:c0 + CH, h * DV:(h + 1) * DV], queue="qsync")
            k.P.act(lam[:, 0:1], G_tm[:, h:h + 1], AF.Exp, tag="lam")
            k.P.act(dec[:, 0:1], psg[:, hs + CH - 1:hs + CH], AF.Exp, bias=negG[:, h:h + 1], tag="dec")
            k.P.act(gam[:, 0:1], psg[:, hs + CH - 1:hs + CH], AF.Exp, tag="gam")
            if sect < 2:
                continue
            psk = k.ps()
            if "kk" not in adrop:
                k.P.copy(kh2[:], kh[:], engine="vector")
                k.P.mm(psk[:, 0:CH], kh[:], kh2[:], tag="kk")
            if "amask" not in adrop:
                k.P.tt(e1[:], psgN[:, hs:hs + CH], maskSL[:], ALU.add, tag="amask")
            if "aexp" not in adrop:
                k.P.act(e2[:], e1[:], AF.Exp, bias=G_tm[:, h:h + 1], tag="aexp")
            if "akke" not in adrop:
                k.P.tt(Af[:], psk[:, 0:CH], e2[:], ALU.mult, tag="akke")
            if "abeta" not in adrop:
                k.P.ts(P_bf[:], Af[:], beta_tm[:, h:h + 1], None, ALU.mult, tag="abeta")
            if "atr" not in adrop:
                k.P.transpose(psbt[:, 0:CH], P_bf[:], idb, tag="atr")
                k.P.copy(PT_bf[:], psbt[:, 0:CH], engine="vector")
            if sect < 3:
                continue
            psr = k.ps()
            k.P.mm(psr[0:CH, 0:DV], kh[:], S_bf[:, h * DV:(h + 1) * DV], tag="ks0")
            k.P.ts(Rt[:, 0:DV], psr[0:CH, 0:DV], lam[:, 0:1], None, ALU.mult, tag="rlam")
            k.P.stt(Rt[:, 0:DV], Rt[:, 0:DV], -1.0, vt[:, 0:DV], ALU.mult, ALU.add, tag="rsub")
            k.P.ts(X_f[:, 0:DV], Rt[:, 0:DV], beta_tm[:, h:h + 1], None, ALU.mult, tag="rbeta")
            k.P.copy(X_bf[:, 0:DV], X_f[:, 0:DV], engine="gpsimd")
            if sect < 4:
                continue
            psx = k.ps()
            k.P.mm(psx[0:CH, 0:DV], PT_bf[:], X_bf[:, 0:DV], tag="ax")
            k.P.tt(X_f[:, 0:DV], X_f[:, 0:DV], psx[0:CH, 0:DV], ALU.subtract, tag="xsub")
            k.P.copy(X_bf[:, 0:DV], X_f[:, 0:DV], engine="gpsimd")
            for j in range(6):
                psp = k.ps()
                k.P.mm(psp[:, 0:CH], PT_bf[:], P_bf[:], tag="psq")
                k.P.copy(P_bf[:], psp[:, 0:CH], engine="vector")
                k.P.transpose(psbt[:, 0:CH], P_bf[:], idb, tag="ptr")
                k.P.copy(PT_bf[:], psbt[:, 0:CH], engine="vector")
                psx = k.ps()
                k.P.mm(psx[0:CH, 0:DV], PT_bf[:], X_bf[:, 0:DV], tag="px")
                k.P.tt(X_f[:, 0:DV], X_f[:, 0:DV], psx[0:CH, 0:DV], ALU.add, tag="xadd")
                k.P.copy(X_bf[:, 0:DV], X_f[:, 0:DV], engine="gpsimd")
            if sect < 5:
                continue
            pso1 = k.ps()
            k.P.mm(pso1[0:CH, 0:DV], qh[:], S_bf[:, h * DV:(h + 1) * DV], tag="qs0")
            k.P.ts(O1[:, 0:DV], pso1[0:CH, 0:DV], lam[:, 0:1], None, ALU.mult, tag="o1lam")
            psq2 = k.ps()
            k.P.mm(psq2[:, 0:CH], kh[:], qh[:], tag="kq")
            k.P.tt(e1[:], psg[:, hs:hs + CH], maskIU[:], ALU.add, tag="wmask")
            k.P.act(e2[:], e1[:], AF.Exp, bias=negG[:, h:h + 1], tag="wexp")
            k.P.tt(WT_bf[:], e2[:], psq2[:, 0:CH], ALU.mult, tag="wmul")
            pso2 = k.ps()
            k.P.mm(pso2[0:CH, 0:DV], WT_bf[:], X_bf[:, 0:DV], tag="o2")
            k.P.tt(ostage[:, 0:DV], O1[:, 0:DV], pso2[0:CH, 0:DV], ALU.add, tag="osum")
            k.P.dma(o_o[c0:c0 + CH, h * DV:(h + 1) * DV], ostage[:, 0:DV], queue="qscalar")
            if sect < 6:
                continue
            k.P.transpose(psbt[:, 0:DK], kh[:], idb[0:DK, 0:DK], tag="ktr")
            k.P.ts(ktm_bf[:, 0:DK], psbt[:, 0:DK], dec[:, 0:1], None, ALU.mult, tag="kdec")
            pss = k.ps()
            k.P.mm(pss[0:DK, 0:DV], ktm_bf[:, 0:DK], X_bf[:, 0:DV], tag="snew")
            k.P.stt(S[:, h * DV:(h + 1) * DV], S[:, h * DV:(h + 1) * DV], gam[0:DK, 0:1], pss[0:DK, 0:DV], ALU.mult, ALU.add, tag="supd")
            k.P.copy(S_bf[:, h * DV:(h + 1) * DV], S[:, h * DV:(h + 1) * DV], engine="gpsimd")
    return k


def build_l3p1():
    k = K("l3p1")
    T = TCORE
    VD = 1536
    o_h = k.inp("o", (VD, T))
    otm_h = k.inp("otm", (T, VD))
    go_h = k.inp("go", (VD, T))
    onw_h = k.inp("onw", (VD,))
    wo_h = k.inp("wo", (VD, DM))
    xn_h = k.inp("xn", (DM, T))
    xmid_o = k.outp("xmid", (DM, T))
    k.mkpsums(8)
    idf = k.ident_f()
    otm = k.sb("otm", 128, VD, F32)
    rtm = k.sb("rtm", 128, 8, F32)
    rfm = k.sb("rfm", 8, T, F32)
    for tt_ in range(T // 128):
        k.P.dma(otm[:], otm_h[tt_ * 128:(tt_ + 1) * 128, :], queue="qsync")
        k.P.act(otm[:], otm[:], AF.Square, tag="osq")
        k.P.red(rtm[:, :, None], otm[:].rearrange("p (g i) -> p g i", i=192), ALU.add, tag="ored")
        k.P.ts(rtm[:], rtm[:], 1.0 / 192.0, 1e-6, ALU.mult, ALU.add, tag="oeps")
        k.P.act(rtm[:], rtm[:], AF.Sqrt, tag="osqrt")
        k.P.reciprocal(rtm[:], rtm[:], tag="orec")
        ps = k.ps()
        k.P.transpose(ps[0:8, 0:128], rtm[:], idf, tag="otr")
        k.P.copy(rfm[:, tt_ * 128:(tt_ + 1) * 128], ps[0:8, 0:128], engine="vector")
    hot = np.zeros((8, 12 * 128), np.float32)
    for c in range(12):
        for p in range(128):
            hot[(128 * c + p) // 192, c * 128 + p] = 1.0
    hotc = k.const_sb("c_hot192", hot)
    onc = load_bias_cols(k, onw_h, VD, "onw")
    t1 = k.fm("t1", 12, T, BF16)
    ob = k.sb("ob", 128, 512, F32)
    gb = k.sb("gb", 128, 512, F32)
    for c in range(12):
        for t0 in range(0, T, 512):
            ps = k.ps()
            k.P.mm(ps[:, 0:512], hotc[:, c * 128:(c + 1) * 128], rfm[:, t0:t0 + 512], tag="obc")
            k.P.dma(ob[:, 0:512], o_h[c * 128:(c + 1) * 128, t0:t0 + 512], queue="qsync")
            k.P.dma(gb[:, 0:512], go_h[c * 128:(c + 1) * 128, t0:t0 + 512], queue="qgpsimd")
            k.P.act(gb[:, 0:512], gb[:, 0:512], AF.Silu, tag="gsil")
            k.P.stt(ob[:, 0:512], ob[:, 0:512], onc[:, c:c + 1], ps[:, 0:512], ALU.mult, ALU.mult, tag="onrm")
            k.P.tt(t1.c(c, t0, t0 + 512), ob[:, 0:512], gb[:, 0:512], ALU.mult, tag="t1m")
    slab = load_wslabT(k, wo_h, VD, DM, "wo")
    xb = k.sb("xb", 128, 512, F32)
    stage = k.sb("stage", 128, 2 * 512, F32)

    def evac(n, t0, tw, ps):
        s = stage[:, (t0 // 512 % 2) * 512:(t0 // 512 % 2) * 512 + tw]
        k.P.dma(xb[:, 0:tw], xn_h[n * 128:(n + 1) * 128, t0:t0 + tw], queue="qsync")
        k.P.tt(s, ps, xb[:, 0:tw], ALU.add)
        k.P.dma(xmid_o[n * 128:(n + 1) * 128, t0:t0 + tw], s, queue="qscalar")
    linear(k, t1, slab, VD, DM, T, evac, tag="womm")
    return k


def build_l3p2():
    k = K("l3p2")
    T = TCORE
    FF = 4096
    x_h = k.inp("x", (DM, T))
    n2_h = k.inp("n2", (DM,))
    wgT_h = k.inp("wgT", (DM, FF))
    wuT_h = k.inp("wuT", (DM, FF))
    t2_o = k.outp("t2", (FF, T), BF16)
    k.mkpsums(8)
    n2c = load_bias_cols(k, n2_h, DM, "n2")
    h2 = k.fm("h2", 8, T, BF16)
    xt = k.sb("rms_x", 128, 8 * 512, F32)
    sqb = k.sb("rms_sq", 128, 512, BF16)
    rows = k.sb("rms_rows", 1, 2 * 512, F32)
    for t0 in range(0, T, 512):
        tw = min(512, T - t0)
        for c in range(8):
            k.P.dma(xt[:, c * 512:c * 512 + tw], x_h[c * 128:(c + 1) * 128, t0:t0 + tw])
        psq = k.ps()
        for c in range(8):
            k.P.act(sqb[:, 0:tw], xt[:, c * 512:c * 512 + tw], AF.Square, tag="rsq")
            k.P.mm(psq[0:1, 0:tw], k.ones_col_bf(), sqb[:, 0:tw], start=(c == 0), stop=(c == 7), tag="rsum")
        r0 = rows[:, 0:tw]
        k.P.ts(r0, psq[0:1, 0:tw], 1.0 / DM, 1e-6, ALU.mult, ALU.add)
        k.P.act(rows[:, 512:512 + tw], r0, AF.Sqrt)
        k.P.reciprocal(r0, rows[:, 512:512 + tw])
        bps = bcast_row_psum(k, r0, tw)
        for c in range(8):
            k.P.stt(h2.c(c, t0, t0 + tw), xt[:, c * 512:c * 512 + tw], n2c[:, c:c + 1], bps[:, 0:tw], ALU.mult, ALU.mult, tag="rnorm")
    slab_g = load_wslabT(k, wgT_h, DM, FF, "wg")
    slab_u = load_wslabT(k, wuT_h, DM, FF, "wu")
    gv = k.sb("gv", 128, 512, F32)
    tstage = k.sb("tstage", 128, 2 * 512, BF16)
    for n in range(32):
        for t0 in range(0, T, 512):
            psgt = k.ps()
            psu = k.ps()
            for kc in range(8):
                k.P.mm(psgt[:, 0:512], slab_g[:, kc * FF + n * 128:kc * FF + (n + 1) * 128], h2.c(kc, t0, t0 + 512), start=(kc == 0), stop=(kc == 7), tag="gmm")
            for kc in range(8):
                k.P.mm(psu[:, 0:512], slab_u[:, kc * FF + n * 128:kc * FF + (n + 1) * 128], h2.c(kc, t0, t0 + 512), start=(kc == 0), stop=(kc == 7), tag="umm")
            k.P.act(gv[:, 0:512], psgt[:, 0:512], AF.Silu, tag="gsil")
            s = tstage[:, (t0 // 512 % 2) * 512:(t0 // 512 % 2) * 512 + 512]
            k.P.tt(s, gv[:, 0:512], psu[:, 0:512], ALU.mult, tag="t2m")
            k.P.dma(t2_o[n * 128:(n + 1) * 128, t0:t0 + 512], s, queue="qscalar")
    return k


def build_l3p3():
    k = K("l3p3")
    T = TCORE
    FF = 4096
    t2_h = k.inp("t2", (FF, T), BF16)
    wdT_h = k.inp("wdT", (FF, DM))
    xmid_h = k.inp("xmid", (DM, T))
    xn_h = k.inp("xn", (DM, T))
    wT_h = k.inp("wT", (DM, DM))
    b_h = k.inp("b", (DM,))
    vr_h = k.inp("vr", (DM, 32))
    cT_h = k.inp("cT", (32, DM))
    x3_h = k.inp("x3", (DM, T))
    x4_o = k.outp("x4", (DM, T))
    k.mkpsums(8)
    xn_bf = k.fm("xn_bf", 8, T, BF16)
    for c in range(8):
        k.P.dma(xn_bf.c(c), xn_h[c * 128:(c + 1) * 128, :], queue="qgpsimd")
    slab_vr = load_wslabT(k, vr_h, DM, 32, "vr")
    rbuf = k.sb("rbuf", 32, T, BF16)

    def evac_r(n, t0, tw, ps):
        k.P.copy(rbuf[:, t0:t0 + tw], ps, engine="vector")
    linear(k, xn_bf, slab_vr, DM, 32, T, evac_r, tag="vr")
    rfm = Fm(rbuf, 1, T)
    slab_w = load_wslabT(k, wT_h, DM, DM, "w")
    slab_c = load_wslabT(k, cT_h, 32, DM, "ct")
    bcols = load_bias_cols(k, b_h, DM, "b")
    slab_d = load_wslabT(k, wdT_h, FF, DM, "wd")
    t2t = k.sb("t2t", 128, 32 * 512, BF16)
    xmb = k.sb("xmb", 128, 512, F32)
    x3b = k.sb("x3b", 128, 512, F32)
    stage = k.sb("stage", 128, 2 * 512, F32)
    for t0 in range(0, T, 512):
        for kc in range(32):
            k.P.dma(t2t[:, kc * 512:(kc + 1) * 512], t2_h[kc * 128:(kc + 1) * 128, t0:t0 + 512], queue="qsync")
        for n in range(8):
            ps = k.ps()
            for kc in range(8):
                k.P.mm(ps[:, 0:512], slab_w[:, kc * DM + n * 128:kc * DM + (n + 1) * 128], xn_bf.c(kc, t0, t0 + 512), start=(kc == 0), stop=False, tag="wmm")
            k.P.mm(ps[:, 0:512], slab_c[0:32, n * 128:(n + 1) * 128], rbuf[:, t0:t0 + 512], start=False, stop=True, tag="cmm")
            ps2 = k.ps()
            for kc in range(32):
                k.P.mm(ps2[:, 0:512], slab_d[:, kc * DM + n * 128:kc * DM + (n + 1) * 128], t2t[:, kc * 512:(kc + 1) * 512], start=(kc == 0), stop=(kc == 31), tag="dmm")
            s = stage[:, (n % 2) * 512:(n % 2) * 512 + 512]
            k.P.dma(xmb[:], xmid_h[n * 128:(n + 1) * 128, t0:t0 + 512], queue="qsync")
            k.P.dma(x3b[:], x3_h[n * 128:(n + 1) * 128, t0:t0 + 512], queue="qgpsimd")
            k.P.ts(s, ps[:, 0:512], bcols[:, n:n + 1], None, ALU.add, tag="fbias")
            k.P.tt(s, s, ps2[:, 0:512], ALU.add, tag="fadd1")
            k.P.tt(s, s, xmb[:], ALU.add, tag="fadd2")
            k.P.tt(s, s, x3b[:], ALU.add, tag="fadd3")
            k.P.dma(x4_o[n * 128:(n + 1) * 128, t0:t0 + 512], s, queue="qscalar")
    return k


def run_l3(inputs, x_dm):
    kl = build_lnx()
    maps = []
    for b in range(BATCH):
        for h in range(2):
            maps.append({"x": np.ascontiguousarray(x_dm[b][:, h * TCORE:(h + 1) * TCORE])})
    r = kl.run(maps)
    xn_dm = [np.concatenate([r.results[2 * b]["xn"], r.results[2 * b + 1]["xn"]], 1) for b in range(BATCH)]

    ka = build_l3a()
    mqk = np.zeros((768, 768), np.float32)
    for h in range(8):
        mqk[96 * h:96 * h + 96, 96 * h:96 * h + 96] = 1.0
    maps = []
    for b in range(BATCH):
        for h in range(2):
            t0, t1 = h * TCORE, (h + 1) * TCORE
            maps.append({"x": _halo_slice(xn_dm[b], t0, t1, 3, 1),
                         "n1": _f32(inputs["L3_attn_norm1_w"]),
                         "wq": _f32(inputs["L3_attn_Wq"]), "wk": _f32(inputs["L3_attn_Wk"]),
                         "wv": _f32(inputs["L3_attn_Wv"]), "wg": _f32(inputs["L3_attn_Wg"]),
                         "wb": _f32(inputs["L3_attn_Wb"]), "wa": _f32(inputs["L3_attn_Wa"]),
                         "cq": _f32(inputs["L3_attn_conv_q_w"]), "ck": _f32(inputs["L3_attn_conv_k_w"]),
                         "cv": _f32(inputs["L3_attn_conv_v_w"]),
                         "alog": _f32(inputs["L3_attn_A_log"]), "dtb": _f32(inputs["L3_attn_dt_bias"]),
                         "mqk": mqk})
    r = ka.run(maps)
    RES = r.results
    xn3_cores = [RES[c]["xn"][:, 3:3 + TCORE] for c in range(NCORES)]
    q_full = [np.concatenate([RES[2 * b]["q"], RES[2 * b + 1]["q"]], 1) for b in range(BATCH)]
    k_full = [np.concatenate([RES[2 * b]["kk"], RES[2 * b + 1]["kk"]], 1) for b in range(BATCH)]
    v_full = [np.concatenate([RES[2 * b]["v"], RES[2 * b + 1]["v"]], 1) for b in range(BATCH)]
    go_cores = [RES[c]["go"] for c in range(NCORES)]
    beta_full = [np.concatenate([RES[2 * b]["beta"], RES[2 * b + 1]["beta"]], 1) for b in range(BATCH)]
    g_full = [np.concatenate([RES[2 * b]["g"], RES[2 * b + 1]["g"]], 1) for b in range(BATCH)]

    ks = build_l3s()
    maps = []
    for b in range(BATCH):
        for hg in range(2):
            rs = slice(hg * 384, (hg + 1) * 384)
            vs = slice(hg * 768, (hg + 1) * 768)
            maps.append({"q": np.ascontiguousarray(q_full[b][rs]),
                         "kk": np.ascontiguousarray(k_full[b][rs]),
                         "vtm": _f32(v_full[b][vs].T),
                         "beta": np.ascontiguousarray(beta_full[b][hg * 4:(hg + 1) * 4]),
                         "g": np.ascontiguousarray(g_full[b][hg * 4:(hg + 1) * 4])})
    r = ks.run(maps)
    o_full = [np.concatenate([_f32(r.results[2 * b]["o"].T), _f32(r.results[2 * b + 1]["o"].T)], 0) for b in range(BATCH)]

    kp1 = build_l3p1()
    onw = _f32(np.tile(np.asarray(inputs["L3_attn_onorm_w"]), 8))
    maps = []
    for b in range(BATCH):
        for h in range(2):
            c = 2 * b + h
            t0, t1 = h * TCORE, (h + 1) * TCORE
            o_sl = np.ascontiguousarray(o_full[b][:, t0:t1])
            maps.append({"o": o_sl, "otm": _f32(o_sl.T), "go": go_cores[c],
                         "onw": onw, "wo": _f32(inputs["L3_attn_Wo"]),
                         "xn": xn3_cores[c]})
    r = kp1.run(maps)
    xmid_cores = [r.results[c]["xmid"] for c in range(NCORES)]

    kp2 = build_l3p2()
    maps = []
    for c in range(NCORES):
        maps.append({"x": xmid_cores[c], "n2": _f32(inputs["L3_attn_norm2_w"]),
                     "wgT": _f32(inputs["L3_attn_Wgate"]), "wuT": _f32(inputs["L3_attn_Wup"])})
    r = kp2.run(maps)
    t2_cores = [r.results[c]["t2"] for c in range(NCORES)]

    kp3 = build_l3p3()
    maps = []
    for b in range(BATCH):
        for h in range(2):
            c = 2 * b + h
            t0, t1 = h * TCORE, (h + 1) * TCORE
            maps.append({"t2": t2_cores[c], "wdT": _f32(inputs["L3_attn_Wdown"]),
                         "xmid": xmid_cores[c], "xn": xn3_cores[c],
                         "wT": _f32(np.asarray(inputs["L3_ffn_W"]).T), "b": _f32(inputs["L3_ffn_b"]),
                         "vr": _f32(inputs["L3_ffn_V_r"]), "cT": _f32(np.asarray(inputs["L3_ffn_C"]).T),
                         "x3": np.ascontiguousarray(x_dm[b][:, t0:t1])})
    r = kp3.run(maps)
    x4_dm = [np.concatenate([r.results[2 * b]["x4"], r.results[2 * b + 1]["x4"]], 1) for b in range(BATCH)]
    return x4_dm


# version 03 (final)
# speedup vs baseline: 1.0978x; 1.0978x over previous
import math
import numpy as np
import ml_dtypes
from contextlib import ExitStack

import concourse.bass as bass
import concourse.mybir as mybir
from concourse.bass_utils import run_bass_kernel_spmd

AF = mybir.ActivationFunctionType
ALU = mybir.AluOpType
F32 = mybir.dt.float32
BF16 = mybir.dt.bfloat16
AXX = mybir.AxisListType.X

NCORES = 8
BATCH = 4
SEQ = 4096
DM = 1024
TCORE = SEQ // 2

SEM_ROTATE_LIMIT = 24000


def _cdiv(a, b):
    return (a + b - 1) // b


def akey(ap):
    return ap.tensor.name


class Op:
    __slots__ = ("queue", "emit", "waits", "inc", "tag", "semkey")

    def __init__(self, queue, emit, waits, inc, tag, semkey):
        self.queue = queue
        self.emit = emit
        self.waits = waits
        self.inc = inc
        self.tag = tag
        self.semkey = semkey


COMPUTE_QUEUES = ("tensor", "vector", "scalar", "gpsimd")
DMA_QUEUES = ("qsync", "qgpsimd", "qscalar", "qvector", "qtensor")
ALL_QUEUES = COMPUTE_QUEUES + DMA_QUEUES
QUEUE_ENGINE = {
    "tensor": "tensor", "vector": "vector", "scalar": "scalar",
    "gpsimd": "gpsimd", "qsync": "sync", "qgpsimd": "gpsimd",
    "qscalar": "scalar", "qvector": "vector", "qtensor": "tensor",
}


class Prog:

    def __init__(self, nc):
        self.nc = nc
        self.ops = []
        self.count = {q: 0 for q in ALL_QUEUES}
        self.gen = {q: 0 for q in ALL_QUEUES}
        self.genbase = {q: 0 for q in ALL_QUEUES}
        self.w = {}
        self.r = {}
        self.waited = {q: {} for q in ALL_QUEUES}
        self.final = {}

    def _need(self, q, deps):
        waits = {}
        for (dq, dgen, dcount) in deps:
            if dq == q and dgen == self.gen[q] and dq == "tensor":
                continue
            key = (dq, dgen)
            have = self.waited[q].get(key, 0)
            if dcount > have:
                waits[key] = max(waits.get(key, 0), dcount)
        out = []
        for key, c in waits.items():
            self.waited[q][key] = c
            out.append((key, c))
        return out

    DMA_SLOTS = 12

    def op(self, queue, emit, reads=(), writes=(), tag="", inc=None):
        assert queue in ALL_QUEUES, queue
        is_dma = queue.startswith("q")
        if inc is None:
            inc = 16 if is_dma else 1
        rkeys = [x if isinstance(x, str) else akey(x) for x in reads]
        wkeys = [x if isinstance(x, str) else akey(x) for x in writes]
        deps = []
        for kk in rkeys:
            lw = self.w.get(kk)
            if lw is not None:
                deps.append(lw)
        for kk in wkeys:
            lw = self.w.get(kk)
            if lw is not None:
                deps.append(lw)
            for (rqg, rcount) in self.r.get(kk, {}).items():
                deps.append((rqg[0], rqg[1], rcount))
        if is_dma:
            j = self.count[queue]
            self.count[queue] += 1
            slot = j % self.DMA_SLOTS
            use = j // self.DMA_SLOTS
            if use > 0:
                deps.append((queue, slot, 16 * use))
            waits = self._need(queue, deps)
            semkey = (queue, slot)
            rel = 16 * (use + 1)
            self.final[semkey] = rel
        else:
            waits = self._need(queue, deps)
            if self.count[queue] - self.genbase[queue] + inc > SEM_ROTATE_LIMIT:
                self.gen[queue] += 1
                self.genbase[queue] = self.count[queue]
            self.count[queue] += inc
            g = self.gen[queue]
            rel = self.count[queue] - self.genbase[queue]
            semkey = (queue, g)
            self.final[semkey] = rel
        o = Op(queue, emit, waits, inc, tag, semkey)
        self.ops.append(o)
        me = (semkey[0], semkey[1], rel)
        for kk in wkeys:
            self.w[kk] = me
            self.r[kk] = {}
        for kk in rkeys:
            self.r.setdefault(kk, {})[semkey] = rel
        return o

    def dma(self, out, in_, queue="qsync", tag="dma", reads=(), writes=(), **kw):
        eng = QUEUE_ENGINE[queue]
        self.op(queue,
                lambda e, out=out, in_=in_: getattr(self.nc, eng).dma_start(out=out, in_=in_, **kw),
                reads=[in_, *reads], writes=[out, *writes], tag=tag)

    def mm(self, out, lhsT, rhs, start=True, stop=True, tag="mm", **kw):
        self.op("tensor",
                lambda e, out=out, l=lhsT, r=rhs: self.nc.tensor.matmul(out, l, r, start=start, stop=stop, **kw),
                reads=[lhsT, rhs], writes=[out], tag=tag)

    def transpose(self, out, in_, ident, tag="tr"):
        self.op("tensor",
                lambda e, out=out, i=in_, d=ident: self.nc.tensor.transpose(out, i, d),
                reads=[in_, ident], writes=[out], tag=tag)

    def act(self, out, in_, func, bias=0.0, scale=1.0, accum_out=None, tag="act"):
        reads = [in_]
        if isinstance(bias, bass.AP):
            reads.append(bias)
        if isinstance(scale, bass.AP):
            reads.append(scale)
        writes = [out]
        if accum_out is not None:
            writes.append(accum_out)
        self.op("scalar",
                lambda e, out=out, i=in_: self.nc.scalar.activation(out, i, func, bias=bias, scale=scale, accum_out=accum_out),
                reads=reads, writes=writes, tag=tag)

    def sign(self, out, in_, tag="sign"):
        self.op("scalar",
                lambda e, out=out, i=in_: self.nc.scalar.sign(out, i),
                reads=[in_], writes=[out], tag=tag)

    def tt(self, out, a, b, alu, engine="vector", tag="tt"):
        self.op(engine,
                lambda e, out=out, a=a, b=b: getattr(self.nc, engine).tensor_tensor(out, a, b, alu),
                reads=[a, b], writes=[out], tag=tag)

    def ts(self, out, in0, s1, s2, op0, op1=ALU.bypass, engine="vector", accum_out=None, tag="ts"):
        reads = [in0]
        if isinstance(s1, bass.AP):
            reads.append(s1)
        if isinstance(s2, bass.AP):
            reads.append(s2)
        writes = [out]
        if accum_out is not None:
            writes.append(accum_out)
        self.op(engine,
                lambda e, out=out, i=in0: getattr(self.nc, engine).tensor_scalar(out, i, s1, s2, op0, op1, accum_out=accum_out),
                reads=reads, writes=writes, tag=tag)

    def stt(self, out, in0, scalar, in1, op0, op1, engine="vector", tag="stt"):
        reads = [in0, in1]
        if isinstance(scalar, bass.AP):
            reads.append(scalar)
        self.op(engine,
                lambda e, out=out, a=in0, b=in1: getattr(self.nc, engine).scalar_tensor_tensor(out, a, scalar, b, op0, op1),
                reads=reads, writes=[out], tag=tag)

    def scan(self, out, data0, data1, initial, op0=ALU.mult, op1=ALU.add, engine="vector", tag="scan"):
        reads = [data0, data1]
        if isinstance(initial, bass.AP):
            reads.append(initial)
        self.op(engine,
                lambda e, out=out, a=data0, b=data1: getattr(self.nc, engine).tensor_tensor_scan(out, a, b, initial, op0, op1),
                reads=reads, writes=[out], tag=tag)

    def red(self, out, in_, alu, axis=AXX, engine="vector", absval=None, tag="red"):
        self.op(engine,
                lambda e, out=out, i=in_: getattr(self.nc, engine).tensor_reduce(out, i, axis, alu, apply_absolute_value=absval),
                reads=[in_], writes=[out], tag=tag)

    def copy(self, out, in_, engine="vector", tag="copy"):
        try:
            if engine == "scalar" and "PSUM" in str(in_.space):
                engine = "vector"
        except Exception:
            pass
        if engine == "scalar":
            self.op(engine, lambda e, out=out, i=in_: self.nc.scalar.copy(out, i),
                    reads=[in_], writes=[out], tag=tag)
        else:
            self.op(engine, lambda e, out=out, i=in_: getattr(self.nc, engine).tensor_copy(out, i),
                    reads=[in_], writes=[out], tag=tag)

    def memset(self, ap, val, engine="vector", tag="memset"):
        self.op(engine, lambda e, ap=ap: getattr(self.nc, engine).memset(ap, val),
                reads=[], writes=[ap], tag=tag)

    def reciprocal(self, out, in_, tag="recip"):
        self.op("vector", lambda e, out=out, i=in_: self.nc.vector.reciprocal(out, i),
                reads=[in_], writes=[out], tag=tag)

    def build(self):
        nc = self.nc
        sems = {}
        ctxs = []
        for semkey in self.final:
            h = nc.semaphore(f"s_{semkey[0]}_{semkey[1]}")
            sems[semkey] = h.__enter__()
            ctxs.append(h)
        per_engine_ops = {}
        for o in self.ops:
            per_engine_ops.setdefault(QUEUE_ENGINE[o.queue], []).append(o)
        dma_finals = [(kk, c) for kk, c in self.final.items() if kk[0].startswith("q")]

        def make_body(ops_list):
            def body(eng):
                for o in ops_list:
                    for (dk, c) in o.waits:
                        eng.wait_ge(sems[dk], c)
                    instr = o.emit(eng)
                    instr.then_inc(sems[o.semkey], o.inc)
                for (kk, fc) in dma_finals:
                    eng.wait_ge(sems[kk], fc)
            return body

        with nc.Block() as block:
            for en in ("sync", "tensor", "vector", "scalar", "gpsimd"):
                ops_list = per_engine_ops.get(en)
                if ops_list:
                    getattr(block, en)(make_body(ops_list))
        for h in reversed(ctxs):
            h.__exit__(None, None, None)


class Fm:

    def __init__(self, t, C, T):
        self.t = t
        self.C = C
        self.T = T

    def c(self, i, t0=0, t1=None):
        if t1 is None:
            t1 = self.T
        return self.t[:, i * self.T + t0: i * self.T + t1]


class K:

    def __init__(self, name):
        self.name = name
        self.nc = bass.Bass("TRN2", target_bir_lowering=False)
        self.P = Prog(self.nc)
        self.st = ExitStack()
        self.consts = {}
        self.outs = []
        self._uid = 0
        self._ps_pool = []
        self._ps_i = 0
        self._built = False

    def uid(self, base):
        self._uid += 1
        return f"{base}_{self._uid}"

    def inp(self, name, shape, dtype=F32):
        return self.nc.dram_tensor(name, list(shape), dtype, kind="ExternalInput").ap()

    def outp(self, name, shape, dtype=F32):
        self.outs.append(name)
        return self.nc.dram_tensor(name, list(shape), dtype, kind="ExternalOutput").ap()

    def sb(self, name, p, f, dtype):
        return self.st.enter_context(self.nc.sbuf_tensor(self.uid(name), [p, f], dtype))

    def fm(self, name, C, T, dtype):
        return Fm(self.sb(name, 128, C * T, dtype), C, T)

    def mkpsums(self, n=8, f=512):
        for i in range(n):
            self._ps_pool.append(self.st.enter_context(self.nc.psum_tensor(self.uid("pp"), [128, f], F32)))

    def ps(self):
        t = self._ps_pool[self._ps_i % len(self._ps_pool)]
        self._ps_i += 1
        return t

    def const(self, name, arr):
        if name not in self.consts:
            self.consts[name] = arr
            dt = {np.dtype(np.float32): F32, np.dtype(ml_dtypes.bfloat16): BF16}[arr.dtype]
            ap = self.inp(name, arr.shape, dt)
            setattr(self, "_c_" + name, ap)
        return getattr(self, "_c_" + name)

    def const_sb(self, name, arr):
        att = "_csb_" + name
        if not hasattr(self, att):
            hap = self.const(name, arr)
            t = self.sb("csb_" + name, arr.shape[0], arr.shape[1], {np.dtype(np.float32): F32, np.dtype(ml_dtypes.bfloat16): BF16}[arr.dtype])
            self.P.dma(t[:], hap, queue="qsync")
            setattr(self, att, t)
        return getattr(self, att)

    def ones_col_bf(self):
        return self.const_sb("c_ones_col", np.ones((128, 1), ml_dtypes.bfloat16))[:]

    def ones_row_f(self):
        return self.const_sb("c_ones_row", np.ones((1, 128), np.float32))[:]

    def ident_f(self):
        return self.const_sb("c_ident_f", np.eye(128, dtype=np.float32))[:]

    def ident_bf(self):
        return self.const_sb("c_ident_bf", np.eye(128, dtype=ml_dtypes.bfloat16))[:]

    def hot16(self):
        h = np.zeros((16, 16 * 128), np.float32)
        for n in range(16):
            h[n, n * 128:(n + 1) * 128] = 1.0
        return self.const("c_hot16", h)

    def hot8x(self):
        h = np.zeros((64, 8 * 128), np.float32)
        for c in range(8):
            for p in range(128):
                h[c * 8 + p // 16, c * 128 + p] = 1.0
        return self.const("c_hot8x", h)

    def build(self):
        if not self._built:
            self.P.build()
            self._built = True
        return self.nc

    def run(self, in_maps, core_ids=None):
        self.build()
        if core_ids is None:
            core_ids = list(range(len(in_maps)))
        for m in in_maps:
            m.update(self.consts)
        return run_bass_kernel_spmd(self.nc, in_maps, core_ids=core_ids)



def load_fm(k, fmobj, hbm, D, T, t0=0, dma_queue="qsync"):
    C = _cdiv(D, 128)
    for c in range(C):
        p = min(128, D - c * 128)
        k.P.dma(fmobj.t[0:p, c * T:c * T + T], hbm[c * 128:c * 128 + p, t0:t0 + T], queue=dma_queue)


def store_fm(k, hbm, fmobj, D, T, dma_queue="qsync"):
    C = _cdiv(D, 128)
    for c in range(C):
        p = min(128, D - c * 128)
        k.P.dma(hbm[c * 128:c * 128 + p, 0:T], fmobj.t[0:p, c * T:c * T + T], queue=dma_queue)


def load_wslabT(k, wT_h, Kdim, N, name="w", queue="qgpsimd", slab=None, col0=0, slabN=None):
    KC = _cdiv(Kdim, 128)
    if slabN is None:
        slabN = N
    if slab is None:
        slab = k.sb(name + "_slab", 128, KC * slabN, BF16)
    for kc in range(KC):
        p = min(128, Kdim - kc * 128)
        k.P.dma(slab[0:p, kc * slabN:kc * slabN + N], wT_h[kc * 128:kc * 128 + p, col0:col0 + N], queue=queue)
    return slab


def load_bias_cols(k, b_h, N, name="b"):
    NC = _cdiv(N, 128)
    t = k.sb(name + "_cols", 128, NC, F32)
    for n in range(NC):
        p = min(128, N - n * 128)
        k.P.dma(t[0:p, n:n + 1], b_h[n * 128:n * 128 + p, None], queue="qsync")
    return t


def linear(k, x, slab, Kdim, N, T, evac, Tt=512, tag="lin", t_tiles=None, slabN=None):
    KC = _cdiv(Kdim, 128)
    NC = _cdiv(N, 128)
    if slabN is None:
        slabN = N
    if t_tiles is None:
        t_tiles = [(t0, min(Tt, T - t0)) for t0 in range(0, T, Tt)]
    for n in range(NC):
        npp = min(128, N - n * 128)
        for (t0, tw) in t_tiles:
            ps = k.ps()
            for kc in range(KC):
                kp = min(128, Kdim - kc * 128)
                k.P.mm(ps[0:npp, 0:tw],
                       slab[0:kp, kc * slabN + n * 128: kc * slabN + n * 128 + npp],
                       x.c(kc, t0, t0 + tw)[0:kp, :],
                       start=(kc == 0), stop=(kc == KC - 1), tag=tag)
            evac(n, t0, tw, ps[0:npp, 0:tw])


def bcast_row_psum(k, row_ap, tw, tag="bcast"):
    ps = k.ps()
    k.P.mm(ps[:, 0:tw], k.ones_row_f(), row_ap, tag=tag)
    return ps


def layernorm_stream(k, x_h, D, Te, xn_bf, xn_f32_h=None, eps=1e-5, Tt=512, also_fp32_fm=None):
    C = _cdiv(D, 128)
    xt = k.sb("ln_x", 128, C * Tt, F32)
    xb = k.sb("ln_xb", 128, Tt, BF16)
    xsq = k.sb("ln_xsq", 128, Tt, BF16)
    rows = k.sb("ln_rows", 1, 6 * Tt, F32)
    xnf = k.sb("ln_xnf", 128, Tt, F32)
    for t0 in range(0, Te, Tt):
        tw = min(Tt, Te - t0)
        for c in range(C):
            k.P.dma(xt[:, c * Tt:c * Tt + tw], x_h[c * 128:(c + 1) * 128, t0:t0 + tw])
        ps_s = k.ps()
        ps_q = k.ps()
        for c in range(C):
            k.P.copy(xb[:, 0:tw], xt[:, c * Tt:c * Tt + tw], engine="gpsimd")
            k.P.mm(ps_s[0:1, 0:tw], k.ones_col_bf(), xb[:, 0:tw], start=(c == 0), stop=(c == C - 1), tag="lnsum")
            k.P.act(xsq[:, 0:tw], xt[:, c * Tt:c * Tt + tw], AF.Square, tag="lnsq")
            k.P.mm(ps_q[0:1, 0:tw], k.ones_col_bf(), xsq[:, 0:tw], start=(c == 0), stop=(c == C - 1), tag="lnsum2")
        mean = rows[:, 0:tw]
        rstd = rows[:, Tt:Tt + tw]
        mr = rows[:, 2 * Tt:2 * Tt + tw]
        var = rows[:, 3 * Tt:3 * Tt + tw]
        k.P.ts(mean, ps_s[0:1, 0:tw], 1.0 / D, None, ALU.mult)
        k.P.ts(var, ps_q[0:1, 0:tw], 1.0 / D, None, ALU.mult)
        k.P.tt(rows[:, 4 * Tt:4 * Tt + tw], mean, mean, ALU.mult)
        k.P.tt(var, var, rows[:, 4 * Tt:4 * Tt + tw], ALU.subtract)
        k.P.ts(var, var, eps, None, ALU.add)
        k.P.act(rows[:, 5 * Tt:5 * Tt + tw], var, AF.Sqrt)
        k.P.reciprocal(rstd, rows[:, 5 * Tt:5 * Tt + tw])
        k.P.tt(mr, mean, rstd, ALU.mult)
        bps = bcast_row_psum(k, rstd, tw)
        bpm = bcast_row_psum(k, mr, tw)
        for c in range(C):
            k.P.tt(xnf[:, 0:tw], xt[:, c * Tt:c * Tt + tw], bps[:, 0:tw], ALU.mult)
            k.P.tt(xnf[:, 0:tw], xnf[:, 0:tw], bpm[:, 0:tw], ALU.subtract)
            k.P.copy(xn_bf.c(c, t0, t0 + tw), xnf[:, 0:tw], engine="scalar")
            if also_fp32_fm is not None:
                k.P.copy(also_fp32_fm.c(c, t0, t0 + tw), xnf[:, 0:tw], engine="gpsimd")
            if xn_f32_h is not None:
                k.P.dma(xn_f32_h[c * 128:(c + 1) * 128, t0:t0 + tw], xnf[:, 0:tw], queue="qscalar")


def softplus_tiles(k, out_ap, in_ap, scr1, scr2):
    k.P.act(scr1, in_ap, AF.Abs)
    k.P.act(scr1, scr1, AF.Exp, scale=-1.0)
    k.P.act(scr1, scr1, AF.Ln, bias=1.0)
    k.P.ts(scr2, in_ap, 0.0, None, ALU.max)
    k.P.tt(out_ap, scr1, scr2, ALU.add)


def quant_fp4_tiles(k, out_ap, x_ap, sc_bc_ap, rsc_bc_ap, scr, tag="q"):
    v, a, kc, m = scr["v"], scr["a"], scr["kc"], scr["m"]
    MID = [0.25, 0.75, 1.25, 1.75, 2.5, 3.5, 5.0]
    k.P.tt(v, x_ap, rsc_bc_ap, ALU.mult, tag=tag + "div")
    k.P.act(a, v, AF.Abs, tag=tag + "abs")
    k.P.ts(kc, a, MID[0], None, ALU.is_gt, tag=tag + "t0")
    for mm_ in MID[1:]:
        k.P.stt(kc, a, mm_, kc, ALU.is_gt, ALU.add, tag=tag + "t")
    k.P.ts(m, kc, -4.0, 0.0, ALU.add, ALU.max, tag=tag + "m1")
    k.P.tt(kc, kc, m, ALU.add, tag=tag + "km1")
    k.P.ts(m, kc, -6.0 - 2.0, 0.0, ALU.add, ALU.max, tag=tag + "m2")
    k.P.ts(m, m, 0.5, None, ALU.mult, tag=tag + "m2b")
    k.P.stt(v, kc, 0.5, m, ALU.mult, ALU.add, tag=tag + "comb")
    k.P.sign(a, x_ap, tag=tag + "sgn")
    k.P.tt(v, v, a, ALU.mult, tag=tag + "sg")
    k.P.tt(out_ap, v, sc_bc_ap, ALU.mult, tag=tag + "rs")



HALO_L0 = 1
TE0 = TCORE + 4


def build_l0a():
    k = K("l0a")
    Te = TE0
    x_h = k.inp("x", (DM, Te))
    winT = k.inp("winT", (DM, 4096))
    wxT = k.inp("wxT", (DM, 96))
    wdtT = k.inp("wdtT", (64, 2048))
    kconv = k.inp("kconv", (2048, 4))
    bconv = k.inp("bconv", (2048,))
    bdt = k.inp("bdt", (2048,))
    xn_o = k.outp("xn", (DM, Te))
    u_o = k.outp("u", (2048, TCORE))
    delta_o = k.outp("delta", (2048, TCORE))
    B_o = k.outp("B", (16, TCORE))
    C_o = k.outp("Cc", (16, TCORE))
    res_o = k.outp("res", (2048, Te))
    k.mkpsums(8)

    xn_bf = k.fm("xn_bf", 8, Te, BF16)
    layernorm_stream(k, x_h, DM, Te, xn_bf, xn_f32_h=xn_o)

    xproj = k.fm("xproj", 16, Te, BF16)
    slab_in = load_wslabT(k, winT, DM, 2048, "win")
    stage = k.sb("stage", 128, 2 * 512, F32)

    def evac_xp(n, t0, tw, ps):
        k.P.copy(xproj.c(n, t0, t0 + tw), ps, engine="scalar")
    linear(k, xn_bf, slab_in, DM, 2048, Te, evac_xp, tag="win")
    load_wslabT(k, winT, DM, 2048, "win2", slab=slab_in, col0=2048)

    def evac_res(n, t0, tw, ps):
        s = stage[:, (t0 // 512 % 2) * 512:(t0 // 512 % 2) * 512 + tw]
        k.P.copy(s, ps, engine="vector")
        k.P.dma(res_o[n * 128:(n + 1) * 128, t0:t0 + tw], s, queue="qscalar")
    linear(k, xn_bf, slab_in, DM, 2048, Te, evac_res, tag="wres")

    slab_x = load_wslabT(k, wxT, DM, 96, "wx")
    xdbl = k.sb("xdbl", 96, Te, F32)
    xdbl_bf = k.sb("xdbl_bf", 64, Te, BF16)

    def evac_xdbl(n, t0, tw, ps):
        k.P.copy(xdbl[:, t0:t0 + tw], ps, engine="vector")
        k.P.copy(xdbl_bf[:, t0:t0 + tw], ps[0:64, :], engine="scalar")
    linear(k, xn_bf, slab_x, DM, 96, Te, evac_xdbl, tag="wx")
    k.P.dma(B_o, xdbl[64:80, 1:1 + TCORE], queue="qsync")
    k.P.dma(C_o, xdbl[80:96, 1:1 + TCORE], queue="qsync")

    slab_dt = load_wslabT(k, wdtT, 64, 2048, "wdt")
    bdt_cols = load_bias_cols(k, bdt, 2048, "bdt")
    xdbl_fm = Fm(xdbl_bf, 1, Te)
    dsc1 = k.sb("dsc1", 128, 512, F32)
    dsc2 = k.sb("dsc2", 128, 512, F32)
    dstage = k.sb("dstage", 128, 2 * 512, F32)

    def evac_delta(n, t0, tw, ps):
        s = dstage[:, (t0 // 512 % 2) * 512:(t0 // 512 % 2) * 512 + tw]
        k.P.ts(s, ps, bdt_cols[:, n:n + 1], None, ALU.add)
        softplus_tiles(k, s, s, dsc1[:, 0:tw], dsc2[:, 0:tw])
        k.P.dma(delta_o[n * 128:(n + 1) * 128, t0 - 1:t0 - 1 + tw], s, queue="qscalar")
    linear(k, xdbl_fm, slab_dt, 64, 2048, Te, evac_delta, tag="wdt",
           t_tiles=[(1, 512), (513, 512), (1025, 512), (1537, 512)])

    kw = k.sb("kw", 128, 16 * 4, F32)
    for c in range(16):
        k.P.dma(kw[:, c * 4:(c + 1) * 4], kconv[c * 128:(c + 1) * 128, :], queue="qsync")
    bcv = load_bias_cols(k, bconv, 2048, "bconv")
    cacc = k.sb("cacc", 128, 512, F32)
    ustage = k.sb("ustage", 128, 2 * 512, F32)
    for c in range(16):
        for t0 in range(0, TCORE, 512):
            tw = min(512, TCORE - t0)
            base = c * Te + t0
            k.P.ts(cacc[:, 0:tw], xproj.t[:, base:base + tw], kw[:, c * 4:c * 4 + 1], bcv[:, c:c + 1], ALU.mult, ALU.add)
            for i in (1, 2, 3):
                k.P.stt(cacc[:, 0:tw], xproj.t[:, base + i:base + i + tw], kw[:, c * 4 + i:c * 4 + i + 1], cacc[:, 0:tw], ALU.mult, ALU.add)
            s = ustage[:, (t0 // 512 % 2) * 512:(t0 // 512 % 2) * 512 + tw]
            k.P.act(s, cacc[:, 0:tw], AF.Silu)
            k.P.dma(u_o[c * 128:(c + 1) * 128, t0:t0 + tw], s, queue="qscalar")
    return k


def build_l0s():
    k = K("l0s")
    DS = 1024
    T = SEQ
    TC = 1024
    delta_h = k.inp("delta", (DS, T))
    u_h = k.inp("u", (DS, T))
    B_h = k.inp("B", (16, T))
    C_h = k.inp("Cc", (16, T))
    alog_h = k.inp("alog", (DS, 16))
    d_h = k.inp("dd", (DS,))
    y_o = k.outp("y", (DS, T))
    k.mkpsums(8)
    hot = k.hot16()
    hot_bf = k.sb("hotbf", 16, 16 * 128, F32)
    k.P.dma(hot_bf[:], hot, queue="qsync")

    At = k.sb("At", 128, 8 * 16, F32)
    for c in range(8):
        k.P.dma(At[:, c * 16:(c + 1) * 16], alog_h[c * 128:(c + 1) * 128, :], queue="qsync")
    k.P.act(At[:], At[:], AF.Exp)
    k.P.ts(At[:], At[:], -1.0, None, ALU.mult)
    Dt = k.sb("Dt", 128, 8, F32)
    for c in range(8):
        k.P.dma(Dt[:, c:c + 1], d_h[c * 128:(c + 1) * 128, None], queue="qsync")

    dbuf = k.fm("dbuf", 8, TC, F32)
    ubuf = k.fm("ubuf", 8, TC, F32)
    wbuf = k.fm("wbuf", 8, TC, BF16)
    ybuf = k.fm("ybuf", 8, TC, F32)
    Brow = k.sb("Brow", 16, TC, F32)
    Crow = k.sb("Crow", 16, TC, F32)
    Bbc = k.sb("Bbc", 128, TC, BF16)
    Cbc = k.sb("Cbc", 128, TC, BF16)
    sa = [k.sb("sa", 128, TC, BF16) for _ in range(2)]
    sbb = [k.sb("sbb", 128, TC, BF16) for _ in range(2)]
    sh = [k.sb("sh", 128, TC, BF16) for _ in range(2)]
    shc = [k.sb("shc", 128, TC, BF16) for _ in range(2)]
    stt_ = k.sb("state", 128, 8 * 16, F32)
    k.P.memset(stt_[:], 0.0)

    for t0 in range(0, T, TC):
        for c in range(8):
            k.P.dma(dbuf.c(c), delta_h[c * 128:(c + 1) * 128, t0:t0 + TC], queue="qsync")
            k.P.dma(ubuf.c(c), u_h[c * 128:(c + 1) * 128, t0:t0 + TC], queue="qgpsimd")
        k.P.dma(Brow[:], B_h[:, t0:t0 + TC], queue="qsync")
        k.P.dma(Crow[:], C_h[:, t0:t0 + TC], queue="qsync")
        for c in range(8):
            k.P.tt(wbuf.c(c), dbuf.c(c), ubuf.c(c), ALU.mult, engine="gpsimd")
            k.P.ts(ybuf.c(c), ubuf.c(c), Dt[:, c:c + 1], None, ALU.mult)
        for n in range(16):
            psB = k.ps()
            psC = k.ps()
            k.P.mm(psB[:, 0:512], hot_bf[:, n * 128:(n + 1) * 128], Brow[:, 0:512], tag="bcB")
            k.P.mm(psB[:, 512:1024] if False else psB[:, 0:512], hot_bf[:, n * 128:(n + 1) * 128], Brow[:, 512:1024], tag="bcB2") if False else None
            psB2 = k.ps()
            k.P.mm(psB2[:, 0:512], hot_bf[:, n * 128:(n + 1) * 128], Brow[:, 512:1024], tag="bcB2")
            k.P.mm(psC[:, 0:512], hot_bf[:, n * 128:(n + 1) * 128], Crow[:, 0:512], tag="bcC")
            psC2 = k.ps()
            k.P.mm(psC2[:, 0:512], hot_bf[:, n * 128:(n + 1) * 128], Crow[:, 512:1024], tag="bcC2")
            k.P.copy(Bbc[:, 0:512], psB[:, 0:512], engine="vector")
            k.P.copy(Bbc[:, 512:1024], psB2[:, 0:512], engine="vector")
            k.P.copy(Cbc[:, 0:512], psC[:, 0:512], engine="vector")
            k.P.copy(Cbc[:, 512:1024], psC2[:, 0:512], engine="vector")
            for c in range(8):
                pp = (n * 8 + c) % 2
                k.P.act(sa[pp][:], dbuf.c(c), AF.Exp, scale=At[:, c * 16 + n:c * 16 + n + 1], tag="deltaA")
                k.P.tt(sbb[pp][:], wbuf.c(c), Bbc[:], ALU.mult, tag="dBu")
                k.P.scan(sh[pp][:], sa[pp][:], sbb[pp][:], stt_[:, c * 16 + n:c * 16 + n + 1], tag="scan")
                k.P.copy(stt_[:, c * 16 + n:c * 16 + n + 1], sh[pp][:, TC - 1:TC], engine="scalar")
                k.P.tt(shc[pp][:], sh[pp][:], Cbc[:], ALU.mult, tag="hC")
                k.P.tt(ybuf.c(c), ybuf.c(c), shc[pp][:], ALU.add, tag="yacc")
        for c in range(8):
            k.P.dma(y_o[c * 128:(c + 1) * 128, t0:t0 + TC], ybuf.c(c), queue="qscalar")
    return k


def build_l0f1():
    k = K("l0f1")
    T = TCORE
    II = 2816
    xn_h = k.inp("xn", (DM, T))
    xntm_h = k.inp("xntm", (T, DM))
    wgq_h = k.inp("wgqT", (DM, II))
    wuq_h = k.inp("wuqT", (DM, II))
    hq_o = k.outp("hq", (II, T), BF16)
    k.mkpsums(8)

    idf = k.ident_f()
    xtm = k.sb("xtm", 128, DM, F32)
    sctm = k.sb("sctm", 128, 64, F32)
    scfm = k.sb("scfm", 64, T, F32)
    rscfm = k.sb("rscfm", 64, T, F32)
    for tt_ in range(T // 128):
        k.P.dma(xtm[:], xntm_h[tt_ * 128:(tt_ + 1) * 128, :], queue="qsync")
        k.P.red(sctm[:, :, None], xtm[:].rearrange("p (g i) -> p g i", i=16), ALU.max, absval=True)
        k.P.ts(sctm[:], sctm[:], 1.0 / 6.0, 1e-12, ALU.mult, ALU.max)
        ps = k.ps()
        k.P.transpose(ps[0:64, 0:128], sctm[:], idf)
        k.P.copy(scfm[:, tt_ * 128:(tt_ + 1) * 128], ps[0:64, 0:128], engine="vector")
        k.P.reciprocal(rscfm[:, tt_ * 128:(tt_ + 1) * 128], ps[0:64, 0:128])

    xq = k.fm("xq", 8, T, BF16)
    hot8 = k.hot8x()
    hot8t = k.sb("hot8t", 64, 8 * 128, F32)
    k.P.dma(hot8t[:], hot8, queue="qsync")
    scbc = k.sb("scbc", 128, 512, F32)
    rscbc = k.sb("rscbc", 128, 512, F32)
    xnc = k.sb("xnc", 128, T, F32)
    qv = k.sb("qv", 128, 512, F32)
    qa = k.sb("qa", 128, 512, F32)
    qk = k.sb("qk", 128, 512, F32)
    qm = k.sb("qm", 128, 512, F32)
    for c in range(8):
        k.P.dma(xnc[:], xn_h[c * 128:(c + 1) * 128, :], queue="qsync")
        for t0 in range(0, T, 512):
            ps = k.ps()
            k.P.mm(ps[:, 0:512], hot8t[:, c * 128:(c + 1) * 128], scfm[:, t0:t0 + 512], tag="scbc")
            k.P.copy(scbc[:, 0:512], ps[:, 0:512], engine="scalar")
            ps = k.ps()
            k.P.mm(ps[:, 0:512], hot8t[:, c * 128:(c + 1) * 128], rscfm[:, t0:t0 + 512], tag="rscbc")
            k.P.copy(rscbc[:, 0:512], ps[:, 0:512], engine="scalar")
            scr2 = {"v": qv[:], "a": qa[:], "kc": qk[:], "m": qm[:]}
            quant_fp4_tiles(k, xq.c(c, t0, t0 + 512), xnc[:, t0:t0 + 512], scbc[:, 0:512], rscbc[:, 0:512], scr2, tag="qx")

    slab_g = load_wslabT(k, wgq_h, DM, II, "wg")
    slab_u = load_wslabT(k, wuq_h, DM, II, "wu")
    hfull = k.sb("hfull", 128, 512, F32)
    htm = k.sb("htm", 128, 512, F32)
    hsc_tm = k.sb("hsctm", 128, 4 * 8, F32)
    hsc_fm = k.sb("hscfm", 8, 512, F32)
    hrsc_fm = k.sb("hrscfm", 8, 512, F32)
    hscbc = k.sb("hscbc", 128, 512, F32)
    hrscbc = k.sb("hrscbc", 128, 512, F32)
    hqt = k.sb("hqt", 128, 512, BF16)
    NC2 = _cdiv(II, 128)
    for n in range(NC2):
        npp = min(128, II - n * 128)
        for t0 in range(0, T, 512):
            psg = k.ps()
            psu = k.ps()
            for kc in range(8):
                k.P.mm(psg[0:npp, 0:512], slab_g[:, kc * II + n * 128: kc * II + n * 128 + npp], xq.c(kc, t0, t0 + 512), start=(kc == 0), stop=(kc == 7), tag="gmm")
            for kc in range(8):
                k.P.mm(psu[0:npp, 0:512], slab_u[:, kc * II + n * 128: kc * II + n * 128 + npp], xq.c(kc, t0, t0 + 512), start=(kc == 0), stop=(kc == 7), tag="umm")
            k.P.act(hfull[0:npp, 0:512], psg[0:npp, 0:512], AF.Silu, tag="hsilu")
            k.P.tt(hfull[0:npp, 0:512], hfull[0:npp, 0:512], psu[0:npp, 0:512], ALU.mult, tag="hmul")
            for s4 in range(4):
                pst = k.ps()
                k.P.transpose(pst[0:128, 0:npp], hfull[0:npp, s4 * 128:(s4 + 1) * 128], idf[0:npp, 0:npp], tag="htr")
                k.P.copy(htm[:, 0:npp], pst[0:128, 0:npp], engine="vector")
                k.P.red(hsc_tm[:, s4 * 8:s4 * 8 + 8, None], htm[:, 0:npp].rearrange("p (g i) -> p g i", i=16), ALU.max, absval=True, tag="hmax")
            k.P.ts(hsc_tm[:], hsc_tm[:], 1.0 / 6.0, 1e-12, ALU.mult, ALU.max, tag="hsc")
            for s4 in range(4):
                pst = k.ps()
                k.P.transpose(pst[0:8, 0:128], hsc_tm[:, s4 * 8:(s4 + 1) * 8], idf, tag="hsctr")
                k.P.copy(hsc_fm[0:8, s4 * 128:(s4 + 1) * 128], pst[0:8, 0:128], engine="vector")
            k.P.reciprocal(hrsc_fm[0:8, 0:512], hsc_fm[0:8, 0:512])
            ps = k.ps()
            k.P.mm(ps[0:npp, 0:512], hot8t[0:8, 0:npp], hsc_fm[0:8, 0:512], tag="hscbc")
            k.P.copy(hscbc[0:npp, 0:512], ps[0:npp, 0:512], engine="scalar")
            ps = k.ps()
            k.P.mm(ps[0:npp, 0:512], hot8t[0:8, 0:npp], hrsc_fm[0:8, 0:512], tag="hrscbc")
            k.P.copy(hrscbc[0:npp, 0:512], ps[0:npp, 0:512], engine="scalar")
            scr2 = {"v": qv[:], "a": qa[:], "kc": qk[:], "m": qm[:]}
            quant_fp4_tiles(k, hqt[0:npp, 0:512], hfull[0:npp, 0:512], hscbc[0:npp, 0:512], hrscbc[0:npp, 0:512], {kk: vv[0:npp, 0:512] for kk, vv in scr2.items()}, tag="qh")
            k.P.dma(hq_o[n * 128:n * 128 + npp, t0:t0 + 512], hqt[0:npp, 0:512], queue="qscalar")
    return k


def build_l0f2():
    k = K("l0f2")
    T = TCORE
    II = 2816
    hq_h = k.inp("hq", (II, T), BF16)
    wdq_h = k.inp("wdqT", (II, DM))
    ffn_o = k.outp("ffn", (DM, T))
    k.mkpsums(8)
    KC = _cdiv(II, 128)
    slab_d = load_wslabT(k, wdq_h, II, DM, "wd")
    hqt = k.sb("hqtile", 128, KC * 512, BF16)
    dstage = k.sb("dnstage", 128, 2 * 512, F32)
    for t0 in range(0, T, 512):
        for kc in range(KC):
            p = min(128, II - kc * 128)
            k.P.dma(hqt[0:p, kc * 512:(kc + 1) * 512], hq_h[kc * 128:kc * 128 + p, t0:t0 + 512], queue="qsync")
        for n in range(8):
            ps = k.ps()
            for kc in range(KC):
                p = min(128, II - kc * 128)
                k.P.mm(ps[:, 0:512], slab_d[0:p, kc * DM + n * 128: kc * DM + (n + 1) * 128], hqt[0:p, kc * 512:(kc + 1) * 512], start=(kc == 0), stop=(kc == KC - 1), tag="dn")
            s = dstage[:, (n % 2) * 512:(n % 2) * 512 + 512]
            k.P.copy(s, ps[:, 0:512], engine="vector")
            k.P.dma(ffn_o[n * 128:(n + 1) * 128, t0:t0 + 512], s, queue="qscalar")
    return k


def build_l0w():
    k = K("l0w")
    wg_h = k.inp("wg", (352, DM))
    wu_h = k.inp("wu", (352, DM))
    wd_h = k.inp("wd", (128, 2816))
    wgq_o = k.outp("wgq", (352, DM))
    wuq_o = k.outp("wuq", (352, DM))
    wdq_o = k.outp("wdq", (128, 2816))
    k.mkpsums(2)
    for (w_h, w_o, rows, Kd) in ((wg_h, wgq_o, 352, DM), (wu_h, wuq_o, 352, DM), (wd_h, wdq_o, 128, 2816)):
        wt = k.sb("wt", 128, Kd, F32)
        wq = k.sb("wq", 128, Kd, F32)
        sc = k.sb("sc", 128, Kd // 16, F32)
        rsc = k.sb("rsc", 128, Kd // 16, F32)
        qa = k.sb("qa", 128, Kd, F32)
        qk2 = k.sb("qk2", 128, Kd, F32)
        qm2 = k.sb("qm2", 128, Kd, F32)
        for r0 in range(0, rows, 128):
            rp = min(128, rows - r0)
            k.P.dma(wt[0:rp, :], w_h[r0:r0 + rp, :], queue="qsync")
            k.P.red(sc[0:rp, :, None], wt[0:rp, :].rearrange("p (g i) -> p g i", i=16), ALU.max, absval=True)
            k.P.ts(sc[0:rp, :], sc[0:rp, :], 1.0 / 6.0, 1e-12, ALU.mult, ALU.max)
            k.P.reciprocal(rsc[0:rp, :], sc[0:rp, :])
            scb = bass.AP(sc.tensor if hasattr(sc, 'tensor') else sc[:].tensor, sc[:].offset,
                          [list(sc[:].ap[0][:]), list(sc[:].ap[1][:])][:2])
            scg = sc[0:rp, :, None]
            sc0 = bass.AP(scg.tensor, scg.offset, [list(scg.ap[0]), list(scg.ap[1]), [0, 16]])
            rscg = rsc[0:rp, :, None]
            rsc0 = bass.AP(rscg.tensor, rscg.offset, [list(rscg.ap[0]), list(rscg.ap[1]), [0, 16]])
            wtg = wt[0:rp, :].rearrange("p (g i) -> p g i", i=16)
            wqg = wq[0:rp, :].rearrange("p (g i) -> p g i", i=16)
            scr = {"v": wq[0:rp, :], "a": qa[0:rp, :], "kc": qk2[0:rp, :], "m": qm2[0:rp, :]}
            MID = [0.25, 0.75, 1.25, 1.75, 2.5, 3.5, 5.0]
            k.P.tt(wqg, wtg, rsc0, ALU.mult)
            k.P.act(qa[0:rp, :], wq[0:rp, :], AF.Abs)
            k.P.ts(qk2[0:rp, :], qa[0:rp, :], MID[0], None, ALU.is_gt)
            for m_ in MID[1:]:
                k.P.stt(qk2[0:rp, :], qa[0:rp, :], m_, qk2[0:rp, :], ALU.is_gt, ALU.add)
            k.P.ts(qm2[0:rp, :], qk2[0:rp, :], -4.0, 0.0, ALU.add, ALU.max)
            k.P.tt(qk2[0:rp, :], qk2[0:rp, :], qm2[0:rp, :], ALU.add)
            k.P.ts(qm2[0:rp, :], qk2[0:rp, :], -8.0, 0.0, ALU.add, ALU.max)
            k.P.ts(qm2[0:rp, :], qm2[0:rp, :], 0.5, None, ALU.mult)
            k.P.stt(qk2[0:rp, :], qk2[0:rp, :], 0.5, qm2[0:rp, :], ALU.mult, ALU.add)
            k.P.sign(qa[0:rp, :], wq[0:rp, :])
            k.P.tt(qk2[0:rp, :], qk2[0:rp, :], qa[0:rp, :], ALU.mult)
            k.P.tt(wqg, qk2[0:rp, :].rearrange("p (g i) -> p g i", i=16), sc0, ALU.mult)
            k.P.dma(w_o[r0:r0 + rp, :], wq[0:rp, :], queue="qsync")
    return k


def build_l0p():
    k = K("l0p")
    T = TCORE
    y_h = k.inp("y", (2048, T))
    res_h = k.inp("res", (2048, T))
    x_h = k.inp("x", (DM, T))
    ffn_h = k.inp("ffn", (DM, T))
    wout_h = k.inp("woutT", (2048, DM))
    bout_h = k.inp("bout", (DM,))
    x1_o = k.outp("x1", (DM, T))
    k.mkpsums(8)
    t1 = k.fm("t1", 16, T, BF16)
    ybuf = k.sb("yb", 128, T, F32)
    rbuf = k.sb("rb", 128, T, F32)
    for c in range(16):
        k.P.dma(ybuf[:], y_h[c * 128:(c + 1) * 128, :], queue="qsync")
        k.P.dma(rbuf[:], res_h[c * 128:(c + 1) * 128, :], queue="qgpsimd")
        k.P.act(rbuf[:], rbuf[:], AF.Silu)
        k.P.tt(t1.c(c), ybuf[:], rbuf[:], ALU.mult)
    slab_o = load_wslabT(k, wout_h, 2048, DM, "wo")
    bcols = load_bias_cols(k, bout_h, DM, "bout")
    xbuf = k.sb("xb", 128, 512, F32)
    fbuf = k.sb("fb", 128, 512, F32)
    ostage = k.sb("ostage", 128, 2 * 512, F32)

    def evac_out(n, t0, tw, ps):
        s = ostage[:, (t0 // 512 % 2) * 512:(t0 // 512 % 2) * 512 + tw]
        k.P.dma(xbuf[:, 0:tw], x_h[n * 128:(n + 1) * 128, t0:t0 + tw], queue="qsync")
        k.P.dma(fbuf[:, 0:tw], ffn_h[n * 128:(n + 1) * 128, t0:t0 + tw], queue="qgpsimd")
        k.P.ts(s, ps, bcols[:, n:n + 1], None, ALU.add)
        k.P.tt(s, s, xbuf[:, 0:tw], ALU.add)
        k.P.tt(s, s, fbuf[:, 0:tw], ALU.add)
        k.P.dma(x1_o[n * 128:(n + 1) * 128, t0:t0 + tw], s, queue="qscalar")
    linear(k, t1, slab_o, 2048, DM, T, evac_out, tag="wo")
    return k


def _f32(a):
    return np.ascontiguousarray(np.asarray(a), dtype=np.float32)


def _halo_slice(x_dm, t0, t1, pad_l, pad_r):
    D = x_dm.shape[0]
    lo, hi = t0 - pad_l, t1 + pad_r
    out = np.zeros((D, hi - lo), np.float32)
    clo, chi = max(lo, 0), min(hi, x_dm.shape[1])
    out[:, clo - lo:chi - lo] = x_dm[:, clo:chi]
    return out


def run_l0(inputs, x_dm, ks=None):
    kq = build_l0w()
    wg = _f32(inputs["L0_ffn_w_gate"])
    wu = _f32(inputs["L0_ffn_w_up"])
    wd = _f32(inputs["L0_ffn_w_down"])
    maps = []
    for c in range(NCORES):
        maps.append({"wg": wg[c * 352:(c + 1) * 352], "wu": wu[c * 352:(c + 1) * 352],
                     "wd": wd[c * 128:(c + 1) * 128]})
    r = kq.run(maps)
    wgq = np.concatenate([r.results[c]["wgq"] for c in range(NCORES)], 0)
    wuq = np.concatenate([r.results[c]["wuq"] for c in range(NCORES)], 0)
    wdq = np.concatenate([r.results[c]["wdq"] for c in range(NCORES)], 0)

    ka = build_l0a()
    winT = _f32(inputs["L0_attn_W_in"])
    wxT = _f32(inputs["L0_attn_W_x"])
    wdtT = _f32(inputs["L0_attn_W_dt"])
    kconv = _f32(np.transpose(np.asarray(inputs["L0_attn_K_conv"]), (2, 1, 0)).reshape(2048, 4))
    bconv = _f32(inputs["L0_attn_b_conv"])
    bdt = _f32(inputs["L0_attn_b_dt"])
    maps = []
    for b in range(BATCH):
        for h in range(2):
            t0, t1 = h * TCORE, (h + 1) * TCORE
            maps.append({"x": _halo_slice(x_dm[b], t0, t1, 1, 3),
                         "winT": winT, "wxT": wxT, "wdtT": wdtT,
                         "kconv": kconv, "bconv": bconv, "bdt": bdt})
    r = ka.run(maps)
    RES = r.results
    xn_cores = [RES[c]["xn"][:, 1:1 + TCORE] for c in range(NCORES)]
    res_cores = [RES[c]["res"][:, 1:1 + TCORE] for c in range(NCORES)]
    u_full = [np.concatenate([RES[2 * b]["u"], RES[2 * b + 1]["u"]], 1) for b in range(BATCH)]
    d_full = [np.concatenate([RES[2 * b]["delta"], RES[2 * b + 1]["delta"]], 1) for b in range(BATCH)]
    B_full = [np.concatenate([RES[2 * b]["B"], RES[2 * b + 1]["B"]], 1) for b in range(BATCH)]
    C_full = [np.concatenate([RES[2 * b]["Cc"], RES[2 * b + 1]["Cc"]], 1) for b in range(BATCH)]

    if ks is None:
        ks = build_l0s()
    alog = _f32(inputs["L0_attn_A_log"])
    dvec = _f32(inputs["L0_attn_D"])
    maps = []
    for b in range(BATCH):
        for dh in range(2):
            sl = slice(dh * 1024, (dh + 1) * 1024)
            maps.append({"delta": d_full[b][sl], "u": u_full[b][sl],
                         "B": B_full[b], "Cc": C_full[b],
                         "alog": alog[sl], "dd": dvec[sl]})
    r = ks.run(maps)
    y_full = [np.concatenate([r.results[2 * b]["y"], r.results[2 * b + 1]["y"]], 0) for b in range(BATCH)]

    kf1 = build_l0f1()
    wgqT = _f32(wgq.T)
    wuqT = _f32(wuq.T)
    wdqT = _f32(wdq.T)
    maps = []
    for c in range(NCORES):
        maps.append({"xn": xn_cores[c], "xntm": _f32(xn_cores[c].T),
                     "wgqT": wgqT, "wuqT": wuqT})
    r = kf1.run(maps)
    hq_cores = [r.results[c]["hq"] for c in range(NCORES)]
    kf2 = build_l0f2()
    maps = [{"hq": hq_cores[c], "wdqT": wdqT} for c in range(NCORES)]
    r = kf2.run(maps)
    ffn_cores = [r.results[c]["ffn"] for c in range(NCORES)]

    kp = build_l0p()
    woutT = _f32(inputs["L0_attn_W_out"])
    bout = _f32(inputs["L0_attn_b_out"])
    maps = []
    for b in range(BATCH):
        for h in range(2):
            c = 2 * b + h
            t0, t1 = h * TCORE, (h + 1) * TCORE
            maps.append({"y": y_full[b][:, t0:t1], "res": res_cores[c],
                         "x": x_dm[b][:, t0:t1], "ffn": ffn_cores[c],
                         "woutT": woutT, "bout": bout})
    r = kp.run(maps)
    x1_dm = [np.concatenate([r.results[2 * b]["x1"], r.results[2 * b + 1]["x1"]], 1) for b in range(BATCH)]
    return x1_dm


def kernel(**inputs):
    x = _f32(inputs["L0_attn_inputs"])
    x_dm = [np.ascontiguousarray(x[b].T) for b in range(BATCH)]
    ks = build_l0s()
    x_dm = run_l0(inputs, x_dm, ks)
    x_dm = run_l1(inputs, x_dm)
    x_dm = run_l2(inputs, x_dm, ks)
    x_dm = run_l3(inputs, x_dm)
    out = np.stack([x_dm[b].T for b in range(BATCH)], 0)
    return _f32(out)


def _psum_named(k, name, f=512):
    return k.st.enter_context(k.nc.psum_tensor(k.uid(name), [128, f], F32))


def build_l1a():
    k = K("l1a")
    Te = TCORE + 4
    DIN = 2048
    CONV = 2304
    NPROJ = 4384
    x_h = k.inp("x", (DM, Te))
    winT = k.inp("winT", (DM, NPROJ))
    convw_h = k.inp("convw", (CONV, 4))
    convb_h = k.inp("convb", (CONV,))
    dtb_h = k.inp("dtb", (32,))
    xn_o = k.outp("xn", (DM, Te))
    z_o = k.outp("z", (DIN, Te))
    xc_o = k.outp("xc", (CONV, TCORE))
    dt_o = k.outp("dt", (32, TCORE))
    k.mkpsums(8)
    xn_bf = k.fm("xn_bf", 8, Te, BF16)
    layernorm_stream(k, x_h, DM, Te, xn_bf, xn_f32_h=xn_o)

    xbc = k.fm("xbc", 18, Te, BF16)
    slab = load_wslabT(k, winT, DM, 2048, "win", slabN=2336)
    stage = k.sb("stage", 128, 2 * 512, F32)
    dsc1 = k.sb("dsc1", 32, 512, F32)
    dsc2 = k.sb("dsc2", 32, 512, F32)
    dsc3 = k.sb("dsc3", 32, 512, F32)
    dtb_c = k.sb("dtbc", 32, 1, F32)
    k.P.dma(dtb_c[:], dtb_h[:, None], queue="qsync")

    def evac_z(n, t0, tw, ps):
        s = stage[:, (t0 // 512 % 2) * 512:(t0 // 512 % 2) * 512 + tw]
        k.P.copy(s, ps, engine="vector")
        k.P.dma(z_o[n * 128:(n + 1) * 128, t0:t0 + tw], s, queue="qscalar")
    linear(k, xn_bf, slab, DM, 2048, Te, evac_z, tag="winz", slabN=2336)
    load_wslabT(k, winT, DM, 2336, "win2", slab=slab, col0=2048)

    def evac_bc(n, t0, tw, ps):
        if n < 18:
            k.P.copy(xbc.c(n, t0, t0 + tw), ps, engine="scalar")
        else:
            lo = max(t0, 3)
            hi = min(t0 + tw, 3 + TCORE)
            if hi > lo:
                k.P.ts(dsc1[:, 0:tw], ps[0:32, :], dtb_c[:, 0:1], None, ALU.add)
                softplus_tiles(k, dsc1[:, 0:tw], dsc1[:, 0:tw], dsc2[:, 0:tw], dsc3[:, 0:tw])
                k.P.dma(dt_o[:, lo - 3:hi - 3], dsc1[0:32, lo - t0:hi - t0], queue="qscalar")
    linear(k, xn_bf, slab, DM, 2336, Te, evac_bc, tag="winbc")

    kw = k.sb("kw", 128, 18 * 4, F32)
    for c in range(18):
        k.P.dma(kw[:, c * 4:(c + 1) * 4], convw_h[c * 128:(c + 1) * 128, :], queue="qsync")
    bcv = load_bias_cols(k, convb_h, CONV, "convb")
    cacc = k.sb("cacc", 128, 512, F32)
    ustage = k.sb("ustage", 128, 2 * 512, F32)
    for c in range(18):
        for t0 in range(0, TCORE, 512):
            tw = min(512, TCORE - t0)
            base = c * Te + t0
            k.P.ts(cacc[:, 0:tw], xbc.t[:, base:base + tw], kw[:, c * 4:c * 4 + 1], bcv[:, c:c + 1], ALU.mult, ALU.add)
            for i in (1, 2, 3):
                k.P.stt(cacc[:, 0:tw], xbc.t[:, base + i:base + i + tw], kw[:, c * 4 + i:c * 4 + i + 1], cacc[:, 0:tw], ALU.mult, ALU.add)
            s = ustage[:, (t0 // 512 % 2) * 512:(t0 // 512 % 2) * 512 + tw]
            k.P.act(s, cacc[:, 0:tw], AF.Silu)
            k.P.dma(xc_o[c * 128:(c + 1) * 128, t0:t0 + tw], s, queue="qscalar")
    return k


def build_l1s():
    k = K("l1s")
    T = SEQ
    NH = 16
    HD = 64
    NST = 128
    CH = 128
    NCH = T // CH
    xh_h = k.inp("xh", (NH * HD, T))
    dt_h = k.inp("dt", (NH, T))
    B_h = k.inp("B", (NST, T))
    C_h = k.inp("Cc", (NST, T))
    alog_h = k.inp("alog", (NH,))
    drep_h = k.inp("drep", (NH * HD,))
    y_o = k.outp("y", (NH * HD, T))
    k.mkpsums(3)
    psg = [_psum_named(k, "psg") for _ in range(4)]
    psbt = k.st.enter_context(k.nc.psum_tensor(k.uid("psbt"), [128, 128], BF16))
    idf = k.ident_f()
    idb = k.ident_bf()

    B_bf = k.sb("Bbf", 128, T, BF16)
    C_bf = k.sb("Cbf", 128, T, BF16)
    k.P.dma(B_bf[:], B_h, queue="qgpsimd")
    k.P.dma(C_bf[:], C_h, queue="qgpsimd")
    dt_t = k.sb("dtt", NH, T, F32)
    k.P.dma(dt_t[:], dt_h, queue="qsync")
    arow = k.sb("arow", NH, 1, F32)
    k.P.dma(arow[:], alog_h[:, None], queue="qsync")
    k.P.act(arow[:], arow[:], AF.Exp)
    k.P.ts(arow[:], arow[:], -1.0, None, ALU.mult)
    dta = k.sb("dta", NH, T, F32)
    k.P.ts(dta[:], dt_t[:], arow[:, 0:1], None, ALU.mult)
    zrow = k.sb("zrow", NH, CH, F32)
    k.P.memset(zrow[:], 0.0)
    drepc = k.sb("drepc", 64, NH, F32)
    for h in range(NH):
        k.P.dma(drepc[:, h:h + 1], drep_h[h * 64:(h + 1) * 64, None], queue="qsync")

    Gc = k.sb("Gc", NH, CH, F32)
    nGc = k.sb("nGc", NH, CH, F32)
    G_tm = k.sb("Gtm", 128, NH, F32)
    negG = k.sb("negG", 128, NH, F32)
    dt_tm = k.sb("dttm", 128, NH, F32)
    grow = k.sb("grow", 1, NH * CH, F32)
    ngrow = k.sb("ngrow", 1, NH * CH, F32)
    Xc2 = k.sb("Xc2", 64, NH * CH, F32)
    bct = k.sb("bct", 128, CH, F32)
    btm = k.sb("btm", 128, NST, BF16)
    ltf = k.sb("ltf", 128, CH, F32)
    stbf = k.sb("stbf", 128, CH, BF16)
    dtxf = k.sb("dtxf", 128, HD, F32)
    dtx_bf = k.sb("dtxbf", 128, HD, BF16)
    dtxw_bf = k.sb("dtxwbf", 128, HD, BF16)
    dec_c = k.sb("dec", 128, 1, F32)
    gam_c = k.sb("gam", 128, 1, F32)
    elf = k.sb("elf", 128, CH, F32)
    cw_bf = k.sb("cwbf", 128, CH, BF16)
    S = k.sb("S", 128, NH * HD, F32)
    S_bf = k.sb("Sbf", 128, NH * HD, BF16)
    k.P.memset(S[:], 0.0)
    k.P.memset(S_bf[:], 0.0)
    ystage = k.sb("ystage", 64, CH, F32)
    lscr = k.sb("lscr", 128, CH, F32)
    maskadd = k.const_sb("c_maskadd", np.where(np.triu(np.ones((128, 128))) > 0, 0.0, -1e9).astype(np.float32))

    for ci in range(NCH):
        c0 = ci * CH
        k.P.scan(Gc[:], dta[:, c0:c0 + CH], zrow[:], 0.0, op0=ALU.add, op1=ALU.add, tag="gcum")
        pst = k.ps()
        k.P.transpose(pst[0:CH, 0:NH], Gc[:], idf[0:NH, 0:NH], tag="gT")
        k.P.copy(G_tm[:, 0:NH], pst[0:CH, 0:NH], engine="vector")
        k.P.ts(negG[:, 0:NH], G_tm[:, 0:NH], -1.0, None, ALU.mult)
        pst2 = k.ps()
        k.P.transpose(pst2[0:CH, 0:NH], dt_t[:, c0:c0 + CH], idf[0:NH, 0:NH], tag="dtT")
        k.P.copy(dt_tm[:, 0:NH], pst2[0:CH, 0:NH], engine="vector")
        k.P.dma(grow[:], Gc[:], queue="qsync")
        for j in range(4):
            k.P.mm(psg[j][:, 0:512], k.ones_row_f(), grow[:, j * 512:(j + 1) * 512], tag="gbc")
        for h in range(NH):
            k.P.dma(Xc2[:, h * CH:(h + 1) * CH], xh_h[h * 64:(h + 1) * 64, c0:c0 + CH], queue="qgpsimd")
        psbc = k.ps()
        k.P.mm(psbc[:, 0:CH], B_bf[:, c0:c0 + CH], C_bf[:, c0:c0 + CH], tag="bctmm")
        k.P.copy(bct[:], psbc[:, 0:CH], engine="vector")
        k.P.transpose(psbt[:, 0:NST], B_bf[:, c0:c0 + CH], idb, tag="bT")
        k.P.copy(btm[:], psbt[:, 0:NST], engine="vector")
        for h in range(NH):
            gi = psg[h // 4][:, (h % 4) * 128:(h % 4) * 128 + 128]
            k.P.tt(lscr[:], gi, maskadd[:], ALU.add, tag="ltmaskadd")
            k.P.act(ltf[:], lscr[:], AF.Exp, bias=negG[:, h:h + 1], tag="ltexp")
            k.P.tt(stbf[:], ltf[:], bct[:], ALU.mult, tag="stmul")
            pxt = k.ps()
            k.P.transpose(pxt[0:CH, 0:HD], Xc2[:, h * CH:(h + 1) * CH], idf[0:64, 0:64], tag="xT")
            k.P.ts(dtxf[:, 0:HD], pxt[0:CH, 0:HD], dt_tm[:, h:h + 1], None, ALU.mult, tag="dtx")
            k.P.copy(dtx_bf[:, 0:HD], dtxf[:, 0:HD], engine="gpsimd")
            k.P.act(dec_c[:, 0:1], gi[:, CH - 1:CH], AF.Exp, bias=negG[:, h:h + 1], tag="dec")
            k.P.act(gam_c[:, 0:1], gi[:, CH - 1:CH], AF.Exp, tag="gam")
            k.P.ts(dtxw_bf[:, 0:HD], dtxf[:, 0:HD], dec_c[:, 0:1], None, ALU.mult, tag="dtxw")
            psy = k.ps()
            k.P.mm(psy[0:HD, 0:CH], dtx_bf[:, 0:HD], stbf[:], start=True, stop=False, tag="yintra")
            k.P.act(elf[:], gi, AF.Exp, tag="el")
            k.P.tt(cw_bf[:], C_bf[:, c0:c0 + CH], elf[:], ALU.mult, tag="cw")
            k.P.mm(psy[0:HD, 0:CH], S_bf[:, h * HD:(h + 1) * HD], cw_bf[:], start=False, stop=True, tag="yinter")
            k.P.stt(ystage[:], Xc2[:, h * CH:(h + 1) * CH], drepc[:, h:h + 1], psy[0:HD, 0:CH], ALU.mult, ALU.add, tag="yev")
            k.P.dma(y_o[h * HD:(h + 1) * HD, c0:c0 + CH], ystage[:], queue="qscalar")
            pss = k.ps()
            k.P.mm(pss[:, 0:HD], btm[:], dtxw_bf[:, 0:HD], tag="contrib")
            k.P.stt(S[:, h * HD:(h + 1) * HD], S[:, h * HD:(h + 1) * HD], gam_c[:, 0:1], pss[:, 0:HD], ALU.mult, ALU.add, tag="supd")
            k.P.copy(S_bf[:, h * HD:(h + 1) * HD], S[:, h * HD:(h + 1) * HD], engine="gpsimd")
    return k


def build_l1p():
    k = K("l1p")
    T = TCORE
    DIN = 2048
    y_h = k.inp("y", (DIN, T))
    z_h = k.inp("z", (DIN, T))
    x_h = k.inp("x", (DM, T))
    woutT_h = k.inp("woutT", (DIN, DM))
    nw_h = k.inp("nw", (DIN,))
    x2a_o = k.outp("x2a", (DM, T))
    k.mkpsums(8)
    t1 = k.fm("t1", 16, T, BF16)
    ybuf = k.sb("yb", 128, T, F32)
    zbuf = k.sb("zb", 128, T, F32)
    sq = k.sb("sq", 128, 512, BF16)
    nwc = load_bias_cols(k, nw_h, DIN, "nw")
    for c in range(16):
        k.P.dma(ybuf[:], y_h[c * 128:(c + 1) * 128, :], queue="qsync")
        k.P.dma(zbuf[:], z_h[c * 128:(c + 1) * 128, :], queue="qgpsimd")
        k.P.act(zbuf[:], zbuf[:], AF.Silu)
        k.P.tt(t1.c(c), ybuf[:], zbuf[:], ALU.mult)
    rows = k.sb("rows", 1, 3 * 512, F32)
    yn = k.fm("yn", 16, T, BF16)
    for t0 in range(0, T, 512):
        psq = k.ps()
        for c in range(16):
            k.P.act(sq[:], t1.c(c, t0, t0 + 512), AF.Square, tag="rsq")
            k.P.mm(psq[0:1, 0:512], k.ones_col_bf(), sq[:], start=(c == 0), stop=(c == 15), tag="rsum")
        r0 = rows[:, 0:512]
        k.P.ts(r0, psq[0:1, 0:512], 1.0 / DIN, 1e-5, ALU.mult, ALU.add)
        k.P.act(rows[:, 512:1024], r0, AF.Sqrt)
        k.P.reciprocal(r0, rows[:, 512:1024])
        bps = bcast_row_psum(k, r0, 512)
        for c in range(16):
            k.P.stt(yn.c(c, t0, t0 + 512), t1.c(c, t0, t0 + 512), nwc[:, c:c + 1], bps[:, 0:512], ALU.mult, ALU.mult, tag="ynorm")
    slab_o = load_wslabT(k, woutT_h, DIN, DM, "wo")
    xbuf = k.sb("xb", 128, 512, F32)
    ostage = k.sb("ostage", 128, 2 * 512, F32)

    def evac_out(n, t0, tw, ps):
        s = ostage[:, (t0 // 512 % 2) * 512:(t0 // 512 % 2) * 512 + tw]
        k.P.dma(xbuf[:, 0:tw], x_h[n * 128:(n + 1) * 128, t0:t0 + tw], queue="qsync")
        k.P.tt(s, ps, xbuf[:, 0:tw], ALU.add)
        k.P.dma(x2a_o[n * 128:(n + 1) * 128, t0:t0 + tw], s, queue="qscalar")
    linear(k, yn, slab_o, DIN, DM, T, evac_out, tag="wo")
    return k


def build_l1f():
    k = K("l1f")
    T = TCORE
    FF = 4096
    xn_h = k.inp("xn", (DM, T))
    wfcT_h = k.inp("wfcT", (DM, FF))
    wprT_h = k.inp("wprT", (FF, DM))
    x2a_h = k.inp("x2a", (DM, T))
    x2_o = k.outp("x2", (DM, T))
    k.mkpsums(8)
    xn_bf = k.fm("xn_bf", 8, T, BF16)
    for c in range(8):
        k.P.dma(xn_bf.c(c), xn_h[c * 128:(c + 1) * 128, :], queue="qgpsimd")
    slab_fc = load_wslabT(k, wfcT_h, DM, FF, "wfc")
    slab_pr = load_wslabT(k, wprT_h, FF, DM, "wpr")
    hsp = k.sb("hsp", 128, 32 * 512, BF16)
    mem = k.sb("mem", 128, 512, F32)
    acc = k.sb("acc", 128, 512, F32)
    rst = k.sb("rst", 128, 512, F32)
    x2abuf = k.sb("x2ab", 128, 512, F32)
    fstage = k.sb("fstage", 128, 2 * 512, F32)
    for t0 in range(0, T, 512):
        for n in range(32):
            ps = k.ps()
            for kc in range(8):
                k.P.mm(ps[:, 0:512], slab_fc[:, kc * FF + n * 128: kc * FF + (n + 1) * 128], xn_bf.c(kc, t0, t0 + 512), start=(kc == 0), stop=(kc == 7), tag="fcmm")
            k.P.ts(mem[:], ps[:, 0:512], 0.0, None, ALU.max, tag="fr")
            k.P.tt(hsp[:, n * 512:(n + 1) * 512], mem[:], mem[:], ALU.mult, tag="fsq")
        for n2 in range(8):
            ps = k.ps()
            for kc in range(32):
                k.P.mm(ps[:, 0:512], slab_pr[:, kc * DM + n2 * 128:kc * DM + (n2 + 1) * 128], hsp[:, kc * 512:(kc + 1) * 512], start=(kc == 0), stop=(kc == 31), tag="prmm")
            k.P.copy(mem[:], ps[:, 0:512], engine="vector")
            k.P.ts(acc[:], mem[:], 1.0, None, ALU.is_gt, tag="spk1")
            for it in range(4):
                k.P.ts(rst[:], mem[:], 1.0, None, ALU.is_gt, tag="rst")
                k.P.stt(mem[:], mem[:], 0.9, ps[:, 0:512], ALU.mult, ALU.add, tag="mup")
                k.P.tt(mem[:], mem[:], rst[:], ALU.subtract, tag="mrst")
                k.P.stt(acc[:], mem[:], 1.0, acc[:], ALU.is_gt, ALU.add, tag="spka")
            s = fstage[:, (n2 % 2) * 512:(n2 % 2) * 512 + 512]
            k.P.dma(x2abuf[:], x2a_h[n2 * 128:(n2 + 1) * 128, t0:t0 + 512], queue="qsync")
            k.P.stt(s, acc[:], 0.2, x2abuf[:], ALU.mult, ALU.add, tag="fout")
            k.P.dma(x2_o[n2 * 128:(n2 + 1) * 128, t0:t0 + 512], s, queue="qscalar")
    return k


def run_l1(inputs, x_dm):
    ka = build_l1a()
    winT = _f32(np.asarray(inputs["L1_attn_W_in"]).T)
    convw = _f32(inputs["L1_attn_conv_w"])
    convb = _f32(inputs["L1_attn_conv_b"])
    dtb = _f32(inputs["L1_attn_dt_bias"])
    maps = []
    for b in range(BATCH):
        for h in range(2):
            t0, t1 = h * TCORE, (h + 1) * TCORE
            maps.append({"x": _halo_slice(x_dm[b], t0, t1, 3, 1),
                         "winT": winT, "convw": convw, "convb": convb, "dtb": dtb})
    r = ka.run(maps)
    RES = r.results
    xn_cores = [RES[c]["xn"][:, 3:3 + TCORE] for c in range(NCORES)]
    z_cores = [RES[c]["z"][:, 3:3 + TCORE] for c in range(NCORES)]
    xc_full = [np.concatenate([RES[2 * b]["xc"], RES[2 * b + 1]["xc"]], 1) for b in range(BATCH)]
    dt_full = [np.concatenate([RES[2 * b]["dt"], RES[2 * b + 1]["dt"]], 1) for b in range(BATCH)]

    ks = build_l1s()
    alog = _f32(inputs["L1_attn_A_log"])
    drep = _f32(np.repeat(np.asarray(inputs["L1_attn_D"]), 64))
    maps = []
    for b in range(BATCH):
        for hg in range(2):
            rs = slice(hg * 1024, (hg + 1) * 1024)
            maps.append({"xh": xc_full[b][:2048][rs], "dt": dt_full[b][hg * 16:(hg + 1) * 16],
                         "B": xc_full[b][2048:2176], "Cc": xc_full[b][2176:2304],
                         "alog": alog[hg * 16:(hg + 1) * 16], "drep": drep[rs]})
    r = ks.run(maps)
    y_full = [np.concatenate([r.results[2 * b]["y"], r.results[2 * b + 1]["y"]], 0) for b in range(BATCH)]

    kp = build_l1p()
    woutT = _f32(np.asarray(inputs["L1_attn_W_out"]).T)
    nw = _f32(inputs["L1_attn_norm_w"])
    maps = []
    for b in range(BATCH):
        for h in range(2):
            c = 2 * b + h
            t0, t1 = h * TCORE, (h + 1) * TCORE
            maps.append({"y": y_full[b][:, t0:t1], "z": z_cores[c],
                         "x": x_dm[b][:, t0:t1], "woutT": woutT, "nw": nw})
    r = kp.run(maps)
    x2a_cores = [r.results[c]["x2a"] for c in range(NCORES)]

    kf = build_l1f()
    wfcT = _f32(np.asarray(inputs["L1_ffn_W_fc"]).T)
    wprT = _f32(np.asarray(inputs["L1_ffn_W_proj"]).T)
    maps = []
    for c in range(NCORES):
        maps.append({"xn": xn_cores[c], "wfcT": wfcT, "wprT": wprT, "x2a": x2a_cores[c]})
    r = kf.run(maps)
    x2_dm = [np.concatenate([r.results[2 * b]["x2"], r.results[2 * b + 1]["x2"]], 1) for b in range(BATCH)]
    return x2_dm



def build_l2a_pre(parts=5):
    k = K("l2a_pre")
    Te = TCORE + 4
    x_h = k.inp("x", (DM, Te))
    nw_h = k.inp("nw", (DM,))
    winT = k.inp("winT", (DM, 4096))
    convw_h = k.inp("convw", (2048, 4))
    convb_h = k.inp("convb", (2048,))
    wxT_h = k.inp("wxT", (2048, 96))
    wdtT_h = k.inp("wdtT", (64, 2048))
    bdt_h = k.inp("bdt", (2048,))
    z_o = k.outp("z", (2048, Te))
    xc_o = k.outp("xc", (2048, TCORE))
    delta_o = k.outp("delta", (2048, TCORE))
    B_o = k.outp("B", (16, TCORE))
    C_o = k.outp("Cc", (16, TCORE))
    k.mkpsums(8)
    xn_bf = k.fm("xn_bf", 8, Te, BF16)
    nwc = load_bias_cols(k, nw_h, DM, "nw")
    xt = k.sb("rms_x", 128, 8 * 512, F32)
    sqb = k.sb("rms_sq", 128, 512, BF16)
    rows = k.sb("rms_rows", 1, 2 * 512, F32)
    for t0 in range(0, Te, 512):
        tw = min(512, Te - t0)
        for c in range(8):
            k.P.dma(xt[:, c * 512:c * 512 + tw], x_h[c * 128:(c + 1) * 128, t0:t0 + tw])
        psq = k.ps()
        for c in range(8):
            k.P.act(sqb[:, 0:tw], xt[:, c * 512:c * 512 + tw], AF.Square, tag="rsq")
            k.P.mm(psq[0:1, 0:tw], k.ones_col_bf(), sqb[:, 0:tw], start=(c == 0), stop=(c == 7), tag="rsum")
        r0 = rows[:, 0:tw]
        k.P.ts(r0, psq[0:1, 0:tw], 1.0 / DM, 1e-5, ALU.mult, ALU.add)
        k.P.act(rows[:, 512:512 + tw], r0, AF.Sqrt)
        k.P.reciprocal(r0, rows[:, 512:512 + tw])
        bps = bcast_row_psum(k, r0, tw)
        for c in range(8):
            k.P.stt(xn_bf.c(c, t0, t0 + tw), xt[:, c * 512:c * 512 + tw], nwc[:, c:c + 1], bps[:, 0:tw], ALU.mult, ALU.mult, tag="rnorm")
    if parts < 1:
        return k
    slab = load_wslabT(k, winT, DM, 2048, "win", col0=2048)
    stage = k.sb("stage", 128, 2 * 512, F32)

    def evac_z(n, t0, tw, ps):
        s = stage[:, (t0 // 512 % 2) * 512:(t0 // 512 % 2) * 512 + tw]
        k.P.copy(s, ps, engine="vector")
        k.P.dma(z_o[n * 128:(n + 1) * 128, t0:t0 + tw], s, queue="qscalar")
    linear(k, xn_bf, slab, DM, 2048, Te, evac_z, tag="winz")
    if parts < 2:
        return k
    load_wslabT(k, winT, DM, 2048, "win2", slab=slab, col0=0)
    kw = k.sb("kw", 128, 16 * 4, F32)
    for c in range(16):
        k.P.dma(kw[:, c * 4:(c + 1) * 4], convw_h[c * 128:(c + 1) * 128, :], queue="qsync")
    bcv = load_bias_cols(k, convb_h, 2048, "convb")
    xi_c = k.sb("xi_c", 128, Te, BF16)
    xc_bf = k.fm("xc_bf", 16, TCORE, BF16)
    cacc = k.sb("cacc", 128, 512, F32)
    ustage = k.sb("ustage", 128, 2 * 512, F32)
    KC = 8
    for n in range(16):
        for t0 in range(0, Te, 512):
            tw = min(512, Te - t0)
            ps = k.ps()
            for kc in range(KC):
                k.P.mm(ps[:, 0:tw], slab[:, kc * 2048 + n * 128:(kc * 2048 + n * 128 + 128)], xn_bf.c(kc, t0, t0 + tw), start=(kc == 0), stop=(kc == KC - 1), tag="xi")
            k.P.copy(xi_c[:, t0:t0 + tw], ps[:, 0:tw], engine="scalar")
        for t0 in range(0, TCORE, 512):
            tw = min(512, TCORE - t0)
            k.P.ts(cacc[:, 0:tw], xi_c[:, t0:t0 + tw], kw[:, n * 4:n * 4 + 1], bcv[:, n:n + 1], ALU.mult, ALU.add)
            for i in (1, 2, 3):
                k.P.stt(cacc[:, 0:tw], xi_c[:, t0 + i:t0 + i + tw], kw[:, n * 4 + i:n * 4 + i + 1], cacc[:, 0:tw], ALU.mult, ALU.add)
            s = ustage[:, (t0 // 512 % 2) * 512:(t0 // 512 % 2) * 512 + tw]
            k.P.act(s, cacc[:, 0:tw], AF.Silu)
            k.P.copy(xc_bf.c(n, t0, t0 + tw), s, engine="gpsimd")
            k.P.dma(xc_o[n * 128:(n + 1) * 128, t0:t0 + tw], s, queue="qscalar")
    if parts < 3:
        return k
    slab_x = load_wslabT(k, wxT_h, 2048, 96, "wx")
    xdbl = k.sb("xdbl", 96, TCORE, F32)
    xdbl_bf = k.sb("xdbl_bf", 64, TCORE, BF16)

    def evac_xdbl(n, t0, tw, ps):
        k.P.copy(xdbl[:, t0:t0 + tw], ps, engine="vector")
        k.P.copy(xdbl_bf[:, t0:t0 + tw], ps[0:64, :], engine="scalar")
    linear(k, xc_bf, slab_x, 2048, 96, TCORE, evac_xdbl, tag="wx")
    k.P.dma(B_o, xdbl[64:80, :], queue="qsync")
    k.P.dma(C_o, xdbl[80:96, :], queue="qsync")
    if parts < 4:
        return k
    slab_dt = load_wslabT(k, wdtT_h, 64, 2048, "wdt")
    bdt_cols = load_bias_cols(k, bdt_h, 2048, "bdt")
    xdbl_fm = Fm(xdbl_bf, 1, TCORE)
    dsc1 = k.sb("dsc1", 128, 512, F32)
    dsc2 = k.sb("dsc2", 128, 512, F32)
    dstage = k.sb("dstage", 128, 2 * 512, F32)

    def evac_delta(n, t0, tw, ps):
        s = dstage[:, (t0 // 512 % 2) * 512:(t0 // 512 % 2) * 512 + tw]
        k.P.ts(s, ps, bdt_cols[:, n:n + 1], None, ALU.add)
        softplus_tiles(k, s, s, dsc1[:, 0:tw], dsc2[:, 0:tw])
        k.P.dma(delta_o[n * 128:(n + 1) * 128, t0:t0 + tw], s, queue="qscalar")
    linear(k, xdbl_fm, slab_dt, 64, 2048, TCORE, evac_delta, tag="wdt")
    return k


def build_l2a_post():
    k = K("l2a_post")
    T = TCORE
    y_h = k.inp("y", (2048, T))
    z_h = k.inp("z", (2048, T))
    x_h = k.inp("x", (DM, T))
    woutT_h = k.inp("woutT", (2048, DM))
    xo_o = k.outp("xo", (DM, T))
    k.mkpsums(8)
    t1 = k.fm("t1", 16, T, BF16)
    ybuf = k.sb("yb", 128, T, F32)
    zbuf = k.sb("zb", 128, T, F32)
    for c in range(16):
        k.P.dma(ybuf[:], y_h[c * 128:(c + 1) * 128, :], queue="qsync")
        k.P.dma(zbuf[:], z_h[c * 128:(c + 1) * 128, :], queue="qgpsimd")
        k.P.act(zbuf[:], zbuf[:], AF.Silu)
        k.P.tt(t1.c(c), ybuf[:], zbuf[:], ALU.mult)
    slab_o = load_wslabT(k, woutT_h, 2048, DM, "wo")
    xbuf = k.sb("xb", 128, 512, F32)
    ostage = k.sb("ostage", 128, 2 * 512, F32)

    def evac_out(n, t0, tw, ps):
        s = ostage[:, (t0 // 512 % 2) * 512:(t0 // 512 % 2) * 512 + tw]
        k.P.dma(xbuf[:, 0:tw], x_h[n * 128:(n + 1) * 128, t0:t0 + tw], queue="qsync")
        k.P.tt(s, ps, xbuf[:, 0:tw], ALU.add)
        k.P.dma(xo_o[n * 128:(n + 1) * 128, t0:t0 + tw], s, queue="qscalar")
    linear(k, t1, slab_o, 2048, DM, T, evac_out, tag="wo")
    return k


def run_l2a(inputs, x_dm, ks_scan):
    kpre = build_l2a_pre()
    kpost = build_l2a_post()
    for li in range(4):
        winT = _f32(np.asarray(inputs["L2_attn_W_in"][li]).T)
        nw = _f32(inputs["L2_attn_norm_w"][li])
        convw = _f32(inputs["L2_attn_conv_w"][li])
        convb = _f32(inputs["L2_attn_conv_b"][li])
        wxT = _f32(np.asarray(inputs["L2_attn_W_x"][li]).T)
        wdtT = _f32(np.asarray(inputs["L2_attn_W_dt"][li]).T)
        bdt = _f32(inputs["L2_attn_b_dt"][li])
        maps = []
        for b in range(BATCH):
            for h in range(2):
                t0, t1 = h * TCORE, (h + 1) * TCORE
                maps.append({"x": _halo_slice(x_dm[b], t0, t1, 3, 1), "nw": nw,
                             "winT": winT, "convw": convw, "convb": convb,
                             "wxT": wxT, "wdtT": wdtT, "bdt": bdt})
        r = kpre.run(maps)
        RES = r.results
        z_cores = [RES[c]["z"][:, 3:3 + TCORE] for c in range(NCORES)]
        xc_full = [np.concatenate([RES[2 * b]["xc"], RES[2 * b + 1]["xc"]], 1) for b in range(BATCH)]
        d_full = [np.concatenate([RES[2 * b]["delta"], RES[2 * b + 1]["delta"]], 1) for b in range(BATCH)]
        B_full = [np.concatenate([RES[2 * b]["B"], RES[2 * b + 1]["B"]], 1) for b in range(BATCH)]
        C_full = [np.concatenate([RES[2 * b]["Cc"], RES[2 * b + 1]["Cc"]], 1) for b in range(BATCH)]
        alog = _f32(inputs["L2_attn_A_log"][li])
        dvec = _f32(inputs["L2_attn_D"][li])
        maps = []
        for b in range(BATCH):
            for dh in range(2):
                sl = slice(dh * 1024, (dh + 1) * 1024)
                maps.append({"delta": d_full[b][sl], "u": xc_full[b][sl],
                             "B": B_full[b], "Cc": C_full[b],
                             "alog": alog[sl], "dd": dvec[sl]})
        r = ks_scan.run(maps)
        y_full = [np.concatenate([r.results[2 * b]["y"], r.results[2 * b + 1]["y"]], 0) for b in range(BATCH)]
        woutT = _f32(np.asarray(inputs["L2_attn_W_out"][li]).T)
        maps = []
        for b in range(BATCH):
            for h in range(2):
                c = 2 * b + h
                t0, t1 = h * TCORE, (h + 1) * TCORE
                maps.append({"y": y_full[b][:, t0:t1], "z": z_cores[c],
                             "x": x_dm[b][:, t0:t1], "woutT": woutT})
        r = kpost.run(maps)
        x_dm = [np.concatenate([r.results[2 * b]["xo"], r.results[2 * b + 1]["xo"]], 1) for b in range(BATCH)]
    return x_dm



def build_l2f_attn():
    k = K("l2fa")
    T = TCORE
    Te2 = T + 256
    NTIL = Te2 // 128
    x_h = k.inp("x", (DM, T + 257))
    wqkv_h = k.inp("wqkv", (DM, 3072))
    wo_h = k.inp("wo", (DM, DM))
    qsc_h = k.inp("qsc", (128,))
    ksc_h = k.inp("ksc", (128,))
    ctq_h = k.inp("ctq", (128, Te2))
    stq_h = k.inp("stq", (128, Te2))
    ctk_h = k.inp("ctk", (128, Te2))
    stk_h = k.inp("stk", (128, Te2))
    m0_h = k.inp("m0", (128, 512))
    m1_h = k.inp("m1", (128, 512))
    mf0_h = k.inp("mf0", (128, 512))
    mf1_h = k.inp("mf1", (128, 512))
    xatt_o = k.outp("xatt", (DM, T))
    k.mkpsums(6)
    psbt2 = [k.st.enter_context(k.nc.psum_tensor(k.uid("psbt"), [128, 512], BF16)) for _ in range(2)]
    perm64 = np.zeros((128, 128), ml_dtypes.bfloat16)
    for p in range(128):
        perm64[(p % 64 + 32) % 64 + 64 * (p // 64), p] = 1
    permc = k.const_sb("c_perm64", perm64)[:]
    b64 = np.zeros((128, 128), ml_dtypes.bfloat16)
    for kk in range(128):
        for p in range(128):
            if kk // 64 == p // 64:
                b64[kk, p] = 1
    b64c = k.const_sb("c_b64", b64)[:]
    masks = {}
    for nm, hh in (("m0", m0_h), ("m1", m1_h), ("mf0", mf0_h), ("mf1", mf1_h)):
        t = k.sb("msk_" + nm, 128, 512, F32)
        k.P.dma(t[:], hh, queue="qsync")
        masks[nm] = t
    qscc = k.sb("qscc", 128, 1, F32)
    kscc = k.sb("kscc", 128, 1, F32)
    k.P.dma(qscc[:], qsc_h[:, None], queue="qsync")
    k.P.dma(kscc[:], ksc_h[:, None], queue="qsync")

    slab = k.sb("wslab", 128, 8 * 1024, BF16)
    xs_t = k.sb("xs_t", 128, 8 * 512, BF16)
    tab = k.sb("tab", 128, 2 * 512, F32)
    q_rot = k.fm("q_rot", 8, Te2, BF16)
    k_rot = k.fm("k_rot", 8, Te2, BF16)
    v_tm = k.sb("v_tm", 128, NTIL * 1024, BF16)
    sq_b = k.sb("sq_b", 128, 512, BF16)
    nsc2 = k.sb("nsc2", 128, 512, F32)
    qa_f = k.sb("qa_f", 128, 512, F32)
    qa_b = k.sb("qa_b", 128, 512, BF16)
    rt1 = k.sb("rt1", 128, 512, F32)
    rt2 = k.sb("rt2", 128, 512, F32)
    t_tiles = [(0, 512), (512, 512), (1024, 512), (1536, 512), (2048, 256)]

    def build_xs(t0, tw):
        for c in range(8):
            off = 1 if c < 4 else 0
            k.P.dma(xs_t[:, c * 512:c * 512 + tw], x_h[c * 128:(c + 1) * 128, t0 + off:t0 + off + tw], queue="qgpsimd")

    for (col0, ct_h, st_h, scc, rot_fm) in ((0, ctq_h, stq_h, qscc, q_rot), (1024, ctk_h, stk_h, kscc, k_rot)):
        load_wslabT(k, wqkv_h, DM, 1024, "wqk", slab=slab, col0=col0)
        for (t0, tw) in t_tiles:
            build_xs(t0, tw)
            k.P.dma(tab[:, 0:tw], ct_h[:, t0:t0 + tw], queue="qsync")
            k.P.dma(tab[:, 512:512 + tw], st_h[:, t0:t0 + tw], queue="qscalar")
            for n in range(8):
                ps = k.ps()
                for kc in range(8):
                    k.P.mm(ps[:, 0:tw], slab[:, kc * 1024 + n * 128:kc * 1024 + n * 128 + 128], xs_t[:, kc * 512:kc * 512 + tw], start=(kc == 0), stop=(kc == 7), tag="qkmm")
                k.P.act(sq_b[:, 0:tw], ps[:, 0:tw], AF.Square, tag="l2sq")
                psn = k.ps()
                k.P.mm(psn[:, 0:tw], b64c, sq_b[:, 0:tw], tag="nsmm")
                k.P.ts(nsc2[:, 0:tw], psn[:, 0:tw], 1e-24, None, ALU.max, tag="l2mx")
                k.P.act(nsc2[:, 0:tw], nsc2[:, 0:tw], AF.Ln, tag="l2ln")
                k.P.act(nsc2[:, 0:tw], nsc2[:, 0:tw], AF.Exp, scale=-0.5, tag="l2ex")
                k.P.stt(qa_f[:, 0:tw], ps[:, 0:tw], scc[:, 0:1], nsc2[:, 0:tw], ALU.mult, ALU.mult, tag="l2ap")
                k.P.copy(qa_b[:, 0:tw], qa_f[:, 0:tw], engine="gpsimd")
                psp = k.ps()
                k.P.mm(psp[:, 0:tw], permc, qa_b[:, 0:tw], tag="perm")
                k.P.tt(rt1[:, 0:tw], qa_f[:, 0:tw], tab[:, 0:tw], ALU.mult, tag="rotc")
                k.P.tt(rt2[:, 0:tw], psp[:, 0:tw], tab[:, 512:512 + tw], ALU.mult, tag="rots")
                k.P.tt(rot_fm.c(n, t0, t0 + tw), rt1[:, 0:tw], rt2[:, 0:tw], ALU.add, tag="rota")

    load_wslabT(k, wqkv_h, DM, 1024, "wv", slab=slab, col0=2048)
    for (t0, tw) in t_tiles:
        build_xs(t0, tw)
        for tsub in range(tw // 128):
            ti = t0 // 128 + tsub
            for nh in range(2):
                ps = k.ps()
                for kc in range(8):
                    k.P.mm(ps[:, 0:512], xs_t[:, kc * 512 + tsub * 128:kc * 512 + tsub * 128 + 128], slab[:, kc * 1024 + nh * 512:kc * 1024 + (nh + 1) * 512], start=(kc == 0), stop=(kc == 7), tag="vmm")
                k.P.copy(v_tm[:, ti * 1024 + nh * 512:ti * 1024 + (nh + 1) * 512], ps[:, 0:512], engine="scalar")

    afm = k.fm("afm", 8, T, BF16)
    NR = 3
    smsk = [k.sb("smsk", 128, 512, F32) for _ in range(NR)]
    nmax = [k.sb("nmax", 128, 1, F32) for _ in range(NR)]
    rsum = [k.sb("rsum", 128, 1, F32) for _ in range(NR)]
    probs = [k.sb("probs", 128, 512, BF16) for _ in range(NR)]
    probsT = [k.sb("probsT", 128, 512, BF16) for _ in range(NR)]
    it = 0
    for w in range(8):
        kcol0 = 256 * w
        for h in range(16):
            hc, hb = h // 2, (h % 2) * 64
            for s in range(2):
                r_ = it % NR
                it += 1
                qc0 = 256 + 256 * w + 128 * s
                ps = k.ps()
                k.P.mm(ps[:, 0:512], q_rot.c(hc, qc0, qc0 + 128)[hb:hb + 64, :], k_rot.c(hc, kcol0, kcol0 + 512)[hb:hb + 64, :], tag="score")
                mk = masks[("mf" if w == 0 else "m") + str(s)]
                k.P.tt(smsk[r_][:], ps[:, 0:512], mk[:], ALU.add, tag="smask")
                k.P.red(nmax[r_][:, 0:1], smsk[r_][:], ALU.max, tag="smax")
                k.P.ts(nmax[r_][:, 0:1], nmax[r_][:, 0:1], -8.0, None, ALU.mult, tag="sneg")
                k.P.act(probs[r_][:], smsk[r_][:], AF.Exp, scale=8.0, bias=nmax[r_][:, 0:1], accum_out=rsum[r_][:, 0:1], tag="sexp")
                k.P.reciprocal(rsum[r_][:, 0:1], rsum[r_][:, 0:1], tag="srec")
                k.P.ts(probs[r_][:], probs[r_][:], rsum[r_][:, 0:1], None, ALU.mult, tag="snorm")
                pb = psbt2[it % 2]
                for j in range(4):
                    k.P.transpose(pb[:, j * 128:(j + 1) * 128], probs[r_][:, j * 128:(j + 1) * 128], k.ident_bf(), tag="ptr")
                k.P.copy(probsT[r_][:], pb[:], engine="vector")
                psav = k.ps()
                for j in range(4):
                    ti = 2 * w + j
                    k.P.mm(psav[0:64, 0:128], v_tm[:, ti * 1024 + h * 64:ti * 1024 + (h + 1) * 64], probsT[r_][:, j * 128:(j + 1) * 128], start=(j == 0), stop=(j == 3), tag="av")
                k.P.copy(afm.c(hc, 256 * w + 128 * s, 256 * w + 128 * s + 128)[hb:hb + 64, :], psav[0:64, 0:128], engine="scalar")

    load_wslabT(k, wo_h, DM, DM, "wo", slab=slab)
    xbuf = k.sb("xb", 128, 512, F32)
    ostage = k.sb("ostage", 128, 2 * 512, F32)

    def evac_out(n, t0, tw, ps):
        s = ostage[:, (t0 // 512 % 2) * 512:(t0 // 512 % 2) * 512 + tw]
        k.P.dma(xbuf[:, 0:tw], x_h[n * 128:(n + 1) * 128, 257 + t0:257 + t0 + tw], queue="qsync")
        k.P.tt(s, ps, xbuf[:, 0:tw], ALU.add)
        k.P.dma(xatt_o[n * 128:(n + 1) * 128, t0:t0 + tw], s, queue="qscalar")
    linear(k, afm, slab, DM, DM, T, evac_out, tag="womm", slabN=1024)
    return k


def build_l2f_f1():
    k = K("l2ff1")
    T = TCORE
    DI = 2730
    x_h = k.inp("x", (DM, T + 1))
    g_h = k.inp("g", (DM,))
    b_h = k.inp("b", (DM,))
    w1aT_h = k.inp("w1aT", (DM, DI))
    w1gT_h = k.inp("w1gT", (DM, DI))
    b1a_h = k.inp("b1a", (DI,))
    b1g_h = k.inp("b1g", (DI,))
    t_o = k.outp("t", (DI, T), BF16)
    k.mkpsums(8)
    gc = load_bias_cols(k, g_h, DM, "g")
    bc = load_bias_cols(k, b_h, DM, "b")
    xn_bf = k.fm("xn_bf", 8, T, BF16)
    xt = k.sb("ln_x", 128, 8 * 512, F32)
    xb = k.sb("ln_xb", 128, 512, BF16)
    xsq = k.sb("ln_xsq", 128, 512, BF16)
    rows = k.sb("ln_rows", 1, 6 * 512, F32)
    xnf = k.sb("ln_xnf", 128, 512, F32)
    Tt = 512
    for t0 in range(0, T, Tt):
        tw = min(Tt, T - t0)
        for c in range(8):
            off = 1 if c < 4 else 0
            k.P.dma(xt[:, c * Tt:c * Tt + tw], x_h[c * 128:(c + 1) * 128, t0 + off:t0 + off + tw])
        ps_s = k.ps()
        ps_q = k.ps()
        for c in range(8):
            k.P.copy(xb[:, 0:tw], xt[:, c * Tt:c * Tt + tw], engine="gpsimd")
            k.P.mm(ps_s[0:1, 0:tw], k.ones_col_bf(), xb[:, 0:tw], start=(c == 0), stop=(c == 7), tag="lnsum")
            k.P.act(xsq[:, 0:tw], xt[:, c * Tt:c * Tt + tw], AF.Square, tag="lnsq")
            k.P.mm(ps_q[0:1, 0:tw], k.ones_col_bf(), xsq[:, 0:tw], start=(c == 0), stop=(c == 7), tag="lnsum2")
        mean = rows[:, 0:tw]
        rstd = rows[:, Tt:Tt + tw]
        mr = rows[:, 2 * Tt:2 * Tt + tw]
        var = rows[:, 3 * Tt:3 * Tt + tw]
        k.P.ts(mean, ps_s[0:1, 0:tw], 1.0 / DM, None, ALU.mult)
        k.P.ts(var, ps_q[0:1, 0:tw], 1.0 / DM, None, ALU.mult)
        k.P.tt(rows[:, 4 * Tt:4 * Tt + tw], mean, mean, ALU.mult)
        k.P.tt(var, var, rows[:, 4 * Tt:4 * Tt + tw], ALU.subtract)
        k.P.ts(var, var, 1e-5, None, ALU.add)
        k.P.act(rows[:, 5 * Tt:5 * Tt + tw], var, AF.Sqrt)
        k.P.reciprocal(rstd, rows[:, 5 * Tt:5 * Tt + tw])
        k.P.tt(mr, mean, rstd, ALU.mult)
        bps = bcast_row_psum(k, rstd, tw)
        bpm = bcast_row_psum(k, mr, tw)
        for c in range(8):
            k.P.tt(xnf[:, 0:tw], xt[:, c * Tt:c * Tt + tw], bps[:, 0:tw], ALU.mult)
            k.P.tt(xnf[:, 0:tw], xnf[:, 0:tw], bpm[:, 0:tw], ALU.subtract)
            k.P.ts(xn_bf.c(c, t0, t0 + tw), xnf[:, 0:tw], gc[:, c:c + 1], bc[:, c:c + 1], ALU.mult, ALU.add, tag="lngb")
    slab_a = load_wslabT(k, w1aT_h, DM, DI, "w1a")
    slab_g = load_wslabT(k, w1gT_h, DM, DI, "w1g")
    ba = load_bias_cols(k, b1a_h, DI, "b1a")
    bg = load_bias_cols(k, b1g_h, DI, "b1g")
    av = k.sb("av", 128, 512, F32)
    gv = k.sb("gv", 128, 512, F32)
    tstage = k.sb("tstage", 128, 2 * 512, BF16)
    NC = _cdiv(DI, 128)
    for n in range(NC):
        npp = min(128, DI - n * 128)
        for t0 in range(0, T, 512):
            psa = k.ps()
            psg2 = k.ps()
            for kc in range(8):
                k.P.mm(psa[0:npp, 0:512], slab_a[:, kc * DI + n * 128:kc * DI + n * 128 + npp], xn_bf.c(kc, t0, t0 + 512), start=(kc == 0), stop=(kc == 7), tag="amm")
            for kc in range(8):
                k.P.mm(psg2[0:npp, 0:512], slab_g[:, kc * DI + n * 128:kc * DI + n * 128 + npp], xn_bf.c(kc, t0, t0 + 512), start=(kc == 0), stop=(kc == 7), tag="gmm")
            k.P.ts(av[0:npp, 0:512], psa[0:npp, 0:512], ba[0:npp, n:n + 1], None, ALU.add, tag="abias")
            k.P.act(gv[:, 0:512], psg2[:, 0:512], AF.Gelu, bias=bg[:, n:n + 1], tag="ggelu")
            s = tstage[:, (t0 // 512 % 2) * 512:(t0 // 512 % 2) * 512 + 512]
            k.P.tt(s[0:npp, :], av[0:npp, 0:512], gv[0:npp, 0:512], ALU.mult, tag="tmul")
            k.P.dma(t_o[n * 128:n * 128 + npp, t0:t0 + 512], s[0:npp, :], queue="qscalar")
    return k


def build_l2f_f2():
    k = K("l2ff2")
    T = TCORE
    DI = 2730
    t_h = k.inp("t", (DI, T), BF16)
    w2T_h = k.inp("w2T", (DI, DM))
    b2_h = k.inp("b2", (DM,))
    xatt_h = k.inp("xatt", (DM, T))
    xo_o = k.outp("xo", (DM, T))
    k.mkpsums(8)
    KC = _cdiv(DI, 128)
    slab = load_wslabT(k, w2T_h, DI, DM, "w2")
    b2c = load_bias_cols(k, b2_h, DM, "b2")
    tq = k.sb("tq", 128, KC * 512, BF16)
    xbuf = k.sb("xb", 128, 512, F32)
    stage = k.sb("stage", 128, 2 * 512, F32)
    for t0 in range(0, T, 512):
        for kc in range(KC):
            p = min(128, DI - kc * 128)
            k.P.dma(tq[0:p, kc * 512:(kc + 1) * 512], t_h[kc * 128:kc * 128 + p, t0:t0 + 512], queue="qsync")
        for n in range(8):
            ps = k.ps()
            for kc in range(KC):
                p = min(128, DI - kc * 128)
                k.P.mm(ps[:, 0:512], slab[0:p, kc * DM + n * 128:kc * DM + (n + 1) * 128], tq[0:p, kc * 512:(kc + 1) * 512], start=(kc == 0), stop=(kc == KC - 1), tag="w2mm")
            s = stage[:, (n % 2) * 512:(n % 2) * 512 + 512]
            k.P.dma(xbuf[:], xatt_h[n * 128:(n + 1) * 128, t0:t0 + 512], queue="qgpsimd")
            k.P.ts(s, ps[:, 0:512], b2c[:, n:n + 1], None, ALU.add)
            k.P.tt(s, s, xbuf[:], ALU.add)
            k.P.dma(xo_o[n * 128:(n + 1) * 128, t0:t0 + 512], s, queue="qscalar")
    return k


def build_lnx():
    k = K("lnx")
    T = TCORE
    x_h = k.inp("x", (DM, T))
    xn_o = k.outp("xn", (DM, T))
    k.mkpsums(4)
    xn_bf = k.fm("lnx_bf", 8, T, BF16)
    layernorm_stream(k, x_h, DM, T, xn_bf, xn_f32_h=xn_o)
    return k


def build_add3():
    k = K("add3")
    T = TCORE
    a_h = k.inp("a", (DM, T))
    b_h = k.inp("b", (DM, T))
    c_h = k.inp("c", (DM, T))
    o_h = k.outp("o", (DM, T))
    ta = k.sb("ta", 128, T, F32)
    tb = k.sb("tb", 128, T, F32)
    tc = k.sb("tc", 128, T, F32)
    for c in range(8):
        k.P.dma(ta[:], a_h[c * 128:(c + 1) * 128, :], queue="qsync")
        k.P.dma(tb[:], b_h[c * 128:(c + 1) * 128, :], queue="qgpsimd")
        k.P.dma(tc[:], c_h[c * 128:(c + 1) * 128, :], queue="qscalar")
        k.P.tt(ta[:], ta[:], tb[:], ALU.add)
        k.P.tt(ta[:], ta[:], tc[:], ALU.add)
        k.P.dma(o_h[c * 128:(c + 1) * 128, :], ta[:], queue="qsync")
    return k


def _rotary_tables():
    n = SEQ
    DH = 64
    W = 256
    half = np.arange(0, DH, 2, dtype=np.float64)
    inv_freq = 1.0 / 10000.0 ** (half / DH)
    t = np.arange(n, dtype=np.float64)
    freqs = t[:, None] * inv_freq[None, :]
    freqs = np.concatenate([freqs, freqs], -1)
    base_scale = (half + 0.4 * DH) / (1.4 * DH)
    power = (t - n // 2) / (W // 2)
    scale = base_scale[None, :] ** power[:, None]
    scale = np.concatenate([scale, scale], -1)
    cos = np.cos(freqs)
    sin = np.sin(freqs)
    sgn = np.where(np.arange(DH) < 32, -1.0, 1.0)[None, :]
    ctq = (cos * scale).T
    stq = (sin * scale * sgn).T
    inv = 1.0 / scale
    ctk = (cos * inv).T
    stk = (sin * inv * sgn).T
    out = []
    for a in (ctq, stq, ctk, stk):
        out.append(np.ascontiguousarray(np.tile(a, (2, 1)), dtype=np.float32))
    return out


def _l2f_masks():
    m = []
    for s in range(2):
        i = np.arange(128)[:, None]
        j = np.arange(512)[None, :]
        ok = (j >= s * 128 + i) & (j <= s * 128 + i + 256)
        m.append(np.where(ok, 0.0, -1e9).astype(np.float32))
    mf = [np.where((np.arange(512)[None, :] >= 256), m[s], -1e9).astype(np.float32) for s in range(2)]
    return m[0], m[1], mf[0], mf[1]


def run_l2f(inputs, x_dm):
    ka = build_l2f_attn()
    kf1 = build_l2f_f1()
    kf2 = build_l2f_f2()
    ctq, stq, ctk, stk = _rotary_tables()
    m0, m1, mf0, mf1 = _l2f_masks()
    def tabslice(tab, t0):
        lo = t0 - 256
        if lo < 0:
            pad = np.repeat(tab[:, 0:1], -lo, axis=1)
            return _f32(np.concatenate([pad, tab[:, 0:t0 + TCORE]], 1))
        return _f32(tab[:, lo:t0 + TCORE])
    for li in range(6):
        wqkv = _f32(inputs["L2_ffn_Wqkv"][li])
        wo = _f32(inputs["L2_ffn_Wo"][li])
        qsc = _f32(np.tile(np.asarray(inputs["L2_ffn_q_scale"][li]), 2))
        ksc = _f32(np.tile(np.asarray(inputs["L2_ffn_k_scale"][li]), 2))
        maps = []
        for b in range(BATCH):
            for h in range(2):
                t0, t1 = h * TCORE, (h + 1) * TCORE
                maps.append({"x": _halo_slice(x_dm[b], t0, t1, 257, 0),
                             "wqkv": wqkv, "wo": wo, "qsc": qsc, "ksc": ksc,
                             "ctq": tabslice(ctq, t0), "stq": tabslice(stq, t0),
                             "ctk": tabslice(ctk, t0), "stk": tabslice(stk, t0),
                             "m0": m0, "m1": m1,
                             "mf0": mf0 if h == 0 else m0, "mf1": mf1 if h == 0 else m1})
        r = ka.run(maps)
        xatt_dm = [np.concatenate([r.results[2 * b]["xatt"], r.results[2 * b + 1]["xatt"]], 1) for b in range(BATCH)]
        g = _f32(inputs["L2_ffn_ln_g"][li])
        bb = _f32(inputs["L2_ffn_ln_b"][li])
        W1 = np.asarray(inputs["L2_ffn_W1"][li])
        w1aT = _f32(W1[:, :2730])
        w1gT = _f32(W1[:, 2730:])
        b1 = np.asarray(inputs["L2_ffn_b1"][li])
        b1a, b1g = _f32(b1[:2730]), _f32(b1[2730:])
        maps = []
        for b in range(BATCH):
            for h in range(2):
                t0, t1 = h * TCORE, (h + 1) * TCORE
                maps.append({"x": _halo_slice(xatt_dm[b], t0, t1, 1, 0),
                             "g": g, "b": bb, "w1aT": w1aT, "w1gT": w1gT,
                             "b1a": b1a, "b1g": b1g})
        r = kf1.run(maps)
        t_cores = [r.results[c]["t"] for c in range(NCORES)]
        w2T = _f32(inputs["L2_ffn_W2"][li])
        b2 = _f32(inputs["L2_ffn_b2"][li])
        maps = []
        for b in range(BATCH):
            for h in range(2):
                c = 2 * b + h
                t0, t1 = h * TCORE, (h + 1) * TCORE
                maps.append({"t": t_cores[c], "w2T": w2T, "b2": b2,
                             "xatt": xatt_dm[b][:, t0:t1]})
        r = kf2.run(maps)
        x_dm = [np.concatenate([r.results[2 * b]["xo"], r.results[2 * b + 1]["xo"]], 1) for b in range(BATCH)]
    return x_dm


def run_l2(inputs, x_dm, ks_scan):
    kl = build_lnx()
    maps = []
    for b in range(BATCH):
        for h in range(2):
            maps.append({"x": np.ascontiguousarray(x_dm[b][:, h * TCORE:(h + 1) * TCORE])})
    r = kl.run(maps)
    xn_dm = [np.concatenate([r.results[2 * b]["xn"], r.results[2 * b + 1]["xn"]], 1) for b in range(BATCH)]
    attn_dm = run_l2a(inputs, xn_dm, ks_scan)
    ffn_dm = run_l2f(inputs, xn_dm)
    kadd = build_add3()
    maps = []
    for b in range(BATCH):
        for h in range(2):
            sl = slice(h * TCORE, (h + 1) * TCORE)
            maps.append({"a": np.ascontiguousarray(x_dm[b][:, sl]),
                         "b": np.ascontiguousarray(attn_dm[b][:, sl]),
                         "c": np.ascontiguousarray(ffn_dm[b][:, sl])})
    r = kadd.run(maps)
    x3_dm = [np.concatenate([r.results[2 * b]["o"], r.results[2 * b + 1]["o"]], 1) for b in range(BATCH)]
    return x3_dm



def conv4_causal(k, src_fm, CC, Te, kw, bcv, out_cb, tag="cv"):
    if not hasattr(k, "_conv_scr"):
        k._conv_scr = (k.sb("cacc_sh", 128, 512, F32), k.sb("cst_sh", 128, 512, F32))
    cacc, cst = k._conv_scr
    for c in range(CC):
        for t0 in range(0, TCORE, 512):
            tw = min(512, TCORE - t0)
            base = c * Te + t0
            k.P.ts(cacc[:, 0:tw], src_fm.t[:, base:base + tw], kw[:, c * 4:c * 4 + 1], None, ALU.mult, tag=tag)
            for i in (1, 2, 3):
                k.P.stt(cacc[:, 0:tw], src_fm.t[:, base + i:base + i + tw], kw[:, c * 4 + i:c * 4 + i + 1], cacc[:, 0:tw], ALU.mult, ALU.add, tag=tag)
            k.P.act(cst[:, 0:tw], cacc[:, 0:tw], AF.Silu, tag=tag + "si")
            out_cb(c, t0, tw, cst[:, 0:tw])


def build_l3a():
    k = K("l3a")
    Te = TCORE + 4
    KD, VD = 768, 1536
    x_h = k.inp("x", (DM, Te))
    n1_h = k.inp("n1", (DM,))
    wq_h = k.inp("wq", (DM, KD))
    wk_h = k.inp("wk", (DM, KD))
    wv_h = k.inp("wv", (DM, VD))
    wg_h = k.inp("wg", (DM, VD))
    wb_h = k.inp("wb", (DM, 8))
    wa_h = k.inp("wa", (DM, 8))
    cq_h = k.inp("cq", (KD, 4))
    ck_h = k.inp("ck", (KD, 4))
    cv_h = k.inp("cv", (VD, 4))
    alog_h = k.inp("alog", (8,))
    dtb_h = k.inp("dtb", (8,))
    mqk_h = k.inp("mqk", (KD, KD))
    xn_o = k.outp("xn", (DM, Te))
    q_o = k.outp("q", (KD, TCORE))
    k_o = k.outp("kk", (KD, TCORE))
    v_o = k.outp("v", (VD, TCORE))
    go_o = k.outp("go", (VD, TCORE))
    beta_o = k.outp("beta", (8, TCORE))
    g_o = k.outp("g", (8, TCORE))
    k.mkpsums(8)
    xn_bf = k.fm("xn_bf", 8, Te, BF16)
    layernorm_stream(k, x_h, DM, Te, xn_bf, xn_f32_h=xn_o)
    h_bf = k.fm("h_bf", 8, Te, BF16)
    n1c = load_bias_cols(k, n1_h, DM, "n1")
    sqb = k.sb("sqb", 128, 512, BF16)
    rows = k.sb("rrows", 1, 2 * 512, F32)
    for t0 in range(0, Te, 512):
        tw = min(512, Te - t0)
        psq = k.ps()
        for c in range(8):
            k.P.act(sqb[:, 0:tw], xn_bf.c(c, t0, t0 + tw), AF.Square, tag="rsq")
            k.P.mm(psq[0:1, 0:tw], k.ones_col_bf(), sqb[:, 0:tw], start=(c == 0), stop=(c == 7), tag="rsum")
        r0 = rows[:, 0:tw]
        k.P.ts(r0, psq[0:1, 0:tw], 1.0 / DM, 1e-6, ALU.mult, ALU.add)
        k.P.act(rows[:, 512:512 + tw], r0, AF.Sqrt)
        k.P.reciprocal(r0, rows[:, 512:512 + tw])
        bps = bcast_row_psum(k, r0, tw)
        for c in range(8):
            k.P.stt(h_bf.c(c, t0, t0 + tw), xn_bf.c(c, t0, t0 + tw), n1c[:, c:c + 1], bps[:, 0:tw], ALU.mult, ALU.mult, tag="rnorm")
    slab8 = k.sb("slab8", 128, 8 * 8, BF16)
    arow = k.sb("arow", 8, 1, F32)
    dtbc = k.sb("dtbc", 8, 1, F32)
    k.P.dma(arow[:], alog_h[:, None], queue="qsync")
    k.P.act(arow[:], arow[:], AF.Exp)
    k.P.ts(arow[:], arow[:], -1.0, None, ALU.mult)
    k.P.dma(dtbc[:], dtb_h[:, None], queue="qsync")
    bstage = k.sb("bstage", 8, 512, F32)
    bsc1 = k.sb("bsc1", 8, 512, F32)
    bsc2 = k.sb("bsc2", 8, 512, F32)
    for (w_h, out_h, isbeta) in ((wb_h, beta_o, True), (wa_h, g_o, False)):
        load_wslabT(k, w_h, DM, 8, "wba", slab=slab8)
        for (t0, tw) in [(3, 512), (515, 512), (1027, 512), (1539, 512)]:
            ps = k.ps()
            for kc in range(8):
                k.P.mm(ps[0:8, 0:tw], slab8[:, kc * 8:kc * 8 + 8], h_bf.c(kc, t0, t0 + tw), start=(kc == 0), stop=(kc == 7), tag="ba")
            if isbeta:
                k.P.act(bstage[:, 0:tw], ps[0:8, 0:tw], AF.Sigmoid, tag="bsig")
            else:
                k.P.ts(bstage[:, 0:tw], ps[0:8, 0:tw], dtbc[:, 0:1], None, ALU.add, tag="gadd")
                softplus_tiles(k, bstage[:, 0:tw], bstage[:, 0:tw], bsc1[:, 0:tw], bsc2[:, 0:tw])
                k.P.ts(bstage[:, 0:tw], bstage[:, 0:tw], arow[:, 0:1], None, ALU.mult, tag="gmul")
            k.P.dma(out_h[:, t0 - 3:t0 - 3 + tw], bstage[0:8, 0:tw], queue="qscalar")
    slabv = k.sb("slabv", 128, 8 * VD, BF16)
    raw = k.fm("raw", 12, Te, BF16)
    mslab = k.sb("mslab", 128, 6 * KD, BF16)
    rawv = Fm(raw.t, 12, Te)

    class _CsilView:
        def c(self, c, t0=0, t1=None):
            if t1 is None:
                t1 = TCORE
            return rawv.c(6 + c, t0, t1)
    csil = _CsilView()
    kwq = k.sb("kwq", 128, 12 * 4, F32)
    nsq = k.sb("nsq", 128, 512, BF16)
    nrm = k.sb("nrm", 128, 512, F32)
    qstage = k.sb("qstage", 128, 512, F32)
    for (w_h, c_h, out_h, scl) in ((wq_h, cq_h, q_o, 96.0 ** -0.5), (wk_h, ck_h, k_o, 1.0)):
        load_wslabT(k, w_h, DM, KD, "wqk", slab=slabv, slabN=VD)
        def evac_raw(n, t0, tw, ps):
            k.P.copy(raw.c(n, t0, t0 + tw), ps, engine="vector")
        linear(k, h_bf, slabv, DM, KD, Te, evac_raw, tag="qkp", slabN=VD)
        for c in range(6):
            k.P.dma(kwq[:, c * 4:(c + 1) * 4], c_h[c * 128:(c + 1) * 128, :], queue="qsync")

        def conv_cb(c, t0, tw, sil):
            k.P.copy(csil.c(c, t0, t0 + tw), sil, engine="gpsimd")
        conv4_causal(k, raw, 6, Te, kwq, None, conv_cb, tag="cq")
        load_wslabT(k, mqk_h, KD, KD, "mqk", slab=mslab)
        for t0 in range(0, TCORE, 512):
            for co in range(6):
                cis = [ci for ci in range(6) if _mqk_overlap(ci, co)]
                psn = k.ps()
                for (ii, ci) in enumerate(cis):
                    k.P.act(nsq[:, 0:512], csil.c(ci, t0, t0 + 512), AF.Square, tag="l3sq")
                    k.P.mm(psn[:, 0:512], mslab[:, ci * KD + co * 128: ci * KD + co * 128 + 128], nsq[:, 0:512], start=(ii == 0), stop=(ii == len(cis) - 1), tag="gsum")
                k.P.ts(nrm[:, 0:512], psn[:, 0:512], 1e-6, None, ALU.add, tag="l3eps")
                k.P.act(nrm[:, 0:512], nrm[:, 0:512], AF.Sqrt, tag="l3rt")
                k.P.reciprocal(nrm[:, 0:512], nrm[:, 0:512], tag="l3rc")
                k.P.stt(qstage[:, 0:512], csil.c(co, t0, t0 + 512), scl, nrm[:, 0:512], ALU.mult, ALU.mult, tag="l3ap")
                k.P.dma(out_h[co * 128:(co + 1) * 128, t0:t0 + 512], qstage[:, 0:512], queue="qscalar")
    kwv = k.sb("kwv", 128, 12 * 4, F32)
    vstage = k.sb("vstage", 128, 2 * 512, F32)
    load_wslabT(k, wv_h, DM, VD, "wv", slab=slabv)

    def evac_raw2(n, t0, tw, ps):
        k.P.copy(raw.c(n, t0, t0 + tw), ps, engine="vector")
    linear(k, h_bf, slabv, DM, VD, Te, evac_raw2, tag="vp")
    for c in range(12):
        k.P.dma(kwv[:, c * 4:(c + 1) * 4], cv_h[c * 128:(c + 1) * 128, :], queue="qsync")

    def conv_cb_v(c, t0, tw, sil):
        s = vstage[:, (t0 // 512 % 2) * 512:(t0 // 512 % 2) * 512 + tw]
        k.P.copy(s, sil, engine="gpsimd")
        k.P.dma(v_o[c * 128:(c + 1) * 128, t0:t0 + tw], s, queue="qscalar")
    conv4_causal(k, raw, 12, Te, kwv, None, conv_cb_v, tag="cvv")
    load_wslabT(k, wg_h, DM, VD, "wgo", slab=slabv)
    gstage = k.sb("gstage", 128, 2 * 512, F32)

    def evac_go(n, t0, tw, ps):
        s = gstage[:, (t0 // 512 % 2) * 512:(t0 // 512 % 2) * 512 + tw]
        k.P.copy(s, ps, engine="vector")
        lo = max(t0, 3)
        hi = min(t0 + tw, 3 + TCORE)
        if hi > lo:
            k.P.dma(go_o[n * 128:(n + 1) * 128, lo - 3:hi - 3], s[:, lo - t0:hi - t0], queue="qscalar")
    linear(k, h_bf, slabv, DM, VD, Te, evac_go, tag="gop")
    return k


def _mqk_overlap(ci, co):
    for h in range(8):
        lo, hi = 96 * h, 96 * h + 96
        if lo < (ci + 1) * 128 and hi > ci * 128 and lo < (co + 1) * 128 and hi > co * 128:
            return True
    return False


def build_l3s(nch=None, sect=9, nheads=4, adrop=()):
    k = K("l3s")
    T = SEQ
    NH = 4
    DK, DV = 96, 192
    CH = 128
    NCH = (T // CH) if nch is None else nch
    q_h = k.inp("q", (NH * DK, T))
    k_h = k.inp("kk", (NH * DK, T))
    vtm_h = k.inp("vtm", (T, NH * DV))
    beta_h = k.inp("beta", (NH, T))
    g_h = k.inp("g", (NH, T))
    o_o = k.outp("o", (T, NH * DV))
    k.mkpsums(4)
    psg = _psum_named(k, "psg")
    psgN = _psum_named(k, "psgN")
    psb = _psum_named(k, "psb")
    psbt = k.st.enter_context(k.nc.psum_tensor(k.uid("psbt"), [128, 128], BF16))
    idf = k.ident_f()
    idb = k.ident_bf()

    beta_t = k.sb("betat", NH, T, F32)
    g_t = k.sb("gt", NH, T, F32)
    k.P.dma(beta_t[:], beta_h, queue="qsync")
    k.P.dma(g_t[:], g_h, queue="qsync")
    zrow = k.sb("zrow", NH, CH, F32)
    k.P.memset(zrow[:], 0.0)

    Gc = k.sb("Gc", NH, CH, F32)
    nGc = k.sb("nGc", NH, CH, F32)
    G_tm = k.sb("Gtm", 128, NH, F32)
    negG = k.sb("negG", 128, NH, F32)
    beta_tm = k.sb("btm", 128, NH, F32)
    grow = k.sb("grow", 1, NH * CH, F32)
    ngrow = k.sb("ngrow", 1, NH * CH, F32)
    qh = k.sb("qh", DK, CH, BF16)
    kh = k.sb("kh", DK, CH, BF16)
    kh2 = k.sb("kh2", DK, CH, BF16)
    vt = k.sb("vt", 128, DV, F32)
    S = k.sb("S", DK, NH * DV, F32)
    S_bf = k.sb("Sbf", DK, NH * DV, BF16)
    k.P.memset(S[:], 0.0)
    k.P.memset(S_bf[:], 0.0)
    lam = k.sb("lam", 128, 1, F32)
    dec = k.sb("dec", 128, 1, F32)
    gam = k.sb("gam", 128, 1, F32)
    e1 = k.sb("e1", 128, CH, F32)
    e2 = k.sb("e2", 128, CH, F32)
    Af = k.sb("Af", 128, CH, F32)
    P_bf = k.sb("Pbf", 128, CH, BF16)
    PT_bf = k.sb("PTbf", 128, CH, BF16)
    Rt = k.sb("Rt", 128, DV, F32)
    X_f = k.sb("Xf", 128, DV, F32)
    X_bf = k.sb("Xbf", 128, DV, BF16)
    WT_bf = k.sb("WTbf", 128, CH, BF16)
    O1 = k.sb("O1", 128, DV, F32)
    ktm_bf = k.sb("ktmbf", 128, DK, BF16)
    ostage = k.sb("ostage", 128, DV, F32)
    maskSL = k.const_sb("c_maskSL", np.where(np.tril(np.ones((128, 128)), -1) > 0, 0.0, -1e9).astype(np.float32))
    maskIU = k.const_sb("c_maskIU", np.where(np.triu(np.ones((128, 128))) > 0, 0.0, -1e9).astype(np.float32))

    for ci in range(NCH):
        c0 = ci * CH
        k.P.scan(Gc[:], g_t[:, c0:c0 + CH], zrow[:], 0.0, op0=ALU.add, op1=ALU.add, tag="gcum")
        pst = k.ps()
        k.P.transpose(pst[0:CH, 0:NH], Gc[:], idf[0:NH, 0:NH], tag="gT")
        k.P.copy(G_tm[:, 0:NH], pst[0:CH, 0:NH], engine="vector")
        k.P.ts(negG[:, 0:NH], G_tm[:, 0:NH], -1.0, None, ALU.mult)
        pst2 = k.ps()
        k.P.transpose(pst2[0:CH, 0:NH], beta_t[:, c0:c0 + CH], idf[0:NH, 0:NH], tag="bT")
        k.P.copy(beta_tm[:, 0:NH], pst2[0:CH, 0:NH], engine="vector")
        k.P.ts(nGc[:], Gc[:], -1.0, None, ALU.mult, tag="ngc")
        k.P.dma(grow[:], Gc[:], queue="qsync")
        k.P.dma(ngrow[:], nGc[:], queue="qsync")
        k.P.mm(psg[:, 0:512], k.ones_row_f(), grow[:], tag="gbc")
        k.P.mm(psgN[:, 0:512], k.ones_row_f(), ngrow[:], tag="gnbc")
        for h in range(nheads):
            hs = h * 128
            k.P.dma(qh[:], q_h[h * DK:(h + 1) * DK, c0:c0 + CH], queue="qgpsimd")
            k.P.dma(kh[:], k_h[h * DK:(h + 1) * DK, c0:c0 + CH], queue="qgpsimd")
            k.P.dma(vt[:], vtm_h[c0:c0 + CH, h * DV:(h + 1) * DV], queue="qsync")
            k.P.act(lam[:, 0:1], G_tm[:, h:h + 1], AF.Exp, tag="lam")
            k.P.act(dec[:, 0:1], psg[:, hs + CH - 1:hs + CH], AF.Exp, bias=negG[:, h:h + 1], tag="dec")
            k.P.act(gam[:, 0:1], psg[:, hs + CH - 1:hs + CH], AF.Exp, tag="gam")
            if sect < 2:
                continue
            psk = k.ps()
            if "kk" not in adrop:
                k.P.copy(kh2[:], kh[:], engine="vector")
                k.P.mm(psk[:, 0:CH], kh[:], kh2[:], tag="kk")
            if "amask" not in adrop:
                k.P.tt(e1[:], psgN[:, hs:hs + CH], maskSL[:], ALU.add, tag="amask")
            if "aexp" not in adrop:
                k.P.act(e2[:], e1[:], AF.Exp, bias=G_tm[:, h:h + 1], tag="aexp")
            if "akke" not in adrop:
                k.P.tt(Af[:], psk[:, 0:CH], e2[:], ALU.mult, tag="akke")
            if "abeta" not in adrop:
                k.P.ts(P_bf[:], Af[:], beta_tm[:, h:h + 1], None, ALU.mult, tag="abeta")
            if "atr" not in adrop:
                k.P.transpose(psbt[:, 0:CH], P_bf[:], idb, tag="atr")
                k.P.copy(PT_bf[:], psbt[:, 0:CH], engine="vector")
            if sect < 3:
                continue
            psr = k.ps()
            k.P.mm(psr[0:CH, 0:DV], kh[:], S_bf[:, h * DV:(h + 1) * DV], tag="ks0")
            k.P.ts(Rt[:, 0:DV], psr[0:CH, 0:DV], lam[:, 0:1], None, ALU.mult, tag="rlam")
            k.P.stt(Rt[:, 0:DV], Rt[:, 0:DV], -1.0, vt[:, 0:DV], ALU.mult, ALU.add, tag="rsub")
            k.P.ts(X_f[:, 0:DV], Rt[:, 0:DV], beta_tm[:, h:h + 1], None, ALU.mult, tag="rbeta")
            k.P.copy(X_bf[:, 0:DV], X_f[:, 0:DV], engine="gpsimd")
            if sect < 4:
                continue
            psx = k.ps()
            k.P.mm(psx[0:CH, 0:DV], PT_bf[:], X_bf[:, 0:DV], tag="ax")
            k.P.tt(X_f[:, 0:DV], X_f[:, 0:DV], psx[0:CH, 0:DV], ALU.subtract, tag="xsub")
            k.P.copy(X_bf[:, 0:DV], X_f[:, 0:DV], engine="gpsimd")
            for j in range(6):
                psp = k.ps()
                k.P.mm(psp[:, 0:CH], PT_bf[:], P_bf[:], tag="psq")
                k.P.copy(P_bf[:], psp[:, 0:CH], engine="vector")
                k.P.transpose(psbt[:, 0:CH], P_bf[:], idb, tag="ptr")
                k.P.copy(PT_bf[:], psbt[:, 0:CH], engine="vector")
                psx = k.ps()
                k.P.mm(psx[0:CH, 0:DV], PT_bf[:], X_bf[:, 0:DV], tag="px")
                k.P.tt(X_f[:, 0:DV], X_f[:, 0:DV], psx[0:CH, 0:DV], ALU.add, tag="xadd")
                k.P.copy(X_bf[:, 0:DV], X_f[:, 0:DV], engine="gpsimd")
            if sect < 5:
                continue
            pso1 = k.ps()
            k.P.mm(pso1[0:CH, 0:DV], qh[:], S_bf[:, h * DV:(h + 1) * DV], tag="qs0")
            k.P.ts(O1[:, 0:DV], pso1[0:CH, 0:DV], lam[:, 0:1], None, ALU.mult, tag="o1lam")
            psq2 = k.ps()
            k.P.mm(psq2[:, 0:CH], kh[:], qh[:], tag="kq")
            k.P.tt(e1[:], psg[:, hs:hs + CH], maskIU[:], ALU.add, tag="wmask")
            k.P.act(e2[:], e1[:], AF.Exp, bias=negG[:, h:h + 1], tag="wexp")
            k.P.tt(WT_bf[:], e2[:], psq2[:, 0:CH], ALU.mult, tag="wmul")
            pso2 = k.ps()
            k.P.mm(pso2[0:CH, 0:DV], WT_bf[:], X_bf[:, 0:DV], tag="o2")
            k.P.tt(ostage[:, 0:DV], O1[:, 0:DV], pso2[0:CH, 0:DV], ALU.add, tag="osum")
            k.P.dma(o_o[c0:c0 + CH, h * DV:(h + 1) * DV], ostage[:, 0:DV], queue="qscalar")
            if sect < 6:
                continue
            k.P.transpose(psbt[:, 0:DK], kh[:], idb[0:DK, 0:DK], tag="ktr")
            k.P.ts(ktm_bf[:, 0:DK], psbt[:, 0:DK], dec[:, 0:1], None, ALU.mult, tag="kdec")
            pss = k.ps()
            k.P.mm(pss[0:DK, 0:DV], ktm_bf[:, 0:DK], X_bf[:, 0:DV], tag="snew")
            k.P.stt(S[:, h * DV:(h + 1) * DV], S[:, h * DV:(h + 1) * DV], gam[0:DK, 0:1], pss[0:DK, 0:DV], ALU.mult, ALU.add, tag="supd")
            k.P.copy(S_bf[:, h * DV:(h + 1) * DV], S[:, h * DV:(h + 1) * DV], engine="gpsimd")
    return k


def build_l3p1():
    k = K("l3p1")
    T = TCORE
    VD = 1536
    o_h = k.inp("o", (VD, T))
    otm_h = k.inp("otm", (T, VD))
    go_h = k.inp("go", (VD, T))
    onw_h = k.inp("onw", (VD,))
    wo_h = k.inp("wo", (VD, DM))
    xn_h = k.inp("xn", (DM, T))
    xmid_o = k.outp("xmid", (DM, T))
    k.mkpsums(8)
    idf = k.ident_f()
    otm = k.sb("otm", 128, VD, F32)
    rtm = k.sb("rtm", 128, 8, F32)
    rfm = k.sb("rfm", 8, T, F32)
    for tt_ in range(T // 128):
        k.P.dma(otm[:], otm_h[tt_ * 128:(tt_ + 1) * 128, :], queue="qsync")
        k.P.act(otm[:], otm[:], AF.Square, tag="osq")
        k.P.red(rtm[:, :, None], otm[:].rearrange("p (g i) -> p g i", i=192), ALU.add, tag="ored")
        k.P.ts(rtm[:], rtm[:], 1.0 / 192.0, 1e-6, ALU.mult, ALU.add, tag="oeps")
        k.P.act(rtm[:], rtm[:], AF.Sqrt, tag="osqrt")
        k.P.reciprocal(rtm[:], rtm[:], tag="orec")
        ps = k.ps()
        k.P.transpose(ps[0:8, 0:128], rtm[:], idf, tag="otr")
        k.P.copy(rfm[:, tt_ * 128:(tt_ + 1) * 128], ps[0:8, 0:128], engine="vector")
    hot = np.zeros((8, 12 * 128), np.float32)
    for c in range(12):
        for p in range(128):
            hot[(128 * c + p) // 192, c * 128 + p] = 1.0
    hotc = k.const_sb("c_hot192", hot)
    onc = load_bias_cols(k, onw_h, VD, "onw")
    t1 = k.fm("t1", 12, T, BF16)
    ob = k.sb("ob", 128, 512, F32)
    gb = k.sb("gb", 128, 512, F32)
    for c in range(12):
        for t0 in range(0, T, 512):
            ps = k.ps()
            k.P.mm(ps[:, 0:512], hotc[:, c * 128:(c + 1) * 128], rfm[:, t0:t0 + 512], tag="obc")
            k.P.dma(ob[:, 0:512], o_h[c * 128:(c + 1) * 128, t0:t0 + 512], queue="qsync")
            k.P.dma(gb[:, 0:512], go_h[c * 128:(c + 1) * 128, t0:t0 + 512], queue="qgpsimd")
            k.P.act(gb[:, 0:512], gb[:, 0:512], AF.Silu, tag="gsil")
            k.P.stt(ob[:, 0:512], ob[:, 0:512], onc[:, c:c + 1], ps[:, 0:512], ALU.mult, ALU.mult, tag="onrm")
            k.P.tt(t1.c(c, t0, t0 + 512), ob[:, 0:512], gb[:, 0:512], ALU.mult, tag="t1m")
    slab = load_wslabT(k, wo_h, VD, DM, "wo")
    xb = k.sb("xb", 128, 512, F32)
    stage = k.sb("stage", 128, 2 * 512, F32)

    def evac(n, t0, tw, ps):
        s = stage[:, (t0 // 512 % 2) * 512:(t0 // 512 % 2) * 512 + tw]
        k.P.dma(xb[:, 0:tw], xn_h[n * 128:(n + 1) * 128, t0:t0 + tw], queue="qsync")
        k.P.tt(s, ps, xb[:, 0:tw], ALU.add)
        k.P.dma(xmid_o[n * 128:(n + 1) * 128, t0:t0 + tw], s, queue="qscalar")
    linear(k, t1, slab, VD, DM, T, evac, tag="womm")
    return k


def build_l3p2():
    k = K("l3p2")
    T = TCORE
    FF = 4096
    x_h = k.inp("x", (DM, T))
    n2_h = k.inp("n2", (DM,))
    wgT_h = k.inp("wgT", (DM, FF))
    wuT_h = k.inp("wuT", (DM, FF))
    t2_o = k.outp("t2", (FF, T), BF16)
    k.mkpsums(8)
    n2c = load_bias_cols(k, n2_h, DM, "n2")
    h2 = k.fm("h2", 8, T, BF16)
    xt = k.sb("rms_x", 128, 8 * 512, F32)
    sqb = k.sb("rms_sq", 128, 512, BF16)
    rows = k.sb("rms_rows", 1, 2 * 512, F32)
    for t0 in range(0, T, 512):
        tw = min(512, T - t0)
        for c in range(8):
            k.P.dma(xt[:, c * 512:c * 512 + tw], x_h[c * 128:(c + 1) * 128, t0:t0 + tw])
        psq = k.ps()
        for c in range(8):
            k.P.act(sqb[:, 0:tw], xt[:, c * 512:c * 512 + tw], AF.Square, tag="rsq")
            k.P.mm(psq[0:1, 0:tw], k.ones_col_bf(), sqb[:, 0:tw], start=(c == 0), stop=(c == 7), tag="rsum")
        r0 = rows[:, 0:tw]
        k.P.ts(r0, psq[0:1, 0:tw], 1.0 / DM, 1e-6, ALU.mult, ALU.add)
        k.P.act(rows[:, 512:512 + tw], r0, AF.Sqrt)
        k.P.reciprocal(r0, rows[:, 512:512 + tw])
        bps = bcast_row_psum(k, r0, tw)
        for c in range(8):
            k.P.stt(h2.c(c, t0, t0 + tw), xt[:, c * 512:c * 512 + tw], n2c[:, c:c + 1], bps[:, 0:tw], ALU.mult, ALU.mult, tag="rnorm")
    slab_g = load_wslabT(k, wgT_h, DM, FF, "wg")
    slab_u = load_wslabT(k, wuT_h, DM, FF, "wu")
    gv = k.sb("gv", 128, 512, F32)
    tstage = k.sb("tstage", 128, 2 * 512, BF16)
    for n in range(32):
        for t0 in range(0, T, 512):
            psgt = k.ps()
            psu = k.ps()
            for kc in range(8):
                k.P.mm(psgt[:, 0:512], slab_g[:, kc * FF + n * 128:kc * FF + (n + 1) * 128], h2.c(kc, t0, t0 + 512), start=(kc == 0), stop=(kc == 7), tag="gmm")
            for kc in range(8):
                k.P.mm(psu[:, 0:512], slab_u[:, kc * FF + n * 128:kc * FF + (n + 1) * 128], h2.c(kc, t0, t0 + 512), start=(kc == 0), stop=(kc == 7), tag="umm")
            k.P.act(gv[:, 0:512], psgt[:, 0:512], AF.Silu, tag="gsil")
            s = tstage[:, (t0 // 512 % 2) * 512:(t0 // 512 % 2) * 512 + 512]
            k.P.tt(s, gv[:, 0:512], psu[:, 0:512], ALU.mult, tag="t2m")
            k.P.dma(t2_o[n * 128:(n + 1) * 128, t0:t0 + 512], s, queue="qscalar")
    return k


def build_l3p3():
    k = K("l3p3")
    T = TCORE
    FF = 4096
    t2_h = k.inp("t2", (FF, T), BF16)
    wdT_h = k.inp("wdT", (FF, DM))
    xmid_h = k.inp("xmid", (DM, T))
    xn_h = k.inp("xn", (DM, T))
    wT_h = k.inp("wT", (DM, DM))
    b_h = k.inp("b", (DM,))
    vr_h = k.inp("vr", (DM, 32))
    cT_h = k.inp("cT", (32, DM))
    x3_h = k.inp("x3", (DM, T))
    x4_o = k.outp("x4", (DM, T))
    k.mkpsums(8)
    xn_bf = k.fm("xn_bf", 8, T, BF16)
    for c in range(8):
        k.P.dma(xn_bf.c(c), xn_h[c * 128:(c + 1) * 128, :], queue="qgpsimd")
    slab_vr = load_wslabT(k, vr_h, DM, 32, "vr")
    rbuf = k.sb("rbuf", 32, T, BF16)

    def evac_r(n, t0, tw, ps):
        k.P.copy(rbuf[:, t0:t0 + tw], ps, engine="vector")
    linear(k, xn_bf, slab_vr, DM, 32, T, evac_r, tag="vr")
    rfm = Fm(rbuf, 1, T)
    slab_w = load_wslabT(k, wT_h, DM, DM, "w")
    slab_c = load_wslabT(k, cT_h, 32, DM, "ct")
    bcols = load_bias_cols(k, b_h, DM, "b")
    slab_d = load_wslabT(k, wdT_h, FF, DM, "wd")
    t2t = k.sb("t2t", 128, 32 * 512, BF16)
    xmb = k.sb("xmb", 128, 512, F32)
    x3b = k.sb("x3b", 128, 512, F32)
    stage = k.sb("stage", 128, 2 * 512, F32)
    for t0 in range(0, T, 512):
        for kc in range(32):
            k.P.dma(t2t[:, kc * 512:(kc + 1) * 512], t2_h[kc * 128:(kc + 1) * 128, t0:t0 + 512], queue="qsync")
        for n in range(8):
            ps = k.ps()
            for kc in range(8):
                k.P.mm(ps[:, 0:512], slab_w[:, kc * DM + n * 128:kc * DM + (n + 1) * 128], xn_bf.c(kc, t0, t0 + 512), start=(kc == 0), stop=False, tag="wmm")
            k.P.mm(ps[:, 0:512], slab_c[0:32, n * 128:(n + 1) * 128], rbuf[:, t0:t0 + 512], start=False, stop=True, tag="cmm")
            ps2 = k.ps()
            for kc in range(32):
                k.P.mm(ps2[:, 0:512], slab_d[:, kc * DM + n * 128:kc * DM + (n + 1) * 128], t2t[:, kc * 512:(kc + 1) * 512], start=(kc == 0), stop=(kc == 31), tag="dmm")
            s = stage[:, (n % 2) * 512:(n % 2) * 512 + 512]
            k.P.dma(xmb[:], xmid_h[n * 128:(n + 1) * 128, t0:t0 + 512], queue="qsync")
            k.P.dma(x3b[:], x3_h[n * 128:(n + 1) * 128, t0:t0 + 512], queue="qgpsimd")
            k.P.ts(s, ps[:, 0:512], bcols[:, n:n + 1], None, ALU.add, tag="fbias")
            k.P.tt(s, s, ps2[:, 0:512], ALU.add, tag="fadd1")
            k.P.tt(s, s, xmb[:], ALU.add, tag="fadd2")
            k.P.tt(s, s, x3b[:], ALU.add, tag="fadd3")
            k.P.dma(x4_o[n * 128:(n + 1) * 128, t0:t0 + 512], s, queue="qscalar")
    return k


def run_l3(inputs, x_dm):
    kl = build_lnx()
    maps = []
    for b in range(BATCH):
        for h in range(2):
            maps.append({"x": np.ascontiguousarray(x_dm[b][:, h * TCORE:(h + 1) * TCORE])})
    r = kl.run(maps)
    xn_dm = [np.concatenate([r.results[2 * b]["xn"], r.results[2 * b + 1]["xn"]], 1) for b in range(BATCH)]

    ka = build_l3a()
    mqk = np.zeros((768, 768), np.float32)
    for h in range(8):
        mqk[96 * h:96 * h + 96, 96 * h:96 * h + 96] = 1.0
    maps = []
    for b in range(BATCH):
        for h in range(2):
            t0, t1 = h * TCORE, (h + 1) * TCORE
            maps.append({"x": _halo_slice(xn_dm[b], t0, t1, 3, 1),
                         "n1": _f32(inputs["L3_attn_norm1_w"]),
                         "wq": _f32(inputs["L3_attn_Wq"]), "wk": _f32(inputs["L3_attn_Wk"]),
                         "wv": _f32(inputs["L3_attn_Wv"]), "wg": _f32(inputs["L3_attn_Wg"]),
                         "wb": _f32(inputs["L3_attn_Wb"]), "wa": _f32(inputs["L3_attn_Wa"]),
                         "cq": _f32(inputs["L3_attn_conv_q_w"]), "ck": _f32(inputs["L3_attn_conv_k_w"]),
                         "cv": _f32(inputs["L3_attn_conv_v_w"]),
                         "alog": _f32(inputs["L3_attn_A_log"]), "dtb": _f32(inputs["L3_attn_dt_bias"]),
                         "mqk": mqk})
    r = ka.run(maps)
    RES = r.results
    xn3_cores = [RES[c]["xn"][:, 3:3 + TCORE] for c in range(NCORES)]
    q_full = [np.concatenate([RES[2 * b]["q"], RES[2 * b + 1]["q"]], 1) for b in range(BATCH)]
    k_full = [np.concatenate([RES[2 * b]["kk"], RES[2 * b + 1]["kk"]], 1) for b in range(BATCH)]
    v_full = [np.concatenate([RES[2 * b]["v"], RES[2 * b + 1]["v"]], 1) for b in range(BATCH)]
    go_cores = [RES[c]["go"] for c in range(NCORES)]
    beta_full = [np.concatenate([RES[2 * b]["beta"], RES[2 * b + 1]["beta"]], 1) for b in range(BATCH)]
    g_full = [np.concatenate([RES[2 * b]["g"], RES[2 * b + 1]["g"]], 1) for b in range(BATCH)]

    ks = build_l3s()
    maps = []
    for b in range(BATCH):
        for hg in range(2):
            rs = slice(hg * 384, (hg + 1) * 384)
            vs = slice(hg * 768, (hg + 1) * 768)
            maps.append({"q": np.ascontiguousarray(q_full[b][rs]),
                         "kk": np.ascontiguousarray(k_full[b][rs]),
                         "vtm": _f32(v_full[b][vs].T),
                         "beta": np.ascontiguousarray(beta_full[b][hg * 4:(hg + 1) * 4]),
                         "g": np.ascontiguousarray(g_full[b][hg * 4:(hg + 1) * 4])})
    r = ks.run(maps)
    o_full = [np.concatenate([_f32(r.results[2 * b]["o"].T), _f32(r.results[2 * b + 1]["o"].T)], 0) for b in range(BATCH)]

    kp1 = build_l3p1()
    onw = _f32(np.tile(np.asarray(inputs["L3_attn_onorm_w"]), 8))
    maps = []
    for b in range(BATCH):
        for h in range(2):
            c = 2 * b + h
            t0, t1 = h * TCORE, (h + 1) * TCORE
            o_sl = np.ascontiguousarray(o_full[b][:, t0:t1])
            maps.append({"o": o_sl, "otm": _f32(o_sl.T), "go": go_cores[c],
                         "onw": onw, "wo": _f32(inputs["L3_attn_Wo"]),
                         "xn": xn3_cores[c]})
    r = kp1.run(maps)
    xmid_cores = [r.results[c]["xmid"] for c in range(NCORES)]

    kp2 = build_l3p2()
    maps = []
    for c in range(NCORES):
        maps.append({"x": xmid_cores[c], "n2": _f32(inputs["L3_attn_norm2_w"]),
                     "wgT": _f32(inputs["L3_attn_Wgate"]), "wuT": _f32(inputs["L3_attn_Wup"])})
    r = kp2.run(maps)
    t2_cores = [r.results[c]["t2"] for c in range(NCORES)]

    kp3 = build_l3p3()
    maps = []
    for b in range(BATCH):
        for h in range(2):
            c = 2 * b + h
            t0, t1 = h * TCORE, (h + 1) * TCORE
            maps.append({"t2": t2_cores[c], "wdT": _f32(inputs["L3_attn_Wdown"]),
                         "xmid": xmid_cores[c], "xn": xn3_cores[c],
                         "wT": _f32(np.asarray(inputs["L3_ffn_W"]).T), "b": _f32(inputs["L3_ffn_b"]),
                         "vr": _f32(inputs["L3_ffn_V_r"]), "cT": _f32(np.asarray(inputs["L3_ffn_C"]).T),
                         "x3": np.ascontiguousarray(x_dm[b][:, t0:t1])})
    r = kp3.run(maps)
    x4_dm = [np.concatenate([r.results[2 * b]["x4"], r.results[2 * b + 1]["x4"]], 1) for b in range(BATCH)]
    return x4_dm
